# version 12
# baseline (speedup 1.0000x reference)
import numpy as np
import ml_dtypes
import contextlib
import concourse.bass as bass
import concourse.mybir as mybir
from concourse.bass_utils import run_bass_kernel_spmd

F32 = mybir.dt.float32
BF16 = mybir.dt.bfloat16
I16 = mybir.dt.int16
U16 = mybir.dt.uint16
AF = mybir.ActivationFunctionType
ALU = mybir.AluOpType
AX = mybir.AxisListType

NCORES = 8
BL = 2
T = 2048
NT = BL * T
D = 1024
DT = D // 128
I3 = 3072
IT = I3 // 128
ITL = IT // NCORES
KV, REP, HD, LAT = 2, 4, 128, 32
NTT = NT // 128
NTC = NT // 512
NQT = T // 128
MAGIC = 12582912.0
INV_QMAX = float(np.float32(1.0) / np.float32(32767.0))
INV7 = float(np.float32(1.0) / np.float32(7.0))
EPS = 1e-5
FMAX = 3.402823466e38
P2_16 = float(np.float32(2.0) ** -16)
SCALE_ATT = float(np.float32(HD) ** np.float32(-0.5))

SC_SX, SC_RSX, SC_SU, SC_RSU, SC_SG, SC_RSG, SC_SG16, SC_SG2, SC_RSG2, SC_SO, SC_RSO = range(11)

SEM_NAMES = [
    "ph", "dx0", "dx1", "dw", "dqt0", "dqt1", "dsp", "dse", "dqtr",
    "dwl0", "dwl1", "dspb0", "dspb1", "dsr0", "dsr1",
    "dspc0", "dspc1", "dspd0", "dspd1", "dspe0", "dspe1",
    "dmig", "dmiy", "do0", "do1", "cc", "dwi",
    "sv", "ss", "st2", "gv",
    "lmm", "lev", "kmm", "kev", "vmm", "vev", "qmm", "qev",
    "bmm", "brd", "bev",
    "a_sc", "a_ngm", "a_ex", "a_nm", "a_tr", "a_pt", "a_pv", "a_ye", "a_rc", "a_yn",
    "a_ar", "dyt", "dwy0", "dwy1",
    "emm", "eev", "wmm", "wadd", "wtr", "wev",
]


def build_nc(dbg=False):
    nc = bass.Bass("TRN2", target_bir_lowering=False, detect_race_conditions=False)
    DKIND = "ExternalOutput" if dbg else "Internal"

    x_h = nc.dram_tensor("x", [BL, T, D], F32, kind="ExternalInput").ap()
    wq_h = nc.dram_tensor("wq", [D, D], F32, kind="ExternalInput").ap()
    wkd_h = nc.dram_tensor("wk_down", [D, KV * LAT], F32, kind="ExternalInput").ap()
    wvd_h = nc.dram_tensor("wv_down", [D, KV * LAT], F32, kind="ExternalInput").ap()
    wku_h = nc.dram_tensor("wk_up", [LAT, HD], F32, kind="ExternalInput").ap()
    wvu_h = nc.dram_tensor("wv_up", [LAT, HD], F32, kind="ExternalInput").ap()
    wo_h = nc.dram_tensor("wo", [D, D], F32, kind="ExternalInput").ap()
    wup_sh_h = nc.dram_tensor("wup_shard", [ITL * 128, D], F32, kind="ExternalInput").ap()
    wg_sh_h = nc.dram_tensor("wgate_shard", [ITL * 128, D], F32, kind="ExternalInput").ap()
    wd_sh_h = nc.dram_tensor("wdown_shard", [128, I3], F32, kind="ExternalInput").ap()
    ident_h = nc.dram_tensor("ident", [128, 128], BF16, kind="ExternalInput").ap()
    cmask_h = nc.dram_tensor("cmask", [128, 128], BF16, kind="ExternalInput").ap()
    out_h = nc.dram_tensor("out", [BL, T, D], F32, kind="ExternalOutput").ap()

    qt_sp = nc.dram_tensor("qt_sp", [DT, 128, NT], BF16, kind=DKIND).ap()
    yt_sp = nc.dram_tensor("yt_sp", [DT, 128, NT], BF16).ap()
    upraw_sp = nc.dram_tensor("upraw_sp", [IT, 128, NT], F32, kind=DKIND).ap()
    gtraw_sp = nc.dram_tensor("gtraw_sp", [IT, 128, NT], F32, kind=DKIND).ap()
    qg_sp = nc.dram_tensor("qg_sp", [IT, 128, NT], I16, kind=DKIND).ap()
    qsig_sp = nc.dram_tensor("qsig_sp", [IT, 128, NT], U16, kind=DKIND).ap()
    qg2_sp = nc.dram_tensor("qg2_sp", [NTC, IT, 128, 512], I16, kind=DKIND).ap()
    qup_sp = nc.dram_tensor("qup_sp", [NTC, IT, 128, 512], I16, kind=DKIND).ap()
    ffn_sp = nc.dram_tensor("ffn_sp", [BL, T, D], F32, kind=DKIND).ap()
    flip_hb = nc.dram_tensor("flip_hb", [4, 128], F32).ap()
    rflip_h = [nc.dram_tensor(f"rflip{i}", [16, 128], F32).ap() for i in range(2)]
    cc_in = [nc.dram_tensor(f"cc_in{i}", [1, 16], F32).ap() for i in range(4)]
    cc_out = [nc.dram_tensor(f"cc_out{i}", [1, 16], F32).ap() for i in range(4)]
    sc_dump = nc.dram_tensor("sc_dump", [128, 16], F32, kind=DKIND).ap() if dbg else None
    ln_dump = nc.dram_tensor("ln_dump", [128, 8 * NTT], F32, kind=DKIND).ap() if dbg else None
    big_dump = nc.dram_tensor("big_dump", [128, DT * NT], U16, kind=DKIND).ap() if dbg else None
    ag_wu_in = nc.dram_tensor("ag_wu_in", [128, I3], BF16).ap()
    ag_wg_in = nc.dram_tensor("ag_wg_in", [128, I3], BF16).ap()
    ag_wd_in = nc.dram_tensor("ag_wd_in", [128, I3], BF16).ap()
    ag_wu_out = nc.dram_tensor("ag_wu_out", [NCORES * 128, I3], BF16, addr_space="Shared").ap()
    ag_wg_out = nc.dram_tensor("ag_wg_out", [NCORES * 128, I3], BF16, addr_space="Shared").ap()
    ag_wd_out = nc.dram_tensor("ag_wd_out", [NCORES * 128, I3], BF16, addr_space="Shared").ap()

    prog = {k: [] for k in ("y", "v", "s", "t", "g")}
    cnt = {}
    sems = {}
    free_ev = {}

    def n_of(sem):
        return cnt.get(sem, 0)

    def do(eng, fn, inc=None, by=1, dr=None):
        if dr is None:
            dr = eng in ("v", "s")
        if eng in ("v", "s"):
            prog[eng].append(fn)
            if dr:
                prog[eng].append(lambda e: e.drain())
            if inc is not None:
                cnt[inc] = cnt.get(inc, 0) + by
                if dr:
                    prog[eng].append(lambda e, s=inc, b=by: e.nop().then_inc(sems[s], b))
                else:
                    prog[eng].pop()
                    prog[eng].append(lambda e, f=fn, s=inc, b=by: f(e).then_inc(sems[s], b))
        else:
            if inc is None:
                prog[eng].append(fn)
            else:
                cnt[inc] = cnt.get(inc, 0) + by
                prog[eng].append(lambda e, f=fn, s=inc, b=by: f(e).then_inc(sems[s], b))

    def wait(eng, sem, val=None):
        v = cnt.get(sem, 0) if val is None else val
        if v <= 0:
            return
        prog[eng].append(lambda e, s=sem, vv=v: e.wait_ge(sems[s], vv))

    def set_free(key, sem):
        free_ev[key] = (sem, n_of(sem))

    def wait_free(eng, key):
        if key in free_ev:
            s, v = free_ev[key]
            wait(eng, s, v)

    nbar = [0]
    def barrier():
        nbar[0] += 1
        for e in ("y", "v", "s", "t", "g"):
            cnt["ph"] = cnt.get("ph", 0) + 1
            if e in ("v", "s"):
                prog[e].append(lambda en: en.drain())
            prog[e].append(lambda en: en.nop().then_inc(sems["ph"], 1))
        tgt = 5 * nbar[0]
        for e in ("y", "v", "s", "t", "g"):
            prog[e].append(lambda en, t_=tgt: en.wait_ge(sems["ph"], t_))

    with contextlib.ExitStack() as top:
        sb = lambda name, shape, dt=F32: top.enter_context(nc.sbuf_tensor(name, shape, dt))
        for s in SEM_NAMES:
            sems[s] = top.enter_context(nc.semaphore(s))

        PSUM = [top.enter_context(nc.psum_tensor(f"P{i}", [128, 512], F32)) for i in range(8)]
        psum_f32 = lambda i: PSUM[i].ap()
        psum_bf16 = lambda i: PSUM[i].ap().bitcast(BF16)
        s_big = sb("s_big", [128, DT * NT], BF16)
        bigv = lambda d: s_big.ap()[:, d * NT:(d + 1) * NT]
        bigf = s_big.ap().bitcast(F32)
        bigi = s_big.ap().bitcast(I16)
        s_kT = [sb(f"kT{h}", [128, NT], BF16) for h in range(KV)]
        s_v = [sb(f"vT{h}", [128, NT], BF16) for h in range(KV)]
        s_sc = sb("s_sc", [128, 16])
        s_negM = sb("s_negM", [128, 1])
        s_id = sb("s_id", [128, 128], BF16)
        s_cm = sb("s_cm", [128, 128], BF16)
        s_sum = sb("s_sum", [128, NTT]); s_ssq = sb("s_ssq", [128, NTT])
        s_mx = sb("s_mx", [128, NTT]); s_mn = sb("s_mn", [128, NTT])
        s_m = sb("s_m", [128, NTT]); s_rstd = sb("s_rstd", [128, NTT])
        s_t1 = sb("s_t1", [128, NTT]); s_t2 = sb("s_t2", [128, NTT]); s_t3 = sb("s_t3", [128, NTT])
        s_red = sb("s_red", [128, 4])
        s_flip = sb("s_flip", [1, 4 * 128])
        s_loc = sb("s_loc", [1, 16])
        s_bc = sb("s_bc", [128, 16])

        def local_max_to_slot(src_col_ap, slot):
            wait("g", "sv", n_of("sv"))
            do("g", lambda e, a=src_col_ap, f=slot: e.dma_start(
                out=flip_hb[f:f + 1, :], in_=a), inc="dmig", by=16)
            wait("g", "dmig", n_of("dmig"))
            do("g", lambda e, f=slot: e.dma_start(
                out=s_flip.ap()[0:1, f * 128:(f + 1) * 128], in_=flip_hb[f:f + 1, :]), inc="dmig", by=16)
            wait("v", "dmig", n_of("dmig"))
            do("v", lambda e, f=slot: e.tensor_reduce(
                s_loc.ap()[0:1, f:f + 1], s_flip.ap()[0:1, f * 128:(f + 1) * 128], AX.X, ALU.max), inc="sv")

        def all_reduce(ridx):
            wait("g", "sv", n_of("sv"))
            do("g", lambda e, r=ridx: e.dma_start(out=cc_in[r][0:1, :], in_=s_loc.ap()[0:1, :]),
               inc="dmig", by=16)
            wait("g", "dmig", n_of("dmig"))
            do("g", lambda e, r=ridx: e.collective_compute(
                "AllReduce", ALU.max, replica_groups=[list(range(NCORES))],
                ins=[cc_in[r]], outs=[cc_out[r]]), inc="cc")

        def emit_scale_cols(ridx, pairs):
            wait("y", "cc", n_of("cc"))
            do("y", lambda e, r=ridx: e.dma_start(
                out=s_bc.ap(), in_=cc_out[r].broadcast_to([128, 16])), inc="dmiy", by=16)
            wait("v", "dmiy", n_of("dmiy"))
            for (bcol, scol, rscol) in pairs:
                do("v", lambda e, b=bcol, s=scol: e.tensor_scalar(
                    s_sc.ap()[:, s:s + 1], s_bc.ap()[:, b:b + 1], INV_QMAX, 1e-12,
                    ALU.mult, ALU.max), inc="sv")
                do("v", lambda e, s=scol, rs=rscol: e.reciprocal(
                    s_sc.ap()[:, rs:rs + 1], s_sc.ap()[:, s:s + 1]), inc="sv")

        do("v", lambda e: e.memset(s_negM.ap(), -MAGIC))
        do("v", lambda e: e.memset(s_loc.ap(), 0.0))
        do("y", lambda e: e.dma_start(out=s_id.ap(), in_=ident_h), inc="dwi", by=16)
        do("y", lambda e: e.dma_start(out=s_cm.ap(), in_=cmask_h), inc="dwi", by=16)

        ph0 = contextlib.ExitStack()
        sb0 = lambda name, shape, dt=F32: ph0.enter_context(nc.sbuf_tensor(name, shape, dt))
        ps0 = lambda name, shape, dt=F32: ph0.enter_context(nc.psum_tensor(name, shape, dt))

        s_wst = sb0("s_wst", [128, 4, 1024])
        wq_sb = sb0("wq_sb", [128, 8, 1024], BF16)
        wkd_sb = sb0("wkd_sb", [128, 8, 64], BF16)
        wvd_sb = sb0("wvd_sb", [128, 8, 64], BF16)
        wku_sb = sb0("wku_sb", [64, 128], BF16)
        wvu_sb = sb0("wvu_sb", [64, 128], BF16)
        s_updu = sb0("s_updu", [64, 128])

        for half in range(2):
            wait("y", "sv", n_of("sv"))
            do("y", lambda e, hf=half: e.dma_start(
                out=s_wst.ap(),
                in_=wq_h.rearrange("(mt k) d -> k mt d", k=128)[:, hf * 4:(hf + 1) * 4, :]),
                inc="dw", by=16)
            wait("v", "dw", n_of("dw"))
            do("v", lambda e, hf=half: e.tensor_copy(
                wq_sb.ap()[:, hf * 4:(hf + 1) * 4, :], s_wst.ap()), inc="sv")
        s_wkv = sb0("s_wkv", [128, 8, 64])
        for (h_ap, dst) in ((wkd_h, wkd_sb), (wvd_h, wvd_sb)):
            wait("y", "sv", n_of("sv"))
            do("y", lambda e, a=h_ap: e.dma_start(
                out=s_wkv.ap(), in_=a.rearrange("(mt k) d -> k mt d", k=128)), inc="dw", by=16)
            wait("v", "dw", n_of("dw"))
            do("v", lambda e, d=dst: e.tensor_copy(d.ap(), s_wkv.ap()), inc="sv")
        for (h_ap, dst) in ((wku_h, wku_sb), (wvu_h, wvu_sb)):
            wait("y", "sv", n_of("sv"))
            do("y", lambda e, a=h_ap: e.dma_start(out=s_updu.ap()[0:32, :], in_=a), inc="dw", by=16)
            do("y", lambda e, a=h_ap: e.dma_start(out=s_updu.ap()[32:64, :], in_=a), inc="dw", by=16)
            wait("v", "dw", n_of("dw"))
            do("v", lambda e, d=dst: e.tensor_copy(d.ap(), s_updu.ap()), inc="sv")

        s_wsh = sb0("s_wsh", [128, ITL, 1024])
        s_maxb = sb0("s_maxb", [128, 32])
        s_sb_ = sb0("s_sb_", [128, 32])
        s_rsb = sb0("s_rsb", [128, 32])
        s_wr = sb0("s_wr", [128, 1024])
        s_wrr = sb0("s_wrr", [128, 1024])
        s_wdq = sb0("s_wdq", [128, 1024], BF16)
        s_ag = sb0("s_ag", [128, I3], BF16)
        p_trw = None

        def dq_tile(tile_ap, out_col_fn):
            nb = 32
            do("v", lambda e, a=tile_ap: e.tensor_reduce(
                s_maxb.ap(), a.rearrange("p (b w) -> p b w", w=32),
                AX.X, ALU.max, apply_absolute_value=True), inc="gv")
            do("v", lambda e: e.tensor_scalar(
                s_sb_.ap(), s_maxb.ap(), INV7, 1e-12, ALU.mult, ALU.max), inc="gv")
            do("v", lambda e: e.reciprocal(s_rsb.ap(), s_sb_.ap()), inc="gv")
            do("v", lambda e, a=tile_ap: e.tensor_tensor(
                s_wr.ap().rearrange("p (b w) -> p b w", w=32),
                a.rearrange("p (b w) -> p b w", w=32),
                s_rsb.ap().unsqueeze(2).broadcast_to([128, nb, 32]), ALU.mult), inc="gv")
            do("v", lambda e: e.tensor_scalar(
                s_wrr.ap(), s_wr.ap(), MAGIC, MAGIC, ALU.add, ALU.subtract), inc="gv")
            wait("v", "st2", n_of("st2"))
            do("v", lambda e: e.tensor_tensor(
                s_wdq.ap().rearrange("p (b w) -> p b w", w=32),
                s_wrr.ap().rearrange("p (b w) -> p b w", w=32),
                s_sb_.ap().unsqueeze(2).broadcast_to([128, nb, 32]), ALU.mult), inc="gv")
            wait("t", "gv", n_of("gv"))
            wait("t", "st2", n_of("st2"))
            for mt in range(8):
                do("t", lambda e, m=mt: e.transpose(
                    psum_bf16(0)[:, m * 128:(m + 1) * 128],
                    s_wdq.ap()[:, m * 128:(m + 1) * 128], s_id.ap()),
                    inc=("dqtr" if mt == 7 else None))
            wait("s", "dqtr", n_of("dqtr"))
            for mg in range(8):
                do("s", lambda e, m=mg, oc=out_col_fn(mg): e.activation(
                    s_ag.ap()[:, oc:oc + 128],
                    psum_bf16(0)[:, m * 128:(m + 1) * 128], AF.Copy), inc="st2")

        wait("t", "dwi", 32)
        for (src_h, ag_h) in ((wup_sh_h, ag_wu_in), (wg_sh_h, ag_wg_in)):
            wait("y", "sv", n_of("sv"))
            wait("y", "st2", n_of("st2"))
            do("y", lambda e, s=src_h: e.dma_start(
                out=s_wsh.ap(), in_=s.rearrange("(t k) m -> k t m", k=128)), inc="dw", by=16)
            wait("v", "dw", n_of("dw"))
            for tl in range(ITL):
                dq_tile(s_wsh.ap()[:, tl, :], lambda mg, t=tl: mg * (ITL * 128) + t * 128)
            wait("g", "st2", n_of("st2"))
            do("g", lambda e, a=ag_h: e.dma_start(out=a, in_=s_ag.ap()), inc="dsp", by=16)
        wait("y", "st2", n_of("st2"))
        do("y", lambda e: e.dma_start(
            out=s_wsh.ap().rearrange("k t m -> k (t m)"), in_=wd_sh_h), inc="dw", by=16)
        wait("v", "dw", n_of("dw"))
        for sl in range(3):
            dq_tile(s_wsh.ap()[:, sl, :], lambda mg, s=sl: (s * 8 + mg) * 128)
        wait("g", "st2", n_of("st2"))
        do("g", lambda e: e.dma_start(out=ag_wd_in, in_=s_ag.ap()), inc="dsp", by=16)
        wait("g", "dsp", n_of("dsp"))
        for (ai, ao) in ((ag_wu_in, ag_wu_out), (ag_wg_in, ag_wg_out), (ag_wd_in, ag_wd_out)):
            do("g", lambda e, a=ai, b=ao: e.collective_compute(
                "AllGather", ALU.bypass, replica_groups=[list(range(NCORES))],
                ins=[a], outs=[b]), inc="cc")
        CC_AG = n_of("cc")

        do("s", lambda e: e.preload_act_table(AF.Square))
        s_xt = [sb0(f"s_xt{i}", [128, 1024]) for i in range(2)]
        s_xnb = sb0("s_xnb", [128, 1024], BF16)
        s_xn32 = sb0("s_xn32", [128, 1024])
        s_xn32b = sb0("s_xn32b", [128, 1024])

        def wait_xt_free(eng, par):
            wait_free(eng, ("xt", par))
            wait_free(eng, ("xt", par, "s"))

        for tt in range(NTT):
            b, tr = tt // (NTT // BL), (tt % (NTT // BL)) * 128
            par = tt % 2
            dxp = f"dx{par}"
            wait_xt_free("y", par)
            do("y", lambda e, b_=b, t_=tr, p=par: e.dma_start(
                out=s_xt[p].ap(), in_=x_h[b_, t_:t_ + 128, :]), inc=dxp, by=16)
            wait("v", dxp, n_of(dxp))
            do("v", lambda e, p=par, c=tt: e.tensor_reduce(
                s_sum.ap()[:, c:c + 1], s_xt[p].ap(), AX.X, ALU.add), inc="sv", dr=False)
            do("v", lambda e, p=par, c=tt: e.tensor_reduce(
                s_mx.ap()[:, c:c + 1], s_xt[p].ap(), AX.X, ALU.max), inc="sv", dr=False)
            do("v", lambda e, p=par, c=tt: e.tensor_reduce(
                s_mn.ap()[:, c:c + 1], s_xt[p].ap(), AX.X, ALU.min), inc="sv", dr=False)
            wait("s", dxp, n_of(dxp))
            do("s", lambda e, p=par, c=tt: e.activation(
                s_xnb.ap(), s_xt[p].ap(), AF.Square, accum_out=s_ssq.ap()[:, c:c + 1]), inc="ss")
            set_free(("xt", par), "sv")
            free_ev[("xt", par)] = ("sv", n_of("sv"))
            free_ev[("xt", par, "s")] = ("ss", n_of("ss"))
        wait("v", "ss", n_of("ss"))
        prog["v"].append(lambda e: e.drain())
        if dbg:
            wait("g", "sv", n_of("sv"))
            wait("g", "ss", n_of("ss"))
            for di, tns in enumerate((s_sum, s_ssq, s_mx, s_mn)):
                do("g", lambda e, d=di, t_=tns: e.dma_start(
                    out=ln_dump[:, (2 + d) * NTT:(3 + d) * NTT], in_=t_.ap()), inc="dmig", by=16)
            wait("v", "dmig", n_of("dmig"))
        do("v", lambda e: e.tensor_scalar(s_m.ap(), s_sum.ap(), 1.0 / D, None, ALU.mult), inc="sv")
        do("v", lambda e: e.tensor_tensor(s_t1.ap(), s_m.ap(), s_m.ap(), ALU.mult), inc="sv")
        do("v", lambda e: e.tensor_scalar(s_t2.ap(), s_ssq.ap(), 1.0 / D, None, ALU.mult), inc="sv")
        do("v", lambda e: e.tensor_tensor(s_sum.ap(), s_t2.ap(), s_t1.ap(), ALU.subtract), inc="sv")
        do("v", lambda e: e.tensor_scalar(s_t1.ap(), s_sum.ap(), EPS, None, ALU.add), inc="sv")
        wait("s", "sv", n_of("sv"))
        do("s", lambda e: e.preload_act_table(AF.Sqrt))
        do("s", lambda e: e.activation(s_t2.ap(), s_t1.ap(), AF.Sqrt), inc="ss")
        wait("v", "ss", n_of("ss"))
        prog["v"].append(lambda e: e.drain())
        if dbg:
            wait("g", "ss", n_of("ss"))
            do("g", lambda e: e.dma_start(out=ln_dump[:, 6 * NTT:7 * NTT], in_=s_t1.ap()), inc="dmig", by=16)
            do("g", lambda e: e.dma_start(out=ln_dump[:, 7 * NTT:8 * NTT], in_=s_t2.ap()), inc="dmig", by=16)
            wait("v", "dmig", n_of("dmig"))
        for _ in range(2):
            do("v", lambda e: e.reciprocal(s_ssq.ap(), s_t2.ap()), inc="sv")
            do("v", lambda e: e.tensor_tensor(s_sum.ap(), s_t1.ap(), s_ssq.ap(), ALU.mult), inc="sv")
            do("v", lambda e: e.tensor_tensor(s_t3.ap(), s_t2.ap(), s_sum.ap(), ALU.add), inc="sv")
            do("v", lambda e: e.tensor_scalar(s_t2.ap(), s_t3.ap(), 0.5, None, ALU.mult), inc="sv")
        do("v", lambda e: e.reciprocal(s_rstd.ap(), s_t2.ap()), inc="sv")
        do("v", lambda e: e.tensor_tensor(s_sum.ap(), s_mx.ap(), s_m.ap(), ALU.subtract), inc="sv")
        do("v", lambda e: e.tensor_tensor(s_ssq.ap(), s_m.ap(), s_mn.ap(), ALU.subtract), inc="sv")
        do("v", lambda e: e.tensor_tensor(s_t3.ap(), s_sum.ap(), s_ssq.ap(), ALU.max), inc="sv")
        do("v", lambda e: e.tensor_tensor(s_sum.ap(), s_t3.ap(), s_rstd.ap(), ALU.mult), inc="sv")
        do("v", lambda e: e.tensor_reduce(s_red.ap()[:, 0:1], s_sum.ap(), AX.X, ALU.max), inc="sv")
        if dbg:
            wait("g", "sv", n_of("sv"))
            for di, tns in enumerate((s_m, s_rstd)):
                do("g", lambda e, d=di, t_=tns: e.dma_start(
                    out=ln_dump[:, d * NTT:(d + 1) * NTT], in_=t_.ap()), inc="dmig", by=16)
            wait("g", "dmig", n_of("dmig"))
        local_max_to_slot(s_red.ap()[:, 0:1], 0)
        all_reduce(0)
        emit_scale_cols(0, [(0, SC_SX, SC_RSX)])

        def emit_xpass(quant):
            for tt in range(NTT):
                b, tr = tt // (NTT // BL), (tt % (NTT // BL)) * 128
                par = tt % 2
                dxp = f"dx{par}"
                wait_xt_free("y", par)
                do("y", lambda e, b_=b, t_=tr, p=par: e.dma_start(
                    out=s_xt[p].ap(), in_=x_h[b_, t_:t_ + 128, :]), inc=dxp, by=16)
                wait("v", dxp, n_of(dxp))
                wait_free("v", "xnb")
                if not quant:
                    do("v", lambda e, p=par, c=tt: e.tensor_scalar(
                        s_xnb.ap(), s_xt[p].ap(), s_m.ap()[:, c:c + 1], s_rstd.ap()[:, c:c + 1],
                        ALU.subtract, ALU.mult), inc="sv")
                else:
                    do("v", lambda e, p=par, c=tt: e.tensor_scalar(
                        s_xn32.ap(), s_xt[p].ap(), s_m.ap()[:, c:c + 1], s_rstd.ap()[:, c:c + 1],
                        ALU.subtract, ALU.mult), inc="sv")
                    do("v", lambda e: e.tensor_scalar(
                        s_xn32b.ap(), s_xn32.ap(), s_sc.ap()[:, SC_RSX:SC_RSX + 1], MAGIC,
                        ALU.mult, ALU.add), inc="sv")
                    do("v", lambda e: e.tensor_scalar(
                        s_xnb.ap(), s_xn32b.ap(), MAGIC, s_sc.ap()[:, SC_SX:SC_SX + 1],
                        ALU.subtract, ALU.mult), inc="sv")
                free_ev[("xt", par)] = ("sv", n_of("sv"))
                free_ev[("xt", par, "s")] = ("ss", 0)
                wait("t", "sv", n_of("sv"))
                wait_free("t", "ptx")
                for dt in range(DT):
                    do("t", lambda e, d=dt: e.transpose(
                        psum_bf16(1)[:, d * 128:(d + 1) * 128],
                        s_xnb.ap()[:, d * 128:(d + 1) * 128], s_id.ap()),
                        inc=("st2" if dt == DT - 1 else None))
                set_free("xnb", "st2")
                wait("s", "st2", n_of("st2"))
                for dt in range(DT):
                    do("s", lambda e, d=dt, c=tt: e.activation(
                        bigv(d)[:, c * 128:(c + 1) * 128],
                        psum_bf16(1)[:, d * 128:(d + 1) * 128], AF.Copy), inc="ss",
                        dr=(dt == DT - 1))
                set_free("ptx", "ss")

        emit_xpass(quant=False)

        barrier()
        if dbg:
            do("g", lambda e: e.dma_start(
                out=big_dump, in_=s_big.ap().bitcast(U16)), inc="dmig", by=16)
            wait("g", "dmig", n_of("dmig"))
        s_lat = sb0("s_lat", [64, NT], BF16)
        s_latv = sb0("s_latv", [64, NT], BF16)
        s_qtb = sb0("s_qtb", [128, NT], BF16)
        p_q = [psum_f32(2), psum_f32(3)]
        p_l = psum_f32(4)[0:64, :]
        for (dst, wsb) in ((s_lat, wkd_sb), (s_latv, wvd_sb)):
            for tc in range(NTC):
                wait("t", "lev", n_of("lev"))
                for mt in range(DT):
                    do("t", lambda e, m=mt, c=tc, w=wsb: e.matmul(
                        p_l, w.ap()[:, m, :], bigv(m)[:, c * 512:(c + 1) * 512],
                        start=(m == 0), stop=(m == DT - 1)), inc=("lmm" if mt == DT - 1 else None))
                wait("s", "lmm", n_of("lmm"))
                do("s", lambda e, c=tc, d=dst: e.activation(
                    d.ap()[:, c * 512:(c + 1) * 512], p_l, AF.Copy), inc="lev")
        nk = [0]
        for h in range(KV):
            for tc in range(NTC):
                par = nk[0] % 2
                if nk[0] >= 2:
                    wait("t", "kev", nk[0] - 1)
                wait("t", "lev", NTC)
                do("t", lambda e, h_=h, c=tc, p=par: e.matmul(
                    p_q[p], wku_sb.ap()[h_ * 32:(h_ + 1) * 32, :],
                    s_lat.ap()[h_ * 32:(h_ + 1) * 32, c * 512:(c + 1) * 512],
                    start=True, stop=True), inc="kmm")
                wait("s", "kmm", n_of("kmm"))
                do("s", lambda e, h_=h, c=tc, p=par: e.activation(
                    s_kT[h_].ap()[:, c * 512:(c + 1) * 512], p_q[p], AF.Copy), inc="kev")
                nk[0] += 1
        nv = [0]
        wait("t", "kev", n_of("kev"))
        for h in range(KV):
            for tt in range(NTT):
                par = nv[0] % 2
                if nv[0] >= 2:
                    wait("t", "vev", nv[0] - 1)
                wait("t", "lev", 2 * NTC)
                do("t", lambda e, h_=h, c=tt, p=par: e.matmul(
                    p_q[p][:, 0:128], s_latv.ap()[h_ * 32:(h_ + 1) * 32, c * 128:(c + 1) * 128],
                    wvu_sb.ap()[h_ * 32:(h_ + 1) * 32, :], start=True, stop=True), inc="vmm")
                wait("s", "vmm", n_of("vmm"))
                do("s", lambda e, h_=h, c=tt, p=par: e.activation(
                    s_v[h_].ap()[:, c * 128:(c + 1) * 128], p_q[p][:, 0:128], AF.Copy), inc="vev")
                nv[0] += 1
        nq = [0]
        wait("t", "vev", n_of("vev"))
        for dt in range(DT):
            wait_free("s", ("qtb",))
            for tc in range(NTC):
                par = nq[0] % 2
                if nq[0] >= 2:
                    wait("t", "qev", nq[0] - 1)
                for mt in range(DT):
                    do("t", lambda e, m=mt, d=dt, c=tc, p=par: e.matmul(
                        p_q[p], wq_sb.ap()[:, m, d * 128:(d + 1) * 128],
                        bigv(m)[:, c * 512:(c + 1) * 512],
                        start=(m == 0), stop=(m == DT - 1)), inc=("qmm" if mt == DT - 1 else None))
                wait("s", "qmm", n_of("qmm"))
                do("s", lambda e, c=tc, p=par: e.activation(
                    s_qtb.ap()[:, c * 512:(c + 1) * 512], p_q[p], AF.Identity,
                    scale=SCALE_ATT), inc="qev")
                nq[0] += 1
            wait("g", "qev", n_of("qev"))
            do("g", lambda e, d=dt: e.dma_start(out=qt_sp[d], in_=s_qtb.ap()), inc="dsp", by=16)
            set_free(("qtb",), "dsp")

        barrier()
        emit_xpass(quant=True)

        barrier()
        ph0.close()
        phB = contextlib.ExitStack()
        sbB = lambda name, shape, dt=F32: phB.enter_context(nc.sbuf_tensor(name, shape, dt))
        psB = lambda name, shape, dt=F32: phB.enter_context(nc.psum_tensor(name, shape, dt))
        wl = [sbB(f"wl{i}", [128, 8, 128], BF16) for i in range(2)]
        stg = [sbB(f"stg{i}", [128, NT]) for i in range(2)]
        s_mxu = sbB("s_mxu", [128, 2 * IT * NTC])
        p_b = [psum_f32(i) for i in range(4)]
        wait("y", "cc", CC_AG)
        nmmB = [0]
        nspB = [0]
        for src_idx, (ag_h, sp_h) in enumerate(((ag_wu_out, upraw_sp), (ag_wg_out, gtraw_sp))):
            for it in range(IT):
                k = src_idx * IT + it
                wpar, spar = k % 2, k % 2
                rk, il = it // ITL, it % ITL
                dwlp = f"dwl{wpar}"
                wait_free("y", ("wl", wpar))
                do("y", lambda e, a=ag_h, r=rk, i=il, w=wpar: e.dma_start(
                    out=wl[w].ap(),
                    in_=a[r * 128:(r + 1) * 128, :].rearrange(
                        "p (mt q) -> p mt q", q=ITL * 128)[:, :, i * 128:(i + 1) * 128]),
                    inc=dwlp, by=16)
                for tc in range(NTC):
                    slot = nmmB[0] % 4
                    wait("t", dwlp, n_of(dwlp))
                    if nmmB[0] >= 4:
                        wait("t", "brd", nmmB[0] - 3)
                        wait("t", "bev", nmmB[0] - 3)
                    for mt in range(DT):
                        do("t", lambda e, m=mt, c=tc, w=wpar, sl=slot: e.matmul(
                            p_b[sl], wl[w].ap()[:, m, :], bigv(m)[:, c * 512:(c + 1) * 512],
                            start=(m == 0), stop=(m == DT - 1)), inc=("bmm" if mt == DT - 1 else None))
                    wait("v", "bmm", n_of("bmm"))
                    do("v", lambda e, sl=slot, mc=nmmB[0]: e.tensor_reduce(
                        s_mxu.ap()[:, mc:mc + 1], p_b[sl], AX.X, ALU.max,
                        apply_absolute_value=True), inc="brd")
                    wait("s", "brd", n_of("brd"))
                    if tc == 0:
                        wait_free("s", ("stg", spar))
                    do("s", lambda e, sl=slot, c=tc, sp=spar: e.activation(
                        stg[sp].ap()[:, c * 512:(c + 1) * 512], p_b[sl], AF.Copy), inc="bev")
                    nmmB[0] += 1
                set_free(("wl", wpar), "bmm")
                wait("g", "bev", n_of("bev"))
                do("g", lambda e, i=it, sp=spar, h=sp_h: e.dma_start(
                    out=h[i], in_=stg[sp].ap()), inc=f"dspb{spar}", by=16)
                set_free(("stg", spar), f"dspb{spar}")
                nspB[0] += 1
        wait("v", "brd", n_of("brd"))
        do("v", lambda e: e.tensor_reduce(
            s_red.ap()[:, 0:1], s_mxu.ap()[:, 0:IT * NTC], AX.X, ALU.max), inc="sv")
        do("v", lambda e: e.tensor_reduce(
            s_red.ap()[:, 1:2], s_mxu.ap()[:, IT * NTC:2 * IT * NTC], AX.X, ALU.max), inc="sv")
        local_max_to_slot(s_red.ap()[:, 0:1], 0)
        local_max_to_slot(s_red.ap()[:, 1:2], 1)
        all_reduce(1)
        emit_scale_cols(1, [(0, SC_SU, SC_RSU), (1, SC_SG, SC_RSG)])
        do("v", lambda e: e.tensor_scalar(
            s_sc.ap()[:, SC_SG16:SC_SG16 + 1], s_sc.ap()[:, SC_SG:SC_SG + 1], P2_16, None,
            ALU.mult), inc="sv")
        wait("g", "dspb0", n_of("dspb0"))
        wait("g", "dspb1", n_of("dspb1"))
        phB.close()
        phM = contextlib.ExitStack()
        sbM = lambda name, shape, dt=F32: phM.enter_context(nc.sbuf_tensor(name, shape, dt))
        uld = [bigf[:, 0:NT], bigf[:, NT:2 * NT]]
        g2t = bigf[:, 2 * NT:3 * NT]
        sigt = bigf[:, 3 * NT:4 * NT]
        phAT = contextlib.ExitStack()
        sbT = lambda name, shape, dt=F32: phAT.enter_context(nc.sbuf_tensor(name, shape, dt, side="right"))
        s_qhr = [sbT(f"s_qhr{i}", [128, NT], BF16) for i in range(2)]
        s_pb = [sbT(f"s_pb{i}", [128, 2048], BF16) for i in range(2)]
        s_pT = [sbT(f"s_pT{i}", [128, 1024], BF16) for i in range(2)]
        s_cmx = [sbT(f"s_cmx{i}", [128, 4]) for i in range(4)]
        s_csm = [sbT(f"s_csm{i}", [128, 4]) for i in range(4)]
        s_ngm = [sbT(f"s_ngm{i}", [128, 1]) for i in range(4)]
        s_rcp = [sbT(f"s_rcp{i}", [128, 1]) for i in range(4)]
        s_ystA = [sbT(f"s_ystA{i}", [128, 2048], BF16) for i in range(2)]
        p_s = [psum_f32(i) for i in range(4)]
        p_t2 = [psum_bf16(4), psum_bf16(5)]
        p_y2 = [psum_f32(6), psum_f32(7)]

        gchunk = [0]
        gq_ctr = [0]
        tr_grp = [0]
        pend = []
        trcnt_after = {}
        pv_after_grp = {}
        sc_after_hr = {}
        chunk_free = {}
        exp_after_tile = {}

        def emit_tp_pv(info):
            (h, r, b, qi, tile_id) = info
            ntl = qi + 1
            ngr = (ntl + 7) // 8
            for g in range(ngr):
                j0, j1 = g * 8, min(g * 8 + 7, qi)
                bank = tr_grp[0] % 2
                wait_free("t", ("pt2", bank))
                wait("t", "a_ex", exp_after_tile[tile_id])
                for j in range(j0, j1 + 1):
                    do("t", lambda e, j_=j, bk=bank, pb=(tile_id % 2): e.transpose(
                        p_t2[bk][:, (j_ % 8) * 128:((j_ % 8) + 1) * 128],
                        s_pb[pb].ap()[:, j_ * 128:(j_ + 1) * 128], s_id.ap()),
                        inc=("a_tr" if j == j1 else None))
                wait("s", "a_tr", n_of("a_tr"))
                wait_free("s", ("spT", bank))
                do("s", lambda e, bk=bank, n=(j1 - j0 + 1) * 128: e.activation(
                    s_pT[bk].ap()[:, 0:n], p_t2[bk][:, 0:n], AF.Copy), inc="a_pt")
                free_ev[("pt2", bank)] = ("a_pt", n_of("a_pt"))
                wait("t", "a_pt", n_of("a_pt"))
                if g == 0:
                    wait_free("t", ("py2", tile_id % 2))
                for j in range(j0, j1 + 1):
                    gtile = b * NQT + j
                    do("t", lambda e, h_=h, gt=gtile, bk=bank, jj=(j - j0), tb=(tile_id % 2), q_=qi, j_=j: e.matmul(
                        p_y2[tb][:, 0:128],
                        s_pT[bk].ap()[:, jj * 128:(jj + 1) * 128],
                        s_v[h_].ap()[:, gt * 128:(gt + 1) * 128],
                        start=(j_ == 0), stop=(j_ == q_)),
                        inc=("a_pv" if j == j1 else None))
                free_ev[("spT", bank)] = ("a_pv", n_of("a_pv"))
                tr_grp[0] += 1
            trcnt_after[tile_id] = n_of("a_tr")
            wait("s", "a_pv", n_of("a_pv"))
            wait("s", "a_rc", n_of("a_rc"))
            blkp = (tile_id // NQT) % 2
            lastq = (qi == NQT - 1)
            do("s", lambda e, tb=(tile_id % 2), p=(tile_id % 4), bp=blkp, q_=qi: e.activation(
                s_ystA[bp].ap()[:, q_ * 128:(q_ + 1) * 128], p_y2[tb][:, 0:128],
                AF.Identity, scale=s_rcp[p].ap()), inc="a_ar", dr=lastq)
            free_ev[("py2", tile_id % 2)] = ("a_ar", n_of("a_ar"))
            free_ev[("rcp", tile_id % 4)] = ("a_ar", n_of("a_ar"))

        def attn_stream():
            for h in range(KV):
                for r in range(REP):
                    hri = h * REP + r
                    qpar = hri % 2
                    dqtp = f"dqt{qpar}"
                    if hri >= 2:
                        wait("y", "a_sc", sc_after_hr[hri - 2])
                    do("y", lambda e, d=hri, p=qpar: e.dma_start(
                        out=s_qhr[p].ap(), in_=qt_sp[d]), inc=dqtp, by=16)
                    wait("t", dqtp, n_of(dqtp))
                    for b in range(BL):
                        blk = hri * BL + b
                        for qi in range(NQT):
                            tile_id = gq_ctr[0]
                            kl = (qi + 1) * 128
                            nch = (kl + 511) // 512
                            pbp = tile_id % 2
                            sp = tile_id % 4
                            base = gchunk[0]
                            for c in range(nch):
                                cn = min(512, kl - c * 512)
                                slot = gchunk[0] % 4
                                if gchunk[0] >= 4:
                                    wait("t", "a_ex", chunk_free[gchunk[0] - 4])
                                last = (c == nch - 1)
                                do("t", lambda e, h_=h, q_=qi, c_=c, n=cn, sl=slot, b_=b, p=qpar, lst=last: e.matmul(
                                    p_s[sl][:, 0:n],
                                    s_qhr[p].ap()[:, b_ * T + q_ * 128: b_ * T + (q_ + 1) * 128],
                                    s_kT[h_].ap()[:, b_ * T + c_ * 512: b_ * T + c_ * 512 + n],
                                    start=True, stop=(not lst)), inc=(None if last else "a_sc"))
                                if last:
                                    do("t", lambda e, sl=slot, o=cn - 128: e.matmul(
                                        p_s[sl][:, o:o + 128], s_id.ap(), s_cm.ap(),
                                        start=False, stop=True), inc="a_sc")
                                gchunk[0] += 1
                            for c in range(nch):
                                cn = min(512, kl - c * 512)
                                slot = (base + c) % 4
                                wait("v", "a_sc", base + c + 1)
                                if nch == 1:
                                    do("v", lambda e, sl=slot, n=cn, p=sp: e.tensor_reduce(
                                        s_ngm[p].ap(), p_s[sl][:, 0:n], AX.X,
                                        ALU.max, negate=True), inc="a_ngm")
                                else:
                                    do("v", lambda e, sl=slot, n=cn, c_=c, p=sp: e.tensor_reduce(
                                        s_cmx[p].ap()[:, c_:c_ + 1], p_s[sl][:, 0:n], AX.X,
                                        ALU.max), inc=None, dr=(c == nch - 1))
                            if nch > 1:
                                do("v", lambda e, p=sp, n=nch: e.tensor_reduce(
                                    s_ngm[p].ap(), s_cmx[p].ap()[:, 0:n], AX.X, ALU.max,
                                    negate=True), inc="a_ngm")
                            wait("s", "a_ngm", n_of("a_ngm"))
                            if tile_id >= 2:
                                wait("s", "a_tr", trcnt_after[tile_id - 2])
                            if tile_id >= 4:
                                wait("s", "a_rc", n_of("a_rc") - 3)
                            for c in range(nch):
                                cn = min(512, kl - c * 512)
                                slot = (base + c) % 4
                                do("s", lambda e, sl=slot, n=cn, c_=c, pb=pbp, p=sp: e.activation(
                                    s_pb[pb].ap()[:, c_ * 512:c_ * 512 + n], p_s[sl][:, 0:n],
                                    AF.Exp, bias=s_ngm[p].ap(),
                                    accum_out=s_csm[p].ap()[:, c_:c_ + 1]),
                                    inc=("a_ex" if c == nch - 1 else None), by=nch,
                                    dr=(c == nch - 1))
                            exp_after_tile[tile_id] = n_of("a_ex")
                            for c in range(nch):
                                chunk_free[base + c] = n_of("a_ex")
                            wait("v", "a_ex", n_of("a_ex"))
                            wait_free("v", ("rcp", sp))
                            if nch == 1:
                                do("v", lambda e, p=sp: e.reciprocal(
                                    s_rcp[p].ap(), s_csm[p].ap()[:, 0:1]), inc="a_rc")
                            else:
                                do("v", lambda e, p=sp, n=nch: e.tensor_reduce(
                                    s_rcp[p].ap(), s_csm[p].ap()[:, 0:n], AX.X, ALU.add),
                                    inc=None, dr=True)
                                do("v", lambda e, p=sp: e.reciprocal(
                                    s_rcp[p].ap(), s_rcp[p].ap()), inc="a_rc")
                            if pend:
                                emit_tp_pv(pend.pop())
                            pend.append((h, r, b, qi, tile_id))
                            gq_ctr[0] += 1
                            yield
                        if pend:
                            emit_tp_pv(pend.pop())
                        bp = blk % 2
                        wait("y", "a_ar", n_of("a_ar"))
                        do("y", lambda e, d=hri, b_=b, p=bp: e.dma_start(
                            out=yt_sp[d][:, b_ * T:(b_ + 1) * T], in_=s_ystA[p].ap()),
                            inc="dyt", by=16)
                        free_ev[("ystA", bp)] = ("dyt", n_of("dyt"))
                        yst_guard[bp] = n_of("dyt")
                        sc_after_hr[hri] = n_of("a_sc")
                    yield

        yst_guard = {}
        attn_gen = attn_stream()
        n_pumped = [0]

        def pump(n):
            for _ in range(n):
                if next(attn_gen, "END") == "END":
                    return

        _orig_emit = emit_tp_pv
        def emit_tp_pv(info):
            (h, r, b, qi, tile_id) = info
            if qi == 0:
                bp = (tile_id // NQT) % 2
                if bp in yst_guard:
                    wait("s", "dyt", yst_guard[bp])
            _orig_emit(info)

        s_pool = sbM("s_pool", [128, 8 * NT], BF16)
        poolf = s_pool.ap().bitcast(F32)
        pooli = s_pool.ap().bitcast(I16)
        poolu = s_pool.ap().bitcast(U16)
        gld = [poolf[:, 0:NT], poolf[:, NT:2 * NT]]
        s_qg = [pooli[:, 4 * NT:5 * NT], pooli[:, 5 * NT:6 * NT]]
        s_qsig = [poolu[:, 6 * NT:7 * NT], poolu[:, 7 * NT:8 * NT]]
        g2tb = [bigf[:, 2 * NT:3 * NT], bigf[:, 0:NT]]
        sigtb = [bigf[:, 3 * NT:4 * NT], bigf[:, NT:2 * NT]]
        s_mxc = sbM("s_mxc", [128, IT])
        sv_after_c = {}
        ss_after_c = {}
        dsr_after_c = {}
        wait("v", "bmm", n_of("bmm"))
        wait("y", "dspb0", n_of("dspb0"))
        wait("y", "dspb1", n_of("dspb1"))
        do("y", lambda e: e.dma_start(out=gld[0], in_=gtraw_sp[0]), inc="dsr0", by=16)
        dsr_after_c[0] = n_of("dsr0")
        do("s", lambda e: e.preload_act_table(AF.Exp))
        for k in range(IT + 2):
            par = k % 2
            if k >= 2:
                kk = k - 2
                p2 = kk % 2
                wait("v", "ss", ss_after_c[kk])
                do("v", lambda e, p=p2: e.tensor_scalar(
                    g2tb[p], sigtb[p], 65536.0, MAGIC, ALU.mult, ALU.add), inc="sv")
                wait_free("v", ("qsig", p2))
                do("v", lambda e, p=p2: e.tensor_scalar(
                    s_qsig[p], g2tb[p], MAGIC, 65535.0, ALU.subtract, ALU.min), inc="sv")
                do("v", lambda e, p=p2: e.scalar_tensor_tensor(
                    g2tb[p], s_qg[p], s_sc.ap()[:, SC_SG16:SC_SG16 + 1], s_qsig[p],
                    ALU.mult, ALU.mult), inc="sv")
                do("v", lambda e, i=kk, p=p2: e.tensor_reduce(
                    s_mxc.ap()[:, i:i + 1], g2tb[p], AX.X, ALU.max,
                    apply_absolute_value=True), inc="sv")
            if k < IT:
                wait("v", f"dsr{par}", dsr_after_c[k])
                do("v", lambda e, p=par: e.tensor_scalar(
                    g2tb[p], gld[p], s_sc.ap()[:, SC_RSG:SC_RSG + 1], MAGIC,
                    ALU.mult, ALU.add), inc="sv")
                free_ev[("gld", par)] = ("sv", n_of("sv"))
                sv_after_c[k] = n_of("sv")
            if 1 <= k <= IT:
                km = k - 1
                pm = km % 2
                wait("s", "sv", sv_after_c[km])
                wait_free("s", ("qg", pm))
                do("s", lambda e: e.preload_act_table(AF.Sigmoid))
                do("s", lambda e, p=pm: e.activation(
                    s_qg[p], g2tb[p], AF.Identity, bias=s_negM.ap()), inc="ss")
                do("s", lambda e, p=pm: e.activation(
                    sigtb[p], s_qg[p], AF.Sigmoid,
                    scale=s_sc.ap()[:, SC_SG:SC_SG + 1]), inc="ss")
                do("s", lambda e: e.preload_act_table(AF.Exp))
                ss_after_c[km] = n_of("ss")
            if k + 1 < IT:
                pn = (k + 1) % 2
                wait_free("y", ("gld", pn))
                do("y", lambda e, i=k + 1, p=pn: e.dma_start(
                    out=gld[p], in_=gtraw_sp[i]), inc=f"dsr{pn}", by=16)
                dsr_after_c[k + 1] = n_of(f"dsr{pn}")
            if k >= 2:
                kk = k - 2
                p2 = kk % 2
                wait("g", "ss", ss_after_c[kk])
                wait("g", "sv", n_of("sv"))
                do("g", lambda e, i=kk, p=p2: e.dma_start(
                    out=qg_sp[i], in_=s_qg[p]), inc=f"dspc{p2}", by=16)
                do("g", lambda e, i=kk, p=p2: e.dma_start(
                    out=qsig_sp[i], in_=s_qsig[p]), inc=f"dspc{p2}", by=16)
                free_ev[("qg", p2)] = (f"dspc{p2}", n_of(f"dspc{p2}"))
                free_ev[("qsig", p2)] = (f"dspc{p2}", n_of(f"dspc{p2}"))
            pump(4)
        do("v", lambda e: e.tensor_reduce(
            s_red.ap()[:, 0:1], s_mxc.ap(), AX.X, ALU.max), inc="sv")
        local_max_to_slot(s_red.ap()[:, 0:1], 0)
        all_reduce(2)
        pump(10)
        emit_scale_cols(2, [(0, SC_SG2, SC_RSG2)])

        HIT = 2 * IT
        NTH = NT // 2
        gqh = [pooli[:, 0:NTH], pooli[:, NTH:2 * NTH]]
        gsh = [poolu[:, 2 * NTH:3 * NTH], poolu[:, 3 * NTH:4 * NTH]]
        qg2h = [pooli[:, 4 * NTH:5 * NTH], pooli[:, 5 * NTH:6 * NTH]]
        quph = [pooli[:, 6 * NTH:7 * NTH], pooli[:, 7 * NTH:8 * NTH]]
        g2tD = [poolf[:, 8 * (NTH // 2):10 * (NTH // 2)],
                poolf[:, 10 * (NTH // 2):12 * (NTH // 2)]]
        sigtD = [poolf[:, 12 * (NTH // 2):14 * (NTH // 2)],
                 poolf[:, 14 * (NTH // 2):16 * (NTH // 2)]]
        uldh = [bigf[:, 0:NTH], bigf[:, NTH:2 * NTH]]
        s_mxd = sbM("s_mxd", [128, HIT])
        sv_after_d = {}
        ss_after_d = {}
        dsr_after_d = {}

        def d_load(e, k, p, sem):
            i, hh = k // 2, k % 2
            e.dma_start(out=gqh[p], in_=qg_sp[i][:, hh * NTH:(hh + 1) * NTH]).then_inc(sem, 16)
            e.dma_start(out=gsh[p], in_=qsig_sp[i][:, hh * NTH:(hh + 1) * NTH]).then_inc(sem, 16)
            e.dma_start(out=uldh[p], in_=upraw_sp[i][:, hh * NTH:(hh + 1) * NTH]).then_inc(sem, 16)
        wait("y", "dspc0", n_of("dspc0"))
        wait("y", "dspc1", n_of("dspc1"))
        wait("y", "sv", n_of("sv"))
        wait("y", "ss", n_of("ss"))
        wait("v", "dspc0", n_of("dspc0"))
        wait("v", "dspc1", n_of("dspc1"))
        wait("s", "dspc0", n_of("dspc0"))
        wait("s", "dspc1", n_of("dspc1"))
        cnt["dsr0"] = cnt.get("dsr0", 0) + 48
        prog["y"].append(lambda e: d_load(e, 0, 0, sems["dsr0"]))
        dsr_after_d[0] = n_of("dsr0")
        for k in range(HIT + 2):
            par = k % 2
            if k >= 2:
                kk = k - 2
                p2 = kk % 2
                wait("v", "ss", ss_after_d[kk])
                do("v", lambda e, p=p2: e.scalar_tensor_tensor(
                    g2tD[p], qg2h[p], s_sc.ap()[:, SC_SG2:SC_SG2 + 1], sigtD[p],
                    ALU.mult, ALU.mult), inc="sv")
                do("v", lambda e, i=kk, p=p2: e.tensor_reduce(
                    s_mxd.ap()[:, i:i + 1], g2tD[p], AX.X, ALU.max,
                    apply_absolute_value=True), inc="sv")
            if k < HIT:
                wait("v", f"dsr{par}", dsr_after_d[k])
                do("v", lambda e, p=par: e.scalar_tensor_tensor(
                    g2tD[p], gqh[p], s_sc.ap()[:, SC_SG16:SC_SG16 + 1], gsh[p],
                    ALU.mult, ALU.mult), inc="sv")
                free_ev[("gqh", par)] = ("sv", n_of("sv"))
                do("v", lambda e, p=par: e.tensor_scalar(
                    sigtD[p], g2tD[p], s_sc.ap()[:, SC_RSG2:SC_RSG2 + 1], MAGIC,
                    ALU.mult, ALU.add), inc="sv")
                do("v", lambda e, p=par: e.tensor_scalar(
                    g2tD[p], uldh[p], s_sc.ap()[:, SC_RSU:SC_RSU + 1], MAGIC,
                    ALU.mult, ALU.add), inc="sv")
                free_ev[("uldh", par)] = ("sv", n_of("sv"))
                sv_after_d[k] = n_of("sv")
            if 1 <= k <= HIT:
                km = k - 1
                pm = km % 2
                wait("s", "sv", sv_after_d[km])
                wait_free("s", ("qg2h", pm))
                do("s", lambda e, p=pm: e.activation(
                    qg2h[p], sigtD[p], AF.Identity, bias=s_negM.ap()), inc="ss")
                wait_free("s", ("quph", pm))
                do("s", lambda e, p=pm: e.activation(
                    quph[p], g2tD[p], AF.Identity, bias=s_negM.ap()), inc="ss")
                do("s", lambda e, p=pm: e.activation(
                    sigtD[p], quph[p], AF.Identity,
                    scale=s_sc.ap()[:, SC_SU:SC_SU + 1]), inc="ss")
                ss_after_d[km] = n_of("ss")
            if k + 1 < HIT:
                pn = (k + 1) % 2
                wait_free("y", ("gqh", pn))
                wait_free("y", ("uldh", pn))
                cnt[f"dsr{pn}"] = cnt.get(f"dsr{pn}", 0) + 48
                prog["y"].append(lambda e, kn=k + 1, p=pn, sm=sems[f"dsr{pn}"]: d_load(e, kn, p, sm))
                dsr_after_d[k + 1] = n_of(f"dsr{pn}")
            if k >= 2:
                kk = k - 2
                p2 = kk % 2
                it, hh = kk // 2, kk % 2
                wait("g", "ss", ss_after_d[kk])
                wait("g", "sv", n_of("sv"))
                do("g", lambda e, i=it, h_=hh, p=p2: e.dma_start(
                    out=qg2_sp[h_ * 4:(h_ + 1) * 4, i, :, :].rearrange("tc p c -> p tc c"),
                    in_=qg2h[p].rearrange("p (tc c) -> p tc c", c=512)), inc=f"dspd{p2}", by=16)
                do("g", lambda e, i=it, h_=hh, p=p2: e.dma_start(
                    out=qup_sp[h_ * 4:(h_ + 1) * 4, i, :, :].rearrange("tc p c -> p tc c"),
                    in_=quph[p].rearrange("p (tc c) -> p tc c", c=512)), inc=f"dspd{p2}", by=16)
                free_ev[("qg2h", p2)] = (f"dspd{p2}", n_of(f"dspd{p2}"))
                free_ev[("quph", p2)] = (f"dspd{p2}", n_of(f"dspd{p2}"))
            pump(2)
        do("v", lambda e: e.tensor_reduce(
            s_red.ap()[:, 0:1], s_mxd.ap(), AX.X, ALU.max), inc="sv")
        local_max_to_slot(s_red.ap()[:, 0:1], 0)
        all_reduce(3)
        pump(10)
        emit_scale_cols(3, [(0, SC_SO, SC_RSO)])

        wdT = s_pool.ap()[:, 0:IT * 1024].rearrange("p (i c) -> p i c", c=1024)
        s_oTx = s_pool.ap()[:, IT * 1024:IT * 1024 + NT]
        _spare = 14336
        s_up5 = [poolf[:, _spare:_spare + 512], poolf[:, _spare + 512:_spare + 1024]]
        s_or5 = poolf[:, _spare + 1024:_spare + 1536]
        s_or5b = poolf[:, _spare + 1536:_spare + 2048]
        s_fout = [None, None]
        p_e = [psum_f32(0), psum_f32(1)]
        def qg2_ap(it):
            return bigi[:, (it) * 512:(it + 1) * 512]
        def qup_ap(it):
            return bigi[:, (24 + it) * 512:(24 + it + 1) * 512]
        def oT_ap(it):
            if it < 16:
                return s_big.ap()[:, (6 * NT) + it * 512:(6 * NT) + (it + 1) * 512]
            return s_oTx[:, (it - 16) * 512:(it - 15) * 512]
        wait("y", "dspd0", n_of("dspd0"))
        wait("y", "dspd1", n_of("dspd1"))
        wait("y", "sv", n_of("sv"))
        for it in range(IT):
            do("y", lambda e, i=it: e.dma_start(
                out=wdT[:, i, :].rearrange("p (r h) -> p r h", r=NCORES),
                in_=ag_wd_out.rearrange("(r p) (it h) -> p r it h", p=128, h=128)[:, :, i, :]),
                inc="dw", by=16)
        for tc in range(NTC):
            wait_free("y", "ebig")
            wait_free("y", "ebig_s")
            for grp in range(3):
                do("y", lambda e, t_=tc, g=grp: e.dma_start(
                    out=bigi[:, g * 8 * 512:(g + 1) * 8 * 512].rearrange("p (i c) -> p i c", c=512),
                    in_=qg2_sp[t_, g * 8:(g + 1) * 8, :, :].rearrange("i p c -> p i c")),
                    inc="dse", by=16)
                do("y", lambda e, t_=tc, g=grp: e.dma_start(
                    out=bigi[:, (24 + g * 8) * 512:(24 + (g + 1) * 8) * 512].rearrange(
                        "p (i c) -> p i c", c=512),
                    in_=qup_sp[t_, g * 8:(g + 1) * 8, :, :].rearrange("i p c -> p i c")),
                    inc="dse", by=16)
            wait("s", "dse", n_of("dse"))
            wait("v", "dse", n_of("dse"))
            wait_free("v", "eoT")
            for it in range(IT):
                up_par = it % 2
                wait_free("s", ("up5", up_par))
                do("s", lambda e, i=it, p=up_par: e.activation(
                    s_up5[p], qup_ap(i), AF.Identity,
                    scale=s_sc.ap()[:, SC_SU:SC_SU + 1]), inc="ss")
                wait("v", "ss", n_of("ss"))
                do("v", lambda e, i=it, p=up_par: e.scalar_tensor_tensor(
                    s_or5, qg2_ap(i), s_sc.ap()[:, SC_SG2:SC_SG2 + 1], s_up5[p],
                    ALU.mult, ALU.mult), inc="sv")
                free_ev[("up5", up_par)] = ("sv", n_of("sv"))
                do("v", lambda e: e.tensor_scalar(
                    s_or5b, s_or5, s_sc.ap()[:, SC_RSO:SC_RSO + 1], MAGIC,
                    ALU.mult, ALU.add), inc="sv")
                do("v", lambda e, i=it: e.tensor_scalar(
                    oT_ap(i), s_or5b, MAGIC, s_sc.ap()[:, SC_SO:SC_SO + 1],
                    ALU.subtract, ALU.mult), inc="sv")
            free_ev["ebig"] = ("sv", n_of("sv"))
            free_ev["ebig_s"] = ("ss", n_of("ss"))
            if tc == 0:
                pump(10000)
                barrier()
                phAT.close()
                s_fout[0] = sbM("s_fout0", [128, 1024])
                s_fout[1] = sbM("s_fout1", [128, 1024])
                wo_sb = sbM("wo_sb", [128, 8, 1024], BF16)
                s_yld = [sbM(f"s_yld{i}", [128, 8, 128], BF16) for i in range(2)]
                s_ylhs = [sbM(f"s_ylhs{i}", [128, 8, 128], BF16) for i in range(2)]
                s_xw = [sbM(f"s_xw{i}", [128, 1024]) for i in range(2)]
                s_ow_ap = poolf[:, _spare + 1024:_spare + 2048]
                p_tr = [psum_bf16(2), psum_bf16(3)]
                p_w = [psum_f32(4), psum_f32(5)]
                for ch in range(8):
                    wait_free("y", ("xw", ch % 2))
                    do("y", lambda e, c=ch, p=(ch % 2): e.dma_start(
                        out=s_xw[p].ap(),
                        in_=wo_h.rearrange("(mt k) d -> k mt d", k=128)[:, c, :]),
                        inc=f"dx{ch % 2}", by=16)
                    wait("v", f"dx{ch % 2}", n_of(f"dx{ch % 2}"))
                    do("v", lambda e, c=ch, p=(ch % 2): e.tensor_copy(
                        wo_sb.ap()[:, c, :], s_xw[p].ap()), inc="sv")
                    free_ev[("xw", ch % 2)] = ("sv", n_of("sv"))
                WOC = [n_of("sv")]
                wait("y", "dyt", n_of("dyt"))
            wait("t", "sv", n_of("sv"))
            wait("t", "dw", n_of("dw"))
            ng = [0]
            for ttl in range(4):
                gtt = tc * 4 + ttl
                b, tr = gtt // (NTT // BL), (gtt % (NTT // BL)) * 128
                wpar = gtt % 2
                wait_free("y", ("xw", wpar))
                do("y", lambda e, b_=b, t_=tr, p=wpar: e.dma_start(
                    out=s_xw[p].ap(), in_=x_h[b_, t_:t_ + 128, :]), inc=f"dx{wpar}", by=16)
                wait_free("y", ("yld", wpar))
                do("y", lambda e, c=gtt, p=wpar: e.dma_start(
                    out=s_yld[p].ap(),
                    in_=yt_sp.rearrange("d p n -> p d n")[:, :, c * 128:(c + 1) * 128]),
                    inc=f"dwy{wpar}", by=16)
                for mc in range(2):
                    slot = ng[0] % 2
                    if n_of("emm") >= 2:
                        wait("t", "eev", n_of("emm") - 1)
                    for it in range(IT):
                        do("t", lambda e, i=it, tl=ttl, m=mc, sl=slot: e.matmul(
                            p_e[sl], oT_ap(i)[:, tl * 128:(tl + 1) * 128],
                            wdT[:, i, m * 512:(m + 1) * 512],
                            start=(i == 0), stop=(i == IT - 1)),
                            inc=("emm" if it == IT - 1 else None))
                    wait("s", "emm", n_of("emm"))
                    if mc == 0:
                        wait_free("s", ("fout", ttl % 2))
                    do("s", lambda e, sl=slot, m=mc, tl=ttl: e.activation(
                        s_fout[tl % 2].ap()[:, m * 512:(m + 1) * 512], p_e[sl],
                        AF.Copy), inc="eev")
                    ng[0] += 1
                wait("t", f"dwy{wpar}", n_of(f"dwy{wpar}"))
                wait_free("t", ("ptr", wpar))
                for dt in range(DT):
                    do("t", lambda e, d=dt, p=wpar: e.transpose(
                        p_tr[p][:, d * 128:(d + 1) * 128], s_yld[p].ap()[:, d, :],
                        s_id.ap()), inc=("wtr" if dt == DT - 1 else None))
                free_ev[("yld", wpar)] = ("wtr", n_of("wtr"))
                wait("s", "wtr", n_of("wtr"))
                wait_free("s", ("ylhs", wpar))
                do("s", lambda e, p=wpar: e.activation(
                    s_ylhs[p].ap().rearrange("p a b -> p (a b)"), p_tr[p][:, 0:1024],
                    AF.Copy), inc="wev")
                free_ev[("ptr", wpar)] = ("wev", n_of("wev"))
                wait("t", "sv", WOC[0])
                wait("t", "wev", n_of("wev"))
                for mc in range(2):
                    if gtt > 0 or mc > 0:
                        wait("t", "wadd", max(0, 2 * gtt + mc - 1))
                    for dt in range(DT):
                        do("t", lambda e, d=dt, m=mc, p=wpar: e.matmul(
                            p_w[m], s_ylhs[p].ap()[:, d, :],
                            wo_sb.ap()[:, d, m * 512:(m + 1) * 512],
                            start=(d == 0), stop=(d == DT - 1)),
                            inc=("wmm" if dt == DT - 1 else None))
                free_ev[("ylhs", wpar)] = ("wmm", n_of("wmm"))
                wait("v", "wmm", n_of("wmm"))
                wait("v", f"dx{wpar}", n_of(f"dx{wpar}"))
                wait("v", "eev", n_of("eev"))
                for mc in range(2):
                    do("v", lambda e, m=mc, p=wpar: e.tensor_tensor(
                        s_ow_ap[:, m * 512:(m + 1) * 512], p_w[m],
                        s_xw[p].ap()[:, m * 512:(m + 1) * 512], ALU.add), inc=None)
                    do("v", lambda e, m=mc, p=wpar, tl=ttl: e.tensor_tensor(
                        s_xw[p].ap()[:, m * 512:(m + 1) * 512],
                        s_ow_ap[:, m * 512:(m + 1) * 512],
                        s_fout[tl % 2].ap()[:, m * 512:(m + 1) * 512], ALU.add), inc="wadd")
                free_ev[("fout", ttl % 2)] = ("wadd", n_of("wadd"))
                wait("g", "wadd", n_of("wadd"))
                do("g", lambda e, b_=b, t_=tr, p=wpar: e.dma_start(
                    out=out_h[b_, t_:t_ + 128, :], in_=s_xw[p].ap()), inc=f"do{wpar}", by=16)
                free_ev[("xw", wpar)] = (f"do{wpar}", n_of(f"do{wpar}"))
            free_ev["eoT"] = ("emm", n_of("emm"))
        wait("g", "do0", n_of("do0"))
        wait("g", "do1", n_of("do1"))
        wait("y", "do0", n_of("do0"))
        wait("y", "do1", n_of("do1"))
        phM.close()

        block = top.enter_context(nc.Block())
        @block.sync
        def _(eng):
            for f in prog["y"]:
                f(eng)
        @block.vector
        def _(eng):
            for f in prog["v"]:
                f(eng)
        @block.scalar
        def _(eng):
            for f in prog["s"]:
                f(eng)
        @block.tensor
        def _(eng):
            for f in prog["t"]:
                f(eng)
        @block.gpsimd
        def _(eng):
            for f in prog["g"]:
                f(eng)

    return nc


_NC = None


def kernel(**inputs):
    global _NC
    x = np.ascontiguousarray(inputs["L0_attn_x"])
    wq = np.ascontiguousarray(inputs["L0_attn_wq"])
    wkd = np.ascontiguousarray(inputs["L0_attn_wk_down"])
    wvd = np.ascontiguousarray(inputs["L0_attn_wv_down"])
    wku = np.ascontiguousarray(inputs["L0_attn_wk_up"])
    wvu = np.ascontiguousarray(inputs["L0_attn_wv_up"])
    wo = np.ascontiguousarray(inputs["L0_attn_wo"])
    wgate = np.ascontiguousarray(inputs["L0_ffn_w_gate"])
    wup = np.ascontiguousarray(inputs["L0_ffn_w_up"])
    wdown = np.ascontiguousarray(inputs["L0_ffn_w_down"])

    ident = np.eye(128, dtype=ml_dtypes.bfloat16)
    cmask = np.where(np.tril(np.ones((128, 128), bool)), 0.0, -1e38).astype(ml_dtypes.bfloat16)

    if _NC is None:
        _NC = build_nc()
    nc = _NC

    in_maps = []
    for c in range(NCORES):
        in_maps.append(dict(
            x=x[c * BL:(c + 1) * BL],
            wq=wq, wk_down=wkd, wv_down=wvd, wk_up=wku, wv_up=wvu, wo=wo,
            wup_shard=wup[c * ITL * 128:(c + 1) * ITL * 128],
            wgate_shard=wgate[c * ITL * 128:(c + 1) * ITL * 128],
            wdown_shard=wdown[c * 128:(c + 1) * 128],
            ident=ident, cmask=cmask,
        ))
    res = run_bass_kernel_spmd(nc, in_maps, core_ids=list(range(NCORES)))
    out = np.concatenate([r["out"] for r in res.results], axis=0)
    return out


# version 13
# speedup vs baseline: 1.0482x; 1.0482x over previous
import numpy as np
import ml_dtypes
import contextlib
import concourse.bass as bass
import concourse.mybir as mybir
from concourse.bass_utils import run_bass_kernel_spmd

F32 = mybir.dt.float32
BF16 = mybir.dt.bfloat16
I16 = mybir.dt.int16
U16 = mybir.dt.uint16
AF = mybir.ActivationFunctionType
ALU = mybir.AluOpType
AX = mybir.AxisListType

NCORES = 8
BL = 2
T = 2048
NT = BL * T
D = 1024
DT = D // 128
I3 = 3072
IT = I3 // 128
ITL = IT // NCORES
KV, REP, HD, LAT = 2, 4, 128, 32
NTT = NT // 128
NTC = NT // 512
NQT = T // 128
MAGIC = 12582912.0
INV_QMAX = float(np.float32(1.0) / np.float32(32767.0))
INV7 = float(np.float32(1.0) / np.float32(7.0))
EPS = 1e-5
FMAX = 3.402823466e38
P2_16 = float(np.float32(2.0) ** -16)
SCALE_ATT = float(np.float32(HD) ** np.float32(-0.5))

SC_SX, SC_RSX, SC_SU, SC_RSU, SC_SG, SC_RSG, SC_SG16, SC_SG2, SC_RSG2, SC_SO, SC_RSO = range(11)

SEM_NAMES = [
    "ph", "dx0", "dx1", "dw", "dqt0", "dqt1", "dsp", "dse", "dqtr",
    "dwl0", "dwl1", "dspb0", "dspb1", "dsr0", "dsr1",
    "dspc0", "dspc1", "dspd0", "dspd1", "dspe0", "dspe1",
    "dmig", "dmiy", "do0", "do1", "cc", "dwi",
    "sv", "ss", "st2", "gv",
    "lmm", "lev", "kmm", "kev", "vmm", "vev", "qmm", "qev",
    "bmm", "brd", "bev",
    "a_sc", "a_ngm", "a_ex", "a_nm", "a_tr", "a_pt", "a_pv", "a_ye", "a_rc", "a_yn",
    "a_ar", "dyt", "dwy0", "dwy1",
    "emm", "eev", "wmm", "wadd", "wtr", "wev",
]


def build_nc(dbg=False):
    nc = bass.Bass("TRN2", target_bir_lowering=False, detect_race_conditions=False)
    DKIND = "ExternalOutput" if dbg else "Internal"

    x_h = nc.dram_tensor("x", [BL, T, D], F32, kind="ExternalInput").ap()
    wq_h = nc.dram_tensor("wq", [D, D], F32, kind="ExternalInput").ap()
    wkd_h = nc.dram_tensor("wk_down", [D, KV * LAT], F32, kind="ExternalInput").ap()
    wvd_h = nc.dram_tensor("wv_down", [D, KV * LAT], F32, kind="ExternalInput").ap()
    wku_h = nc.dram_tensor("wk_up", [LAT, HD], F32, kind="ExternalInput").ap()
    wvu_h = nc.dram_tensor("wv_up", [LAT, HD], F32, kind="ExternalInput").ap()
    wo_h = nc.dram_tensor("wo", [D, D], F32, kind="ExternalInput").ap()
    wup_sh_h = nc.dram_tensor("wup_shard", [ITL * 128, D], F32, kind="ExternalInput").ap()
    wg_sh_h = nc.dram_tensor("wgate_shard", [ITL * 128, D], F32, kind="ExternalInput").ap()
    wd_sh_h = nc.dram_tensor("wdown_shard", [128, I3], F32, kind="ExternalInput").ap()
    ident_h = nc.dram_tensor("ident", [128, 128], BF16, kind="ExternalInput").ap()
    cmask_h = nc.dram_tensor("cmask", [128, 128], BF16, kind="ExternalInput").ap()
    out_h = nc.dram_tensor("out", [BL, T, D], F32, kind="ExternalOutput").ap()

    qt_sp = nc.dram_tensor("qt_sp", [DT, 128, NT], BF16, kind=DKIND).ap()
    yt_sp = nc.dram_tensor("yt_sp", [DT, 128, NT], BF16).ap()
    upraw_sp = nc.dram_tensor("upraw_sp", [IT, 128, NT], F32, kind=DKIND).ap()
    gtraw_sp = nc.dram_tensor("gtraw_sp", [IT, 128, NT], F32, kind=DKIND).ap()
    qg_sp = nc.dram_tensor("qg_sp", [IT, 128, NT], I16, kind=DKIND).ap()
    qsig_sp = nc.dram_tensor("qsig_sp", [IT, 128, NT], U16, kind=DKIND).ap()
    qg2_sp = nc.dram_tensor("qg2_sp", [NTC, IT, 128, 512], I16, kind=DKIND).ap()
    qup_sp = nc.dram_tensor("qup_sp", [NTC, IT, 128, 512], I16, kind=DKIND).ap()
    ffn_sp = nc.dram_tensor("ffn_sp", [BL, T, D], F32, kind=DKIND).ap()
    flip_hb = nc.dram_tensor("flip_hb", [4, 128], F32).ap()
    rflip_h = [nc.dram_tensor(f"rflip{i}", [16, 128], F32).ap() for i in range(2)]
    cc_in = [nc.dram_tensor(f"cc_in{i}", [1, 16], F32).ap() for i in range(4)]
    cc_out = [nc.dram_tensor(f"cc_out{i}", [1, 16], F32).ap() for i in range(4)]
    sc_dump = nc.dram_tensor("sc_dump", [128, 16], F32, kind=DKIND).ap() if dbg else None
    ln_dump = nc.dram_tensor("ln_dump", [128, 8 * NTT], F32, kind=DKIND).ap() if dbg else None
    big_dump = nc.dram_tensor("big_dump", [128, DT * NT], U16, kind=DKIND).ap() if dbg else None
    ag_wu_in = nc.dram_tensor("ag_wu_in", [128, I3], BF16).ap()
    ag_wg_in = nc.dram_tensor("ag_wg_in", [128, I3], BF16).ap()
    ag_wd_in = nc.dram_tensor("ag_wd_in", [128, I3], BF16).ap()
    ag_wu_out = nc.dram_tensor("ag_wu_out", [NCORES * 128, I3], BF16, addr_space="Shared").ap()
    ag_wg_out = nc.dram_tensor("ag_wg_out", [NCORES * 128, I3], BF16, addr_space="Shared").ap()
    ag_wd_out = nc.dram_tensor("ag_wd_out", [NCORES * 128, I3], BF16, addr_space="Shared").ap()

    prog = {k: [] for k in ("y", "v", "s", "t", "g")}
    cnt = {}
    sems = {}
    free_ev = {}

    def n_of(sem):
        return cnt.get(sem, 0)

    def do(eng, fn, inc=None, by=1, dr=None):
        if dr is None:
            dr = eng in ("v", "s")
        if eng in ("v", "s"):
            prog[eng].append(fn)
            if dr:
                prog[eng].append(lambda e: e.drain())
            if inc is not None:
                cnt[inc] = cnt.get(inc, 0) + by
                if dr:
                    prog[eng].append(lambda e, s=inc, b=by: e.nop().then_inc(sems[s], b))
                else:
                    prog[eng].pop()
                    prog[eng].append(lambda e, f=fn, s=inc, b=by: f(e).then_inc(sems[s], b))
        else:
            if inc is None:
                prog[eng].append(fn)
            else:
                cnt[inc] = cnt.get(inc, 0) + by
                prog[eng].append(lambda e, f=fn, s=inc, b=by: f(e).then_inc(sems[s], b))

    def wait(eng, sem, val=None):
        v = cnt.get(sem, 0) if val is None else val
        if v <= 0:
            return
        prog[eng].append(lambda e, s=sem, vv=v: e.wait_ge(sems[s], vv))

    def set_free(key, sem):
        free_ev[key] = (sem, n_of(sem))

    def wait_free(eng, key):
        if key in free_ev:
            s, v = free_ev[key]
            wait(eng, s, v)

    nbar = [0]
    def barrier():
        nbar[0] += 1
        for e in ("y", "v", "s", "t", "g"):
            cnt["ph"] = cnt.get("ph", 0) + 1
            if e in ("v", "s"):
                prog[e].append(lambda en: en.drain())
            prog[e].append(lambda en: en.nop().then_inc(sems["ph"], 1))
        tgt = 5 * nbar[0]
        for e in ("y", "v", "s", "t", "g"):
            prog[e].append(lambda en, t_=tgt: en.wait_ge(sems["ph"], t_))

    with contextlib.ExitStack() as top:
        sb = lambda name, shape, dt=F32: top.enter_context(nc.sbuf_tensor(name, shape, dt))
        for s in SEM_NAMES:
            sems[s] = top.enter_context(nc.semaphore(s))

        PSUM = [top.enter_context(nc.psum_tensor(f"P{i}", [128, 512], F32)) for i in range(8)]
        psum_f32 = lambda i: PSUM[i].ap()
        psum_bf16 = lambda i: PSUM[i].ap().bitcast(BF16)
        s_big = sb("s_big", [128, DT * NT], BF16)
        bigv = lambda d: s_big.ap()[:, d * NT:(d + 1) * NT]
        bigf = s_big.ap().bitcast(F32)
        bigi = s_big.ap().bitcast(I16)
        s_kT = [sb(f"kT{h}", [128, NT], BF16) for h in range(KV)]
        s_v = [sb(f"vT{h}", [128, NT], BF16) for h in range(KV)]
        s_sc = sb("s_sc", [128, 16])
        s_negM = sb("s_negM", [128, 1])
        s_id = sb("s_id", [128, 128], BF16)
        s_cm = sb("s_cm", [128, 128], BF16)
        s_sum = sb("s_sum", [128, NTT]); s_ssq = sb("s_ssq", [128, NTT])
        s_mx = sb("s_mx", [128, NTT]); s_mn = sb("s_mn", [128, NTT])
        s_m = sb("s_m", [128, NTT]); s_rstd = sb("s_rstd", [128, NTT])
        s_t1 = sb("s_t1", [128, NTT]); s_t2 = sb("s_t2", [128, NTT]); s_t3 = sb("s_t3", [128, NTT])
        s_red = sb("s_red", [128, 4])
        s_flip = sb("s_flip", [1, 4 * 128])
        s_loc = sb("s_loc", [1, 16])
        s_bc = sb("s_bc", [128, 16])

        def local_max_to_slot(src_col_ap, slot):
            wait("g", "sv", n_of("sv"))
            do("g", lambda e, a=src_col_ap, f=slot: e.dma_start(
                out=flip_hb[f:f + 1, :], in_=a), inc="dmig", by=16)
            wait("g", "dmig", n_of("dmig"))
            do("g", lambda e, f=slot: e.dma_start(
                out=s_flip.ap()[0:1, f * 128:(f + 1) * 128], in_=flip_hb[f:f + 1, :]), inc="dmig", by=16)
            wait("v", "dmig", n_of("dmig"))
            do("v", lambda e, f=slot: e.tensor_reduce(
                s_loc.ap()[0:1, f:f + 1], s_flip.ap()[0:1, f * 128:(f + 1) * 128], AX.X, ALU.max), inc="sv")

        def all_reduce(ridx):
            wait("g", "sv", n_of("sv"))
            do("g", lambda e, r=ridx: e.dma_start(out=cc_in[r][0:1, :], in_=s_loc.ap()[0:1, :]),
               inc="dmig", by=16)
            wait("g", "dmig", n_of("dmig"))
            do("g", lambda e, r=ridx: e.collective_compute(
                "AllReduce", ALU.max, replica_groups=[list(range(NCORES))],
                ins=[cc_in[r]], outs=[cc_out[r]]), inc="cc")

        def emit_scale_cols(ridx, pairs):
            wait("y", "cc", n_of("cc"))
            do("y", lambda e, r=ridx: e.dma_start(
                out=s_bc.ap(), in_=cc_out[r].broadcast_to([128, 16])), inc="dmiy", by=16)
            wait("v", "dmiy", n_of("dmiy"))
            for (bcol, scol, rscol) in pairs:
                do("v", lambda e, b=bcol, s=scol: e.tensor_scalar(
                    s_sc.ap()[:, s:s + 1], s_bc.ap()[:, b:b + 1], INV_QMAX, 1e-12,
                    ALU.mult, ALU.max), inc="sv")
                do("v", lambda e, s=scol, rs=rscol: e.reciprocal(
                    s_sc.ap()[:, rs:rs + 1], s_sc.ap()[:, s:s + 1]), inc="sv")

        do("v", lambda e: e.memset(s_negM.ap(), -MAGIC))
        do("v", lambda e: e.memset(s_loc.ap(), 0.0))
        do("y", lambda e: e.dma_start(out=s_id.ap(), in_=ident_h), inc="dwi", by=16)
        do("y", lambda e: e.dma_start(out=s_cm.ap(), in_=cmask_h), inc="dwi", by=16)

        ph0 = contextlib.ExitStack()
        sb0 = lambda name, shape, dt=F32: ph0.enter_context(nc.sbuf_tensor(name, shape, dt))
        ps0 = lambda name, shape, dt=F32: ph0.enter_context(nc.psum_tensor(name, shape, dt))

        s_wst = sb0("s_wst", [128, 4, 1024])
        wq_sb = sb0("wq_sb", [128, 8, 1024], BF16)
        wkd_sb = sb0("wkd_sb", [128, 8, 64], BF16)
        wvd_sb = sb0("wvd_sb", [128, 8, 64], BF16)
        wku_sb = sb0("wku_sb", [64, 128], BF16)
        wvu_sb = sb0("wvu_sb", [64, 128], BF16)
        s_updu = sb0("s_updu", [64, 128])

        for half in range(2):
            wait("y", "sv", n_of("sv"))
            do("y", lambda e, hf=half: e.dma_start(
                out=s_wst.ap(),
                in_=wq_h.rearrange("(mt k) d -> k mt d", k=128)[:, hf * 4:(hf + 1) * 4, :]),
                inc="dw", by=16)
            wait("v", "dw", n_of("dw"))
            do("v", lambda e, hf=half: e.tensor_copy(
                wq_sb.ap()[:, hf * 4:(hf + 1) * 4, :], s_wst.ap()), inc="sv")
        s_wkv = sb0("s_wkv", [128, 8, 64])
        for (h_ap, dst) in ((wkd_h, wkd_sb), (wvd_h, wvd_sb)):
            wait("y", "sv", n_of("sv"))
            do("y", lambda e, a=h_ap: e.dma_start(
                out=s_wkv.ap(), in_=a.rearrange("(mt k) d -> k mt d", k=128)), inc="dw", by=16)
            wait("v", "dw", n_of("dw"))
            do("v", lambda e, d=dst: e.tensor_copy(d.ap(), s_wkv.ap()), inc="sv")
        for (h_ap, dst) in ((wku_h, wku_sb), (wvu_h, wvu_sb)):
            wait("y", "sv", n_of("sv"))
            do("y", lambda e, a=h_ap: e.dma_start(out=s_updu.ap()[0:32, :], in_=a), inc="dw", by=16)
            do("y", lambda e, a=h_ap: e.dma_start(out=s_updu.ap()[32:64, :], in_=a), inc="dw", by=16)
            wait("v", "dw", n_of("dw"))
            do("v", lambda e, d=dst: e.tensor_copy(d.ap(), s_updu.ap()), inc="sv")

        s_wsh = sb0("s_wsh", [128, ITL, 1024])
        s_maxb = sb0("s_maxb", [128, 32])
        s_sb_ = sb0("s_sb_", [128, 32])
        s_rsb = sb0("s_rsb", [128, 32])
        s_wr = sb0("s_wr", [128, 1024])
        s_wrr = sb0("s_wrr", [128, 1024])
        s_wdq = sb0("s_wdq", [128, 1024], BF16)
        s_ag = sb0("s_ag", [128, I3], BF16)
        p_trw = None

        def dq_tile(tile_ap, out_col_fn):
            nb = 32
            do("v", lambda e, a=tile_ap: e.tensor_reduce(
                s_maxb.ap(), a.rearrange("p (b w) -> p b w", w=32),
                AX.X, ALU.max, apply_absolute_value=True), inc="gv")
            do("v", lambda e: e.tensor_scalar(
                s_sb_.ap(), s_maxb.ap(), INV7, 1e-12, ALU.mult, ALU.max), inc="gv")
            do("v", lambda e: e.reciprocal(s_rsb.ap(), s_sb_.ap()), inc="gv")
            do("v", lambda e, a=tile_ap: e.tensor_tensor(
                s_wr.ap().rearrange("p (b w) -> p b w", w=32),
                a.rearrange("p (b w) -> p b w", w=32),
                s_rsb.ap().unsqueeze(2).broadcast_to([128, nb, 32]), ALU.mult), inc="gv")
            do("v", lambda e: e.tensor_scalar(
                s_wrr.ap(), s_wr.ap(), MAGIC, MAGIC, ALU.add, ALU.subtract), inc="gv")
            wait("v", "st2", n_of("st2"))
            do("v", lambda e: e.tensor_tensor(
                s_wdq.ap().rearrange("p (b w) -> p b w", w=32),
                s_wrr.ap().rearrange("p (b w) -> p b w", w=32),
                s_sb_.ap().unsqueeze(2).broadcast_to([128, nb, 32]), ALU.mult), inc="gv")
            wait("t", "gv", n_of("gv"))
            wait("t", "st2", n_of("st2"))
            for mt in range(8):
                do("t", lambda e, m=mt: e.transpose(
                    psum_bf16(0)[:, m * 128:(m + 1) * 128],
                    s_wdq.ap()[:, m * 128:(m + 1) * 128], s_id.ap()),
                    inc=("dqtr" if mt == 7 else None))
            wait("s", "dqtr", n_of("dqtr"))
            for mg in range(8):
                do("s", lambda e, m=mg, oc=out_col_fn(mg): e.activation(
                    s_ag.ap()[:, oc:oc + 128],
                    psum_bf16(0)[:, m * 128:(m + 1) * 128], AF.Copy), inc="st2")

        wait("t", "dwi", 32)
        for (src_h, ag_h) in ((wup_sh_h, ag_wu_in), (wg_sh_h, ag_wg_in)):
            wait("y", "sv", n_of("sv"))
            wait("y", "st2", n_of("st2"))
            do("y", lambda e, s=src_h: e.dma_start(
                out=s_wsh.ap(), in_=s.rearrange("(t k) m -> k t m", k=128)), inc="dw", by=16)
            wait("v", "dw", n_of("dw"))
            for tl in range(ITL):
                dq_tile(s_wsh.ap()[:, tl, :], lambda mg, t=tl: mg * (ITL * 128) + t * 128)
            wait("g", "st2", n_of("st2"))
            do("g", lambda e, a=ag_h: e.dma_start(out=a, in_=s_ag.ap()), inc="dsp", by=16)
        wait("y", "st2", n_of("st2"))
        do("y", lambda e: e.dma_start(
            out=s_wsh.ap().rearrange("k t m -> k (t m)"), in_=wd_sh_h), inc="dw", by=16)
        wait("v", "dw", n_of("dw"))
        for sl in range(3):
            dq_tile(s_wsh.ap()[:, sl, :], lambda mg, s=sl: (s * 8 + mg) * 128)
        wait("g", "st2", n_of("st2"))
        do("g", lambda e: e.dma_start(out=ag_wd_in, in_=s_ag.ap()), inc="dsp", by=16)
        wait("g", "dsp", n_of("dsp"))
        for (ai, ao) in ((ag_wu_in, ag_wu_out), (ag_wg_in, ag_wg_out), (ag_wd_in, ag_wd_out)):
            do("g", lambda e, a=ai, b=ao: e.collective_compute(
                "AllGather", ALU.bypass, replica_groups=[list(range(NCORES))],
                ins=[a], outs=[b]), inc="cc")
        CC_AG = n_of("cc")

        do("s", lambda e: e.preload_act_table(AF.Square))
        s_xt = [sb0(f"s_xt{i}", [128, 1024]) for i in range(2)]
        s_xnb = sb0("s_xnb", [128, 1024], BF16)
        s_xn32 = sb0("s_xn32", [128, 1024])
        s_xn32b = sb0("s_xn32b", [128, 1024])

        def wait_xt_free(eng, par):
            wait_free(eng, ("xt", par))
            wait_free(eng, ("xt", par, "s"))

        for tt in range(NTT):
            b, tr = tt // (NTT // BL), (tt % (NTT // BL)) * 128
            par = tt % 2
            dxp = f"dx{par}"
            wait_xt_free("y", par)
            do("y", lambda e, b_=b, t_=tr, p=par: e.dma_start(
                out=s_xt[p].ap(), in_=x_h[b_, t_:t_ + 128, :]), inc=dxp, by=16)
            wait("v", dxp, n_of(dxp))
            do("v", lambda e, p=par, c=tt: e.tensor_reduce(
                s_sum.ap()[:, c:c + 1], s_xt[p].ap(), AX.X, ALU.add), inc="sv", dr=False)
            do("v", lambda e, p=par, c=tt: e.tensor_reduce(
                s_mx.ap()[:, c:c + 1], s_xt[p].ap(), AX.X, ALU.max), inc="sv", dr=False)
            do("v", lambda e, p=par, c=tt: e.tensor_reduce(
                s_mn.ap()[:, c:c + 1], s_xt[p].ap(), AX.X, ALU.min), inc="sv", dr=False)
            wait("s", dxp, n_of(dxp))
            do("s", lambda e, p=par, c=tt: e.activation(
                s_xnb.ap(), s_xt[p].ap(), AF.Square, accum_out=s_ssq.ap()[:, c:c + 1]), inc="ss")
            set_free(("xt", par), "sv")
            free_ev[("xt", par)] = ("sv", n_of("sv"))
            free_ev[("xt", par, "s")] = ("ss", n_of("ss"))
        wait("v", "ss", n_of("ss"))
        prog["v"].append(lambda e: e.drain())
        if dbg:
            wait("g", "sv", n_of("sv"))
            wait("g", "ss", n_of("ss"))
            for di, tns in enumerate((s_sum, s_ssq, s_mx, s_mn)):
                do("g", lambda e, d=di, t_=tns: e.dma_start(
                    out=ln_dump[:, (2 + d) * NTT:(3 + d) * NTT], in_=t_.ap()), inc="dmig", by=16)
            wait("v", "dmig", n_of("dmig"))
        do("v", lambda e: e.tensor_scalar(s_m.ap(), s_sum.ap(), 1.0 / D, None, ALU.mult), inc="sv")
        do("v", lambda e: e.tensor_tensor(s_t1.ap(), s_m.ap(), s_m.ap(), ALU.mult), inc="sv")
        do("v", lambda e: e.tensor_scalar(s_t2.ap(), s_ssq.ap(), 1.0 / D, None, ALU.mult), inc="sv")
        do("v", lambda e: e.tensor_tensor(s_sum.ap(), s_t2.ap(), s_t1.ap(), ALU.subtract), inc="sv")
        do("v", lambda e: e.tensor_scalar(s_t1.ap(), s_sum.ap(), EPS, None, ALU.add), inc="sv")
        wait("s", "sv", n_of("sv"))
        do("s", lambda e: e.preload_act_table(AF.Sqrt))
        do("s", lambda e: e.activation(s_t2.ap(), s_t1.ap(), AF.Sqrt), inc="ss")
        wait("v", "ss", n_of("ss"))
        prog["v"].append(lambda e: e.drain())
        if dbg:
            wait("g", "ss", n_of("ss"))
            do("g", lambda e: e.dma_start(out=ln_dump[:, 6 * NTT:7 * NTT], in_=s_t1.ap()), inc="dmig", by=16)
            do("g", lambda e: e.dma_start(out=ln_dump[:, 7 * NTT:8 * NTT], in_=s_t2.ap()), inc="dmig", by=16)
            wait("v", "dmig", n_of("dmig"))
        for _ in range(2):
            do("v", lambda e: e.reciprocal(s_ssq.ap(), s_t2.ap()), inc="sv")
            do("v", lambda e: e.tensor_tensor(s_sum.ap(), s_t1.ap(), s_ssq.ap(), ALU.mult), inc="sv")
            do("v", lambda e: e.tensor_tensor(s_t3.ap(), s_t2.ap(), s_sum.ap(), ALU.add), inc="sv")
            do("v", lambda e: e.tensor_scalar(s_t2.ap(), s_t3.ap(), 0.5, None, ALU.mult), inc="sv")
        do("v", lambda e: e.reciprocal(s_rstd.ap(), s_t2.ap()), inc="sv")
        do("v", lambda e: e.tensor_tensor(s_sum.ap(), s_mx.ap(), s_m.ap(), ALU.subtract), inc="sv")
        do("v", lambda e: e.tensor_tensor(s_ssq.ap(), s_m.ap(), s_mn.ap(), ALU.subtract), inc="sv")
        do("v", lambda e: e.tensor_tensor(s_t3.ap(), s_sum.ap(), s_ssq.ap(), ALU.max), inc="sv")
        do("v", lambda e: e.tensor_tensor(s_sum.ap(), s_t3.ap(), s_rstd.ap(), ALU.mult), inc="sv")
        do("v", lambda e: e.tensor_reduce(s_red.ap()[:, 0:1], s_sum.ap(), AX.X, ALU.max), inc="sv")
        if dbg:
            wait("g", "sv", n_of("sv"))
            for di, tns in enumerate((s_m, s_rstd)):
                do("g", lambda e, d=di, t_=tns: e.dma_start(
                    out=ln_dump[:, d * NTT:(d + 1) * NTT], in_=t_.ap()), inc="dmig", by=16)
            wait("g", "dmig", n_of("dmig"))
        local_max_to_slot(s_red.ap()[:, 0:1], 0)
        all_reduce(0)
        emit_scale_cols(0, [(0, SC_SX, SC_RSX)])

        def emit_xpass(quant):
            for tt in range(NTT):
                b, tr = tt // (NTT // BL), (tt % (NTT // BL)) * 128
                par = tt % 2
                dxp = f"dx{par}"
                wait_xt_free("y", par)
                do("y", lambda e, b_=b, t_=tr, p=par: e.dma_start(
                    out=s_xt[p].ap(), in_=x_h[b_, t_:t_ + 128, :]), inc=dxp, by=16)
                wait("v", dxp, n_of(dxp))
                wait_free("v", "xnb")
                if not quant:
                    do("v", lambda e, p=par, c=tt: e.tensor_scalar(
                        s_xnb.ap(), s_xt[p].ap(), s_m.ap()[:, c:c + 1], s_rstd.ap()[:, c:c + 1],
                        ALU.subtract, ALU.mult), inc="sv")
                else:
                    do("v", lambda e, p=par, c=tt: e.tensor_scalar(
                        s_xn32.ap(), s_xt[p].ap(), s_m.ap()[:, c:c + 1], s_rstd.ap()[:, c:c + 1],
                        ALU.subtract, ALU.mult), inc="sv")
                    do("v", lambda e: e.tensor_scalar(
                        s_xn32b.ap(), s_xn32.ap(), s_sc.ap()[:, SC_RSX:SC_RSX + 1], MAGIC,
                        ALU.mult, ALU.add), inc="sv")
                    do("v", lambda e: e.tensor_scalar(
                        s_xnb.ap(), s_xn32b.ap(), MAGIC, s_sc.ap()[:, SC_SX:SC_SX + 1],
                        ALU.subtract, ALU.mult), inc="sv")
                free_ev[("xt", par)] = ("sv", n_of("sv"))
                free_ev[("xt", par, "s")] = ("ss", 0)
                wait("t", "sv", n_of("sv"))
                wait_free("t", "ptx")
                for dt in range(DT):
                    do("t", lambda e, d=dt: e.transpose(
                        psum_bf16(1)[:, d * 128:(d + 1) * 128],
                        s_xnb.ap()[:, d * 128:(d + 1) * 128], s_id.ap()),
                        inc=("st2" if dt == DT - 1 else None))
                set_free("xnb", "st2")
                wait("s", "st2", n_of("st2"))
                for dt in range(DT):
                    do("s", lambda e, d=dt, c=tt: e.activation(
                        bigv(d)[:, c * 128:(c + 1) * 128],
                        psum_bf16(1)[:, d * 128:(d + 1) * 128], AF.Copy), inc="ss",
                        dr=(dt == DT - 1))
                set_free("ptx", "ss")

        emit_xpass(quant=False)

        barrier()
        if dbg:
            do("g", lambda e: e.dma_start(
                out=big_dump, in_=s_big.ap().bitcast(U16)), inc="dmig", by=16)
            wait("g", "dmig", n_of("dmig"))
        s_lat = sb0("s_lat", [64, NT], BF16)
        s_latv = sb0("s_latv", [64, NT], BF16)
        s_qtb = sb0("s_qtb", [128, NT], BF16)
        p_q = [psum_f32(2), psum_f32(3)]
        p_l = psum_f32(4)[0:64, :]
        for (dst, wsb) in ((s_lat, wkd_sb), (s_latv, wvd_sb)):
            for tc in range(NTC):
                wait("t", "lev", n_of("lev"))
                for mt in range(DT):
                    do("t", lambda e, m=mt, c=tc, w=wsb: e.matmul(
                        p_l, w.ap()[:, m, :], bigv(m)[:, c * 512:(c + 1) * 512],
                        start=(m == 0), stop=(m == DT - 1)), inc=("lmm" if mt == DT - 1 else None))
                wait("s", "lmm", n_of("lmm"))
                do("s", lambda e, c=tc, d=dst: e.activation(
                    d.ap()[:, c * 512:(c + 1) * 512], p_l, AF.Copy), inc="lev")
        nk = [0]
        for h in range(KV):
            for tc in range(NTC):
                par = nk[0] % 2
                if nk[0] >= 2:
                    wait("t", "kev", nk[0] - 1)
                wait("t", "lev", NTC)
                do("t", lambda e, h_=h, c=tc, p=par: e.matmul(
                    p_q[p], wku_sb.ap()[h_ * 32:(h_ + 1) * 32, :],
                    s_lat.ap()[h_ * 32:(h_ + 1) * 32, c * 512:(c + 1) * 512],
                    start=True, stop=True), inc="kmm")
                wait("s", "kmm", n_of("kmm"))
                do("s", lambda e, h_=h, c=tc, p=par: e.activation(
                    s_kT[h_].ap()[:, c * 512:(c + 1) * 512], p_q[p], AF.Copy), inc="kev")
                nk[0] += 1
        nv = [0]
        wait("t", "kev", n_of("kev"))
        for h in range(KV):
            for tt in range(NTT):
                par = nv[0] % 2
                if nv[0] >= 2:
                    wait("t", "vev", nv[0] - 1)
                wait("t", "lev", 2 * NTC)
                do("t", lambda e, h_=h, c=tt, p=par: e.matmul(
                    p_q[p][:, 0:128], s_latv.ap()[h_ * 32:(h_ + 1) * 32, c * 128:(c + 1) * 128],
                    wvu_sb.ap()[h_ * 32:(h_ + 1) * 32, :], start=True, stop=True), inc="vmm")
                wait("s", "vmm", n_of("vmm"))
                do("s", lambda e, h_=h, c=tt, p=par: e.activation(
                    s_v[h_].ap()[:, c * 128:(c + 1) * 128], p_q[p][:, 0:128], AF.Copy), inc="vev")
                nv[0] += 1
        nq = [0]
        wait("t", "vev", n_of("vev"))
        for dt in range(DT):
            wait_free("s", ("qtb",))
            for tc in range(NTC):
                par = nq[0] % 2
                if nq[0] >= 2:
                    wait("t", "qev", nq[0] - 1)
                for mt in range(DT):
                    do("t", lambda e, m=mt, d=dt, c=tc, p=par: e.matmul(
                        p_q[p], wq_sb.ap()[:, m, d * 128:(d + 1) * 128],
                        bigv(m)[:, c * 512:(c + 1) * 512],
                        start=(m == 0), stop=(m == DT - 1)), inc=("qmm" if mt == DT - 1 else None))
                wait("s", "qmm", n_of("qmm"))
                do("s", lambda e, c=tc, p=par: e.activation(
                    s_qtb.ap()[:, c * 512:(c + 1) * 512], p_q[p], AF.Identity,
                    scale=SCALE_ATT), inc="qev")
                nq[0] += 1
            wait("g", "qev", n_of("qev"))
            do("g", lambda e, d=dt: e.dma_start(out=qt_sp[d], in_=s_qtb.ap()), inc="dsp", by=16)
            set_free(("qtb",), "dsp")

        barrier()
        emit_xpass(quant=True)

        barrier()
        ph0.close()
        phB = contextlib.ExitStack()
        sbB = lambda name, shape, dt=F32: phB.enter_context(nc.sbuf_tensor(name, shape, dt))
        psB = lambda name, shape, dt=F32: phB.enter_context(nc.psum_tensor(name, shape, dt))
        wl = [sbB(f"wl{i}", [128, 8, 128], BF16) for i in range(2)]
        stg = [sbB(f"stg{i}", [128, NT]) for i in range(2)]
        s_mxu = sbB("s_mxu", [128, 2 * IT * NTC])
        p_b = [psum_f32(i) for i in range(4)]
        wait("y", "cc", CC_AG)
        nmmB = [0]
        nspB = [0]
        for src_idx, (ag_h, sp_h) in enumerate(((ag_wu_out, upraw_sp), (ag_wg_out, gtraw_sp))):
            for it in range(IT):
                k = src_idx * IT + it
                wpar, spar = k % 2, k % 2
                rk, il = it // ITL, it % ITL
                dwlp = f"dwl{wpar}"
                wait_free("y", ("wl", wpar))
                do("y", lambda e, a=ag_h, r=rk, i=il, w=wpar: e.dma_start(
                    out=wl[w].ap(),
                    in_=a[r * 128:(r + 1) * 128, :].rearrange(
                        "p (mt q) -> p mt q", q=ITL * 128)[:, :, i * 128:(i + 1) * 128]),
                    inc=dwlp, by=16)
                for tc in range(NTC):
                    slot = nmmB[0] % 4
                    wait("t", dwlp, n_of(dwlp))
                    if nmmB[0] >= 4:
                        wait("t", "brd", nmmB[0] - 3)
                        wait("t", "bev", nmmB[0] - 3)
                    for mt in range(DT):
                        do("t", lambda e, m=mt, c=tc, w=wpar, sl=slot: e.matmul(
                            p_b[sl], wl[w].ap()[:, m, :], bigv(m)[:, c * 512:(c + 1) * 512],
                            start=(m == 0), stop=(m == DT - 1)), inc=("bmm" if mt == DT - 1 else None))
                    wait("v", "bmm", n_of("bmm"))
                    do("v", lambda e, sl=slot, mc=nmmB[0]: e.tensor_reduce(
                        s_mxu.ap()[:, mc:mc + 1], p_b[sl], AX.X, ALU.max,
                        apply_absolute_value=True), inc="brd")
                    wait("s", "brd", n_of("brd"))
                    if tc == 0:
                        wait_free("s", ("stg", spar))
                    do("s", lambda e, sl=slot, c=tc, sp=spar: e.activation(
                        stg[sp].ap()[:, c * 512:(c + 1) * 512], p_b[sl], AF.Copy), inc="bev")
                    nmmB[0] += 1
                set_free(("wl", wpar), "bmm")
                wait("g", "bev", n_of("bev"))
                do("g", lambda e, i=it, sp=spar, h=sp_h: e.dma_start(
                    out=h[i], in_=stg[sp].ap()), inc=f"dspb{spar}", by=16)
                set_free(("stg", spar), f"dspb{spar}")
                nspB[0] += 1
        wait("v", "brd", n_of("brd"))
        do("v", lambda e: e.tensor_reduce(
            s_red.ap()[:, 0:1], s_mxu.ap()[:, 0:IT * NTC], AX.X, ALU.max), inc="sv")
        do("v", lambda e: e.tensor_reduce(
            s_red.ap()[:, 1:2], s_mxu.ap()[:, IT * NTC:2 * IT * NTC], AX.X, ALU.max), inc="sv")
        local_max_to_slot(s_red.ap()[:, 0:1], 0)
        local_max_to_slot(s_red.ap()[:, 1:2], 1)
        all_reduce(1)
        emit_scale_cols(1, [(0, SC_SU, SC_RSU), (1, SC_SG, SC_RSG)])
        do("v", lambda e: e.tensor_scalar(
            s_sc.ap()[:, SC_SG16:SC_SG16 + 1], s_sc.ap()[:, SC_SG:SC_SG + 1], P2_16, None,
            ALU.mult), inc="sv")
        wait("g", "dspb0", n_of("dspb0"))
        wait("g", "dspb1", n_of("dspb1"))
        phB.close()
        phM = contextlib.ExitStack()
        sbM = lambda name, shape, dt=F32: phM.enter_context(nc.sbuf_tensor(name, shape, dt))
        uld = [bigf[:, 0:NT], bigf[:, NT:2 * NT]]
        g2t = bigf[:, 2 * NT:3 * NT]
        sigt = bigf[:, 3 * NT:4 * NT]
        phAT = contextlib.ExitStack()
        sbT = lambda name, shape, dt=F32: phAT.enter_context(nc.sbuf_tensor(name, shape, dt, side="right"))
        s_qhr = [sbT(f"s_qhr{i}", [128, NT], BF16) for i in range(2)]
        s_pb = [sbT(f"s_pb{i}", [128, 2048], BF16) for i in range(2)]
        s_pT = [sbT(f"s_pT{i}", [128, 1024], BF16) for i in range(2)]
        s_cmx = [sbT(f"s_cmx{i}", [128, 4]) for i in range(4)]
        s_csm = [sbT(f"s_csm{i}", [128, 4]) for i in range(4)]
        s_ngm = [sbT(f"s_ngm{i}", [128, 1]) for i in range(4)]
        s_rcp = [sbT(f"s_rcp{i}", [128, 1]) for i in range(4)]
        s_ystA = [sbT(f"s_ystA{i}", [128, 2048], BF16) for i in range(2)]
        p_s = [psum_f32(i) for i in range(4)]
        p_t2 = [psum_bf16(4), psum_bf16(5)]
        p_y2 = [psum_f32(6), psum_f32(7)]

        gchunk = [0]
        gq_ctr = [0]
        tr_grp = [0]
        pend = []
        trcnt_after = {}
        pv_after_grp = {}
        sc_after_hr = {}
        chunk_free = {}
        exp_after_tile = {}

        def emit_tp_pv(info):
            (h, r, b, qi, tile_id) = info
            ntl = qi + 1
            ngr = (ntl + 7) // 8
            for g in range(ngr):
                j0, j1 = g * 8, min(g * 8 + 7, qi)
                bank = tr_grp[0] % 2
                wait_free("t", ("pt2", bank))
                wait("t", "a_ex", exp_after_tile[tile_id])
                for j in range(j0, j1 + 1):
                    do("t", lambda e, j_=j, bk=bank, pb=(tile_id % 2): e.transpose(
                        p_t2[bk][:, (j_ % 8) * 128:((j_ % 8) + 1) * 128],
                        s_pb[pb].ap()[:, j_ * 128:(j_ + 1) * 128], s_id.ap()),
                        inc=("a_tr" if j == j1 else None))
                wait("s", "a_tr", n_of("a_tr"))
                wait_free("s", ("spT", bank))
                do("s", lambda e, bk=bank, n=(j1 - j0 + 1) * 128: e.activation(
                    s_pT[bk].ap()[:, 0:n], p_t2[bk][:, 0:n], AF.Copy), inc="a_pt")
                free_ev[("pt2", bank)] = ("a_pt", n_of("a_pt"))
                wait("t", "a_pt", n_of("a_pt"))
                if g == 0:
                    wait_free("t", ("py2", tile_id % 2))
                for j in range(j0, j1 + 1):
                    gtile = b * NQT + j
                    do("t", lambda e, h_=h, gt=gtile, bk=bank, jj=(j - j0), tb=(tile_id % 2), q_=qi, j_=j: e.matmul(
                        p_y2[tb][:, 0:128],
                        s_pT[bk].ap()[:, jj * 128:(jj + 1) * 128],
                        s_v[h_].ap()[:, gt * 128:(gt + 1) * 128],
                        start=(j_ == 0), stop=(j_ == q_)),
                        inc=("a_pv" if j == j1 else None))
                free_ev[("spT", bank)] = ("a_pv", n_of("a_pv"))
                tr_grp[0] += 1
            trcnt_after[tile_id] = n_of("a_tr")
            wait("s", "a_pv", n_of("a_pv"))
            wait("s", "a_rc", n_of("a_rc"))
            blkp = (tile_id // NQT) % 2
            lastq = (qi == NQT - 1)
            do("s", lambda e, tb=(tile_id % 2), p=(tile_id % 4), bp=blkp, q_=qi: e.activation(
                s_ystA[bp].ap()[:, q_ * 128:(q_ + 1) * 128], p_y2[tb][:, 0:128],
                AF.Identity, scale=s_rcp[p].ap()), inc="a_ar", dr=lastq)
            free_ev[("py2", tile_id % 2)] = ("a_ar", n_of("a_ar"))
            free_ev[("rcp", tile_id % 4)] = ("a_ar", n_of("a_ar"))

        def attn_stream():
            for h in range(KV):
                for r in range(REP):
                    hri = h * REP + r
                    qpar = hri % 2
                    dqtp = f"dqt{qpar}"
                    if hri >= 2:
                        wait("y", "a_sc", sc_after_hr[hri - 2])
                    do("y", lambda e, d=hri, p=qpar: e.dma_start(
                        out=s_qhr[p].ap(), in_=qt_sp[d]), inc=dqtp, by=16)
                    wait("t", dqtp, n_of(dqtp))
                    for b in range(BL):
                        blk = hri * BL + b
                        for qi in range(NQT):
                            tile_id = gq_ctr[0]
                            kl = (qi + 1) * 128
                            nch = (kl + 511) // 512
                            pbp = tile_id % 2
                            sp = tile_id % 4
                            base = gchunk[0]
                            for c in range(nch):
                                cn = min(512, kl - c * 512)
                                slot = gchunk[0] % 4
                                if gchunk[0] >= 4:
                                    wait("t", "a_ex", chunk_free[gchunk[0] - 4])
                                last = (c == nch - 1)
                                do("t", lambda e, h_=h, q_=qi, c_=c, n=cn, sl=slot, b_=b, p=qpar, lst=last: e.matmul(
                                    p_s[sl][:, 0:n],
                                    s_qhr[p].ap()[:, b_ * T + q_ * 128: b_ * T + (q_ + 1) * 128],
                                    s_kT[h_].ap()[:, b_ * T + c_ * 512: b_ * T + c_ * 512 + n],
                                    start=True, stop=(not lst)), inc=(None if last else "a_sc"))
                                if last:
                                    do("t", lambda e, sl=slot, o=cn - 128: e.matmul(
                                        p_s[sl][:, o:o + 128], s_id.ap(), s_cm.ap(),
                                        start=False, stop=True), inc="a_sc")
                                gchunk[0] += 1
                            for c in range(nch):
                                cn = min(512, kl - c * 512)
                                slot = (base + c) % 4
                                wait("v", "a_sc", base + c + 1)
                                if nch == 1:
                                    do("v", lambda e, sl=slot, n=cn, p=sp: e.tensor_reduce(
                                        s_ngm[p].ap(), p_s[sl][:, 0:n], AX.X,
                                        ALU.max, negate=True), inc="a_ngm")
                                else:
                                    do("v", lambda e, sl=slot, n=cn, c_=c, p=sp: e.tensor_reduce(
                                        s_cmx[p].ap()[:, c_:c_ + 1], p_s[sl][:, 0:n], AX.X,
                                        ALU.max), inc=None, dr=(c == nch - 1))
                            if nch > 1:
                                do("v", lambda e, p=sp, n=nch: e.tensor_reduce(
                                    s_ngm[p].ap(), s_cmx[p].ap()[:, 0:n], AX.X, ALU.max,
                                    negate=True), inc="a_ngm")
                            wait("s", "a_ngm", n_of("a_ngm"))
                            if tile_id >= 2:
                                wait("s", "a_tr", trcnt_after[tile_id - 2])
                            if tile_id >= 4:
                                wait("s", "a_rc", n_of("a_rc") - 3)
                            for c in range(nch):
                                cn = min(512, kl - c * 512)
                                slot = (base + c) % 4
                                do("s", lambda e, sl=slot, n=cn, c_=c, pb=pbp, p=sp: e.activation(
                                    s_pb[pb].ap()[:, c_ * 512:c_ * 512 + n], p_s[sl][:, 0:n],
                                    AF.Exp, bias=s_ngm[p].ap(),
                                    accum_out=s_csm[p].ap()[:, c_:c_ + 1]),
                                    inc=("a_ex" if c == nch - 1 else None), by=nch,
                                    dr=(c == nch - 1))
                            exp_after_tile[tile_id] = n_of("a_ex")
                            for c in range(nch):
                                chunk_free[base + c] = n_of("a_ex")
                            wait("v", "a_ex", n_of("a_ex"))
                            wait_free("v", ("rcp", sp))
                            if nch == 1:
                                do("v", lambda e, p=sp: e.reciprocal(
                                    s_rcp[p].ap(), s_csm[p].ap()[:, 0:1]), inc="a_rc")
                            else:
                                do("v", lambda e, p=sp, n=nch: e.tensor_reduce(
                                    s_rcp[p].ap(), s_csm[p].ap()[:, 0:n], AX.X, ALU.add),
                                    inc=None, dr=True)
                                do("v", lambda e, p=sp: e.reciprocal(
                                    s_rcp[p].ap(), s_rcp[p].ap()), inc="a_rc")
                            if pend:
                                emit_tp_pv(pend.pop())
                            pend.append((h, r, b, qi, tile_id))
                            gq_ctr[0] += 1
                            yield
                        if pend:
                            emit_tp_pv(pend.pop())
                        bp = blk % 2
                        wait("y", "a_ar", n_of("a_ar"))
                        do("y", lambda e, d=hri, b_=b, p=bp: e.dma_start(
                            out=yt_sp[d][:, b_ * T:(b_ + 1) * T], in_=s_ystA[p].ap()),
                            inc="dyt", by=16)
                        free_ev[("ystA", bp)] = ("dyt", n_of("dyt"))
                        yst_guard[bp] = n_of("dyt")
                        sc_after_hr[hri] = n_of("a_sc")
                    yield

        yst_guard = {}
        attn_gen = attn_stream()
        n_pumped = [0]

        def pump(n):
            for _ in range(n):
                if next(attn_gen, "END") == "END":
                    return

        _orig_emit = emit_tp_pv
        def emit_tp_pv(info):
            (h, r, b, qi, tile_id) = info
            if qi == 0:
                bp = (tile_id // NQT) % 2
                if bp in yst_guard:
                    wait("s", "dyt", yst_guard[bp])
            _orig_emit(info)

        s_pool = sbM("s_pool", [128, 8 * NT], BF16)
        poolf = s_pool.ap().bitcast(F32)
        pooli = s_pool.ap().bitcast(I16)
        poolu = s_pool.ap().bitcast(U16)
        gld = [poolf[:, 0:NT], poolf[:, NT:2 * NT]]
        s_qg = [pooli[:, 4 * NT:5 * NT], pooli[:, 5 * NT:6 * NT]]
        s_qsig = [poolu[:, 6 * NT:7 * NT], poolu[:, 7 * NT:8 * NT]]
        g2tb = [bigf[:, 2 * NT:3 * NT], bigf[:, 0:NT]]
        sigtb = [bigf[:, 3 * NT:4 * NT], bigf[:, NT:2 * NT]]
        s_mxc = sbM("s_mxc", [128, IT])
        sv_after_c = {}
        ss_after_c = {}
        dsr_after_c = {}
        wait("v", "bmm", n_of("bmm"))
        wait("y", "dspb0", n_of("dspb0"))
        wait("y", "dspb1", n_of("dspb1"))
        do("y", lambda e: e.dma_start(out=gld[0], in_=gtraw_sp[0]), inc="dsr0", by=16)
        dsr_after_c[0] = n_of("dsr0")
        do("s", lambda e: e.preload_act_table(AF.Exp))
        for k in range(IT + 2):
            par = k % 2
            if k >= 2:
                kk = k - 2
                p2 = kk % 2
                wait("v", "ss", ss_after_c[kk])
                do("v", lambda e, p=p2: e.tensor_scalar(
                    g2tb[p], sigtb[p], 65536.0, MAGIC, ALU.mult, ALU.add), inc="sv")
                wait_free("v", ("qsig", p2))
                do("v", lambda e, p=p2: e.tensor_scalar(
                    s_qsig[p], g2tb[p], MAGIC, 65535.0, ALU.subtract, ALU.min), inc="sv")
                do("v", lambda e, p=p2: e.scalar_tensor_tensor(
                    g2tb[p], s_qg[p], s_sc.ap()[:, SC_SG16:SC_SG16 + 1], s_qsig[p],
                    ALU.mult, ALU.mult), inc="sv")
                do("v", lambda e, i=kk, p=p2: e.tensor_reduce(
                    s_mxc.ap()[:, i:i + 1], g2tb[p], AX.X, ALU.max,
                    apply_absolute_value=True), inc="sv")
            if k < IT:
                wait("v", f"dsr{par}", dsr_after_c[k])
                do("v", lambda e, p=par: e.tensor_scalar(
                    g2tb[p], gld[p], s_sc.ap()[:, SC_RSG:SC_RSG + 1], MAGIC,
                    ALU.mult, ALU.add), inc="sv")
                free_ev[("gld", par)] = ("sv", n_of("sv"))
                sv_after_c[k] = n_of("sv")
            if 1 <= k <= IT:
                km = k - 1
                pm = km % 2
                wait("s", "sv", sv_after_c[km])
                wait_free("s", ("qg", pm))
                do("s", lambda e: e.preload_act_table(AF.Sigmoid))
                do("s", lambda e, p=pm: e.activation(
                    s_qg[p], g2tb[p], AF.Identity, bias=s_negM.ap()), inc="ss")
                do("s", lambda e, p=pm: e.activation(
                    sigtb[p], s_qg[p], AF.Sigmoid,
                    scale=s_sc.ap()[:, SC_SG:SC_SG + 1]), inc="ss")
                do("s", lambda e: e.preload_act_table(AF.Exp))
                ss_after_c[km] = n_of("ss")
            if k + 1 < IT:
                pn = (k + 1) % 2
                wait_free("y", ("gld", pn))
                do("y", lambda e, i=k + 1, p=pn: e.dma_start(
                    out=gld[p], in_=gtraw_sp[i]), inc=f"dsr{pn}", by=16)
                dsr_after_c[k + 1] = n_of(f"dsr{pn}")
            if k >= 2:
                kk = k - 2
                p2 = kk % 2
                wait("g", "ss", ss_after_c[kk])
                wait("g", "sv", n_of("sv"))
                do("g", lambda e, i=kk, p=p2: e.dma_start(
                    out=qg_sp[i], in_=s_qg[p]), inc=f"dspc{p2}", by=16)
                do("g", lambda e, i=kk, p=p2: e.dma_start(
                    out=qsig_sp[i], in_=s_qsig[p]), inc=f"dspc{p2}", by=16)
                free_ev[("qg", p2)] = (f"dspc{p2}", n_of(f"dspc{p2}"))
                free_ev[("qsig", p2)] = (f"dspc{p2}", n_of(f"dspc{p2}"))
            pump(4)
        do("v", lambda e: e.tensor_reduce(
            s_red.ap()[:, 0:1], s_mxc.ap(), AX.X, ALU.max), inc="sv")
        local_max_to_slot(s_red.ap()[:, 0:1], 0)
        all_reduce(2)
        pump(10)
        emit_scale_cols(2, [(0, SC_SG2, SC_RSG2)])

        HIT = 2 * IT
        NTH = NT // 2
        gqh = [pooli[:, 0:NTH], pooli[:, NTH:2 * NTH]]
        gsh = [poolu[:, 2 * NTH:3 * NTH], poolu[:, 3 * NTH:4 * NTH]]
        qg2h = [pooli[:, 4 * NTH:5 * NTH], pooli[:, 5 * NTH:6 * NTH]]
        quph = [pooli[:, 6 * NTH:7 * NTH], pooli[:, 7 * NTH:8 * NTH]]
        g2tD = [poolf[:, 8 * (NTH // 2):10 * (NTH // 2)],
                poolf[:, 10 * (NTH // 2):12 * (NTH // 2)]]
        sigtD = [poolf[:, 12 * (NTH // 2):14 * (NTH // 2)],
                 poolf[:, 14 * (NTH // 2):16 * (NTH // 2)]]
        uldh = [bigf[:, 0:NTH], bigf[:, NTH:2 * NTH]]
        s_mxd = sbM("s_mxd", [128, HIT])
        sv_after_d = {}
        ss_after_d = {}
        dsr_after_d = {}

        def d_load(e, k, p, sem):
            i, hh = k // 2, k % 2
            e.dma_start(out=gqh[p], in_=qg_sp[i][:, hh * NTH:(hh + 1) * NTH]).then_inc(sem, 16)
            e.dma_start(out=gsh[p], in_=qsig_sp[i][:, hh * NTH:(hh + 1) * NTH]).then_inc(sem, 16)
            e.dma_start(out=uldh[p], in_=upraw_sp[i][:, hh * NTH:(hh + 1) * NTH]).then_inc(sem, 16)
        wait("y", "dspc0", n_of("dspc0"))
        wait("y", "dspc1", n_of("dspc1"))
        wait("y", "sv", n_of("sv"))
        wait("y", "ss", n_of("ss"))
        wait("v", "dspc0", n_of("dspc0"))
        wait("v", "dspc1", n_of("dspc1"))
        wait("s", "dspc0", n_of("dspc0"))
        wait("s", "dspc1", n_of("dspc1"))
        cnt["dsr0"] = cnt.get("dsr0", 0) + 48
        prog["y"].append(lambda e: d_load(e, 0, 0, sems["dsr0"]))
        dsr_after_d[0] = n_of("dsr0")
        for k in range(HIT + 2):
            par = k % 2
            if k >= 2:
                kk = k - 2
                p2 = kk % 2
                wait("v", "ss", ss_after_d[kk])
                do("v", lambda e, p=p2: e.scalar_tensor_tensor(
                    g2tD[p], qg2h[p], s_sc.ap()[:, SC_SG2:SC_SG2 + 1], sigtD[p],
                    ALU.mult, ALU.mult), inc="sv")
                do("v", lambda e, i=kk, p=p2: e.tensor_reduce(
                    s_mxd.ap()[:, i:i + 1], g2tD[p], AX.X, ALU.max,
                    apply_absolute_value=True), inc="sv")
            if k < HIT:
                wait("v", f"dsr{par}", dsr_after_d[k])
                do("v", lambda e, p=par: e.scalar_tensor_tensor(
                    g2tD[p], gqh[p], s_sc.ap()[:, SC_SG16:SC_SG16 + 1], gsh[p],
                    ALU.mult, ALU.mult), inc="sv")
                free_ev[("gqh", par)] = ("sv", n_of("sv"))
                do("v", lambda e, p=par: e.tensor_scalar(
                    sigtD[p], g2tD[p], s_sc.ap()[:, SC_RSG2:SC_RSG2 + 1], MAGIC,
                    ALU.mult, ALU.add), inc="sv")
                do("v", lambda e, p=par: e.tensor_scalar(
                    g2tD[p], uldh[p], s_sc.ap()[:, SC_RSU:SC_RSU + 1], MAGIC,
                    ALU.mult, ALU.add), inc="sv")
                free_ev[("uldh", par)] = ("sv", n_of("sv"))
                sv_after_d[k] = n_of("sv")
            if 1 <= k <= HIT:
                km = k - 1
                pm = km % 2
                wait("s", "sv", sv_after_d[km])
                wait_free("s", ("qg2h", pm))
                do("s", lambda e, p=pm: e.activation(
                    qg2h[p], sigtD[p], AF.Identity, bias=s_negM.ap()), inc="ss")
                wait_free("s", ("quph", pm))
                do("s", lambda e, p=pm: e.activation(
                    quph[p], g2tD[p], AF.Identity, bias=s_negM.ap()), inc="ss")
                do("s", lambda e, p=pm: e.activation(
                    sigtD[p], quph[p], AF.Identity,
                    scale=s_sc.ap()[:, SC_SU:SC_SU + 1]), inc="ss")
                ss_after_d[km] = n_of("ss")
            if k + 1 < HIT:
                pn = (k + 1) % 2
                wait_free("y", ("gqh", pn))
                wait_free("y", ("uldh", pn))
                cnt[f"dsr{pn}"] = cnt.get(f"dsr{pn}", 0) + 48
                prog["y"].append(lambda e, kn=k + 1, p=pn, sm=sems[f"dsr{pn}"]: d_load(e, kn, p, sm))
                dsr_after_d[k + 1] = n_of(f"dsr{pn}")
            if k >= 2:
                kk = k - 2
                p2 = kk % 2
                it, hh = kk // 2, kk % 2
                wait("g", "ss", ss_after_d[kk])
                wait("g", "sv", n_of("sv"))
                do("g", lambda e, i=it, h_=hh, p=p2: e.dma_start(
                    out=qg2_sp[h_ * 4:(h_ + 1) * 4, i, :, :].rearrange("tc p c -> p tc c"),
                    in_=qg2h[p].rearrange("p (tc c) -> p tc c", c=512)), inc=f"dspd{p2}", by=16)
                do("g", lambda e, i=it, h_=hh, p=p2: e.dma_start(
                    out=qup_sp[h_ * 4:(h_ + 1) * 4, i, :, :].rearrange("tc p c -> p tc c"),
                    in_=quph[p].rearrange("p (tc c) -> p tc c", c=512)), inc=f"dspd{p2}", by=16)
                free_ev[("qg2h", p2)] = (f"dspd{p2}", n_of(f"dspd{p2}"))
                free_ev[("quph", p2)] = (f"dspd{p2}", n_of(f"dspd{p2}"))
            pump(2)
        do("v", lambda e: e.tensor_reduce(
            s_red.ap()[:, 0:1], s_mxd.ap(), AX.X, ALU.max), inc="sv")
        local_max_to_slot(s_red.ap()[:, 0:1], 0)
        all_reduce(3)
        pump(10)
        emit_scale_cols(3, [(0, SC_SO, SC_RSO)])

        wdT = s_pool.ap()[:, 0:IT * 1024].rearrange("p (i c) -> p i c", c=1024)
        s_oTx = s_pool.ap()[:, IT * 1024:IT * 1024 + NT]
        _spare = 14336
        s_up5 = [poolf[:, _spare:_spare + 512], poolf[:, _spare + 512:_spare + 1024]]
        s_or5 = poolf[:, _spare + 1024:_spare + 1536]
        s_or5b = poolf[:, _spare + 1536:_spare + 2048]
        s_fout = [None, None]
        p_e = [psum_f32(0), psum_f32(1)]
        s_up2 = [poolf[:, _spare:_spare + 1024], poolf[:, _spare + 1024:_spare + 2048]]

        def qg22_ap(c):
            return bigi[:, c * 1024:(c + 1) * 1024]

        def qup2_ap(c):
            return bigi[:, 24 * 512 + c * 1024:24 * 512 + (c + 1) * 1024]

        def oT_ap(par, it):
            if par == 0:
                if it < 16:
                    return s_big.ap()[:, (6 * NT) + it * 512:(6 * NT) + (it + 1) * 512]
                return s_oTx[:, (it - 16) * 512:(it - 15) * 512]
            if it < 8:
                return s_kT[0].ap()[:, it * 512:(it + 1) * 512]
            if it < 16:
                return s_kT[1].ap()[:, (it - 8) * 512:(it - 7) * 512]
            return s_v[0].ap()[:, (it - 16) * 512:(it - 15) * 512]

        def oT2_ap(par, c):
            if par == 0:
                if c < 8:
                    return s_big.ap()[:, (6 * NT) + c * 1024:(6 * NT) + (c + 1) * 1024]
                return s_oTx[:, (c - 8) * 1024:(c - 7) * 1024]
            if c < 4:
                return s_kT[0].ap()[:, c * 1024:(c + 1) * 1024]
            if c < 8:
                return s_kT[1].ap()[:, (c - 4) * 1024:(c - 3) * 1024]
            return s_v[0].ap()[:, (c - 8) * 1024:(c - 7) * 1024]

        wait("y", "dspd0", n_of("dspd0"))
        wait("y", "dspd1", n_of("dspd1"))
        wait("y", "sv", n_of("sv"))
        pump(10000)
        barrier()
        phAT.close()
        s_fout[0] = sbM("s_fout0", [128, 1024])
        s_fout[1] = sbM("s_fout1", [128, 1024])
        wo_sb = sbM("wo_sb", [128, 8, 1024], BF16)
        s_yld = [sbM(f"s_yld{i}", [128, 8, 128], BF16) for i in range(2)]
        s_ylhs = [sbM(f"s_ylhs{i}", [128, 8, 128], BF16) for i in range(2)]
        s_xw = [sbM(f"s_xw{i}", [128, 1024]) for i in range(2)]
        s_or2 = s_v[1].ap().bitcast(F32)[:, 0:1024]
        s_ow_ap = s_v[1].ap().bitcast(F32)[:, 1024:2048]
        p_tr = [psum_bf16(2), psum_bf16(3)]
        p_w = [psum_f32(4), psum_f32(5)]
        for ch in range(8):
            wait_free("y", ("xw", ch % 2))
            do("y", lambda e, c=ch, p=(ch % 2): e.dma_start(
                out=s_xw[p].ap(),
                in_=wo_h.rearrange("(mt k) d -> k mt d", k=128)[:, c, :]),
                inc=f"dx{ch % 2}", by=16)
            wait("v", f"dx{ch % 2}", n_of(f"dx{ch % 2}"))
            do("v", lambda e, c=ch, p=(ch % 2): e.tensor_copy(
                wo_sb.ap()[:, c, :], s_xw[p].ap()), inc="sv")
            free_ev[("xw", ch % 2)] = ("sv", n_of("sv"))
        WOC = [n_of("sv")]
        wait("y", "dyt", n_of("dyt"))
        for it in range(IT):
            do("y", lambda e, i=it: e.dma_start(
                out=wdT[:, i, :].rearrange("p (r h) -> p r h", r=NCORES),
                in_=ag_wd_out.rearrange("(r p) (it h) -> p r it h", p=128, h=128)[:, :, i, :]),
                inc="dw", by=16)

        dse_after = {}
        sv_after_deq = {}
        sv_after_chunk = {}
        gch = [0]

        def emit_loads(tc):
            wait_free("y", "ebig")
            wait_free("y", "ebig_s")
            for grp in range(3):
                do("y", lambda e, t_=tc, g=grp: e.dma_start(
                    out=bigi[:, g * 8 * 512:(g + 1) * 8 * 512].rearrange("p (i c) -> p i c", c=512),
                    in_=qg2_sp[t_, g * 8:(g + 1) * 8, :, :].rearrange("i p c -> p i c")),
                    inc="dse", by=16)
                do("y", lambda e, t_=tc, g=grp: e.dma_start(
                    out=bigi[:, (24 + g * 8) * 512:(24 + (g + 1) * 8) * 512].rearrange(
                        "p (i c) -> p i c", c=512),
                    in_=qup_sp[t_, g * 8:(g + 1) * 8, :, :].rearrange("i p c -> p i c")),
                    inc="dse", by=16)
            dse_after[tc] = n_of("dse")

        def emit_deq(tc):
            par = tc % 2
            wait_free("v", ("eoT", par))
            wait("v", "dse", dse_after[tc])
            wait("s", "dse", dse_after[tc])
            for c in range(12):
                g = gch[0]
                cp = g % 2
                if g >= 2:
                    wait("s", "sv", sv_after_chunk[g - 2])
                do("s", lambda e, c_=c, p=cp: e.activation(
                    s_up2[p], qup2_ap(c_), AF.Identity,
                    scale=s_sc.ap()[:, SC_SU:SC_SU + 1]), inc="ss")
                wait("v", "ss", n_of("ss"))
                do("v", lambda e, c_=c, p=cp: e.scalar_tensor_tensor(
                    s_or2, qg22_ap(c_), s_sc.ap()[:, SC_SG2:SC_SG2 + 1], s_up2[p],
                    ALU.mult, ALU.mult), inc="sv")
                do("v", lambda e, p=cp: e.tensor_scalar(
                    s_up2[p], s_or2, s_sc.ap()[:, SC_RSO:SC_RSO + 1], MAGIC,
                    ALU.mult, ALU.add), inc="sv")
                do("v", lambda e, c_=c, p=cp, pr=par: e.tensor_scalar(
                    oT2_ap(pr, c_), s_up2[p], MAGIC, s_sc.ap()[:, SC_SO:SC_SO + 1],
                    ALU.subtract, ALU.mult), inc="sv")
                sv_after_chunk[g] = n_of("sv")
                gch[0] += 1
            free_ev["ebig"] = ("sv", n_of("sv"))
            free_ev["ebig_s"] = ("ss", n_of("ss"))
            sv_after_deq[tc] = n_of("sv")

        emit_loads(0)
        emit_deq(0)
        for tc in range(NTC):
            if tc + 1 < NTC:
                emit_loads(tc + 1)
                emit_deq(tc + 1)
            wait("t", "sv", sv_after_deq[tc])
            wait("t", "dw", n_of("dw"))
            ng = [0]
            for ttl in range(4):
                gtt = tc * 4 + ttl
                b, tr = gtt // (NTT // BL), (gtt % (NTT // BL)) * 128
                wpar = gtt % 2
                wait_free("y", ("xw", wpar))
                do("y", lambda e, b_=b, t_=tr, p=wpar: e.dma_start(
                    out=s_xw[p].ap(), in_=x_h[b_, t_:t_ + 128, :]), inc=f"dx{wpar}", by=16)
                wait_free("y", ("yld", wpar))
                do("y", lambda e, c=gtt, p=wpar: e.dma_start(
                    out=s_yld[p].ap(),
                    in_=yt_sp.rearrange("d p n -> p d n")[:, :, c * 128:(c + 1) * 128]),
                    inc=f"dwy{wpar}", by=16)
                for mc in range(2):
                    slot = ng[0] % 2
                    if n_of("emm") >= 2:
                        wait("t", "eev", n_of("emm") - 1)
                    for it in range(IT):
                        do("t", lambda e, i=it, tl=ttl, m=mc, sl=slot, pr=(tc % 2): e.matmul(
                            p_e[sl], oT_ap(pr, i)[:, tl * 128:(tl + 1) * 128],
                            wdT[:, i, m * 512:(m + 1) * 512],
                            start=(i == 0), stop=(i == IT - 1)),
                            inc=("emm" if it == IT - 1 else None))
                    wait("s", "emm", n_of("emm"))
                    if mc == 0:
                        wait_free("s", ("fout", ttl % 2))
                    do("s", lambda e, sl=slot, m=mc, tl=ttl: e.activation(
                        s_fout[tl % 2].ap()[:, m * 512:(m + 1) * 512], p_e[sl],
                        AF.Copy), inc="eev")
                    ng[0] += 1
                wait("t", f"dwy{wpar}", n_of(f"dwy{wpar}"))
                wait_free("t", ("ptr", wpar))
                for dt in range(DT):
                    do("t", lambda e, d=dt, p=wpar: e.transpose(
                        p_tr[p][:, d * 128:(d + 1) * 128], s_yld[p].ap()[:, d, :],
                        s_id.ap()), inc=("wtr" if dt == DT - 1 else None))
                free_ev[("yld", wpar)] = ("wtr", n_of("wtr"))
                wait("s", "wtr", n_of("wtr"))
                wait_free("s", ("ylhs", wpar))
                do("s", lambda e, p=wpar: e.activation(
                    s_ylhs[p].ap().rearrange("p a b -> p (a b)"), p_tr[p][:, 0:1024],
                    AF.Copy), inc="wev")
                free_ev[("ptr", wpar)] = ("wev", n_of("wev"))
                wait("t", "sv", WOC[0])
                wait("t", "wev", n_of("wev"))
                for mc in range(2):
                    if gtt > 0 or mc > 0:
                        wait("t", "wadd", max(0, 2 * gtt + mc - 1))
                    for dt in range(DT):
                        do("t", lambda e, d=dt, m=mc, p=wpar: e.matmul(
                            p_w[m], s_ylhs[p].ap()[:, d, :],
                            wo_sb.ap()[:, d, m * 512:(m + 1) * 512],
                            start=(d == 0), stop=(d == DT - 1)),
                            inc=("wmm" if dt == DT - 1 else None))
                free_ev[("ylhs", wpar)] = ("wmm", n_of("wmm"))
                wait("v", "wmm", n_of("wmm"))
                wait("v", f"dx{wpar}", n_of(f"dx{wpar}"))
                wait("v", "eev", n_of("eev"))
                for mc in range(2):
                    do("v", lambda e, m=mc, p=wpar: e.tensor_tensor(
                        s_ow_ap[:, m * 512:(m + 1) * 512], p_w[m],
                        s_xw[p].ap()[:, m * 512:(m + 1) * 512], ALU.add), inc=None)
                    do("v", lambda e, m=mc, p=wpar, tl=ttl: e.tensor_tensor(
                        s_xw[p].ap()[:, m * 512:(m + 1) * 512],
                        s_ow_ap[:, m * 512:(m + 1) * 512],
                        s_fout[tl % 2].ap()[:, m * 512:(m + 1) * 512], ALU.add), inc="wadd")
                free_ev[("fout", ttl % 2)] = ("wadd", n_of("wadd"))
                wait("g", "wadd", n_of("wadd"))
                do("g", lambda e, b_=b, t_=tr, p=wpar: e.dma_start(
                    out=out_h[b_, t_:t_ + 128, :], in_=s_xw[p].ap()), inc=f"do{wpar}", by=16)
                free_ev[("xw", wpar)] = (f"do{wpar}", n_of(f"do{wpar}"))
            free_ev[("eoT", tc % 2)] = ("emm", n_of("emm"))
        wait("g", "do0", n_of("do0"))
        wait("g", "do1", n_of("do1"))
        wait("y", "do0", n_of("do0"))
        wait("y", "do1", n_of("do1"))
        phM.close()

        block = top.enter_context(nc.Block())
        @block.sync
        def _(eng):
            for f in prog["y"]:
                f(eng)
        @block.vector
        def _(eng):
            for f in prog["v"]:
                f(eng)
        @block.scalar
        def _(eng):
            for f in prog["s"]:
                f(eng)
        @block.tensor
        def _(eng):
            for f in prog["t"]:
                f(eng)
        @block.gpsimd
        def _(eng):
            for f in prog["g"]:
                f(eng)

    return nc


_NC = None


def kernel(**inputs):
    global _NC
    x = np.ascontiguousarray(inputs["L0_attn_x"])
    wq = np.ascontiguousarray(inputs["L0_attn_wq"])
    wkd = np.ascontiguousarray(inputs["L0_attn_wk_down"])
    wvd = np.ascontiguousarray(inputs["L0_attn_wv_down"])
    wku = np.ascontiguousarray(inputs["L0_attn_wk_up"])
    wvu = np.ascontiguousarray(inputs["L0_attn_wv_up"])
    wo = np.ascontiguousarray(inputs["L0_attn_wo"])
    wgate = np.ascontiguousarray(inputs["L0_ffn_w_gate"])
    wup = np.ascontiguousarray(inputs["L0_ffn_w_up"])
    wdown = np.ascontiguousarray(inputs["L0_ffn_w_down"])

    ident = np.eye(128, dtype=ml_dtypes.bfloat16)
    cmask = np.where(np.tril(np.ones((128, 128), bool)), 0.0, -1e38).astype(ml_dtypes.bfloat16)

    if _NC is None:
        _NC = build_nc()
    nc = _NC

    in_maps = []
    for c in range(NCORES):
        in_maps.append(dict(
            x=x[c * BL:(c + 1) * BL],
            wq=wq, wk_down=wkd, wv_down=wvd, wk_up=wku, wv_up=wvu, wo=wo,
            wup_shard=wup[c * ITL * 128:(c + 1) * ITL * 128],
            wgate_shard=wgate[c * ITL * 128:(c + 1) * ITL * 128],
            wdown_shard=wdown[c * 128:(c + 1) * 128],
            ident=ident, cmask=cmask,
        ))
    res = run_bass_kernel_spmd(nc, in_maps, core_ids=list(range(NCORES)))
    out = np.concatenate([r["out"] for r in res.results], axis=0)
    return out


# version 14 (final)
# speedup vs baseline: 1.0568x; 1.0083x over previous
import numpy as np
import ml_dtypes
import contextlib
import concourse.bass as bass
import concourse.mybir as mybir
from concourse.bass_utils import run_bass_kernel_spmd

F32 = mybir.dt.float32
BF16 = mybir.dt.bfloat16
I16 = mybir.dt.int16
U16 = mybir.dt.uint16
AF = mybir.ActivationFunctionType
ALU = mybir.AluOpType
AX = mybir.AxisListType

NCORES = 8
BL = 2
T = 2048
NT = BL * T
D = 1024
DT = D // 128
I3 = 3072
IT = I3 // 128
ITL = IT // NCORES
KV, REP, HD, LAT = 2, 4, 128, 32
NTT = NT // 128
NTC = NT // 512
NQT = T // 128
MAGIC = 12582912.0
INV_QMAX = float(np.float32(1.0) / np.float32(32767.0))
INV7 = float(np.float32(1.0) / np.float32(7.0))
EPS = 1e-5
FMAX = 3.402823466e38
P2_16 = float(np.float32(2.0) ** -16)
SCALE_ATT = float(np.float32(HD) ** np.float32(-0.5))

SC_SX, SC_RSX, SC_SU, SC_RSU, SC_SG, SC_RSG, SC_SG16, SC_SG2, SC_RSG2, SC_SO, SC_RSO = range(11)

SEM_NAMES = [
    "ph", "dx0", "dx1", "dw", "dqt0", "dqt1", "dsp", "dse", "dqtr",
    "dwl0", "dwl1", "dspb0", "dspb1", "dsr0", "dsr1",
    "dspc0", "dspc1", "dspd0", "dspd1", "dspe0", "dspe1",
    "dmig", "dmiy", "do0", "do1", "cc", "dwi",
    "sv", "ss", "st2", "gv",
    "lmm", "lev", "kmm", "kev", "vmm", "vev", "qmm", "qev",
    "bmm", "brd", "bev",
    "a_sc", "a_ngm", "a_ex", "a_nm", "a_tr", "a_pt", "a_pv", "a_ye", "a_rc", "a_yn",
    "a_ar", "dyt", "dwy0", "dwy1",
    "emm", "eev", "wmm", "wadd", "wtr", "wev",
]


def build_nc(dbg=False):
    nc = bass.Bass("TRN2", target_bir_lowering=False, detect_race_conditions=False)
    DKIND = "ExternalOutput" if dbg else "Internal"

    x_h = nc.dram_tensor("x", [BL, T, D], F32, kind="ExternalInput").ap()
    wq_h = nc.dram_tensor("wq", [D, D], F32, kind="ExternalInput").ap()
    wkd_h = nc.dram_tensor("wk_down", [D, KV * LAT], F32, kind="ExternalInput").ap()
    wvd_h = nc.dram_tensor("wv_down", [D, KV * LAT], F32, kind="ExternalInput").ap()
    wku_h = nc.dram_tensor("wk_up", [LAT, HD], F32, kind="ExternalInput").ap()
    wvu_h = nc.dram_tensor("wv_up", [LAT, HD], F32, kind="ExternalInput").ap()
    wo_h = nc.dram_tensor("wo", [D, D], F32, kind="ExternalInput").ap()
    wup_sh_h = nc.dram_tensor("wup_shard", [ITL * 128, D], F32, kind="ExternalInput").ap()
    wg_sh_h = nc.dram_tensor("wgate_shard", [ITL * 128, D], F32, kind="ExternalInput").ap()
    wd_sh_h = nc.dram_tensor("wdown_shard", [128, I3], F32, kind="ExternalInput").ap()
    ident_h = nc.dram_tensor("ident", [128, 128], BF16, kind="ExternalInput").ap()
    cmask_h = nc.dram_tensor("cmask", [128, 128], BF16, kind="ExternalInput").ap()
    out_h = nc.dram_tensor("out", [BL, T, D], F32, kind="ExternalOutput").ap()

    qt_sp = nc.dram_tensor("qt_sp", [DT, 128, NT], BF16, kind=DKIND).ap()
    yt_sp = nc.dram_tensor("yt_sp", [DT, 128, NT], BF16).ap()
    upraw_sp = nc.dram_tensor("upraw_sp", [IT, 128, NT], F32, kind=DKIND).ap()
    gtraw_sp = nc.dram_tensor("gtraw_sp", [IT, 128, NT], F32, kind=DKIND).ap()
    qg_sp = nc.dram_tensor("qg_sp", [IT, 128, NT], I16, kind=DKIND).ap()
    qsig_sp = nc.dram_tensor("qsig_sp", [IT, 128, NT], U16, kind=DKIND).ap()
    qg2_sp = nc.dram_tensor("qg2_sp", [NTC, IT, 128, 512], I16, kind=DKIND).ap()
    qup_sp = nc.dram_tensor("qup_sp", [NTC, IT, 128, 512], I16, kind=DKIND).ap()
    ffn_sp = nc.dram_tensor("ffn_sp", [BL, T, D], F32, kind=DKIND).ap()
    flip_hb = nc.dram_tensor("flip_hb", [4, 128], F32).ap()
    rflip_h = [nc.dram_tensor(f"rflip{i}", [16, 128], F32).ap() for i in range(2)]
    cc_in = [nc.dram_tensor(f"cc_in{i}", [1, 16], F32).ap() for i in range(4)]
    cc_out = [nc.dram_tensor(f"cc_out{i}", [1, 16], F32).ap() for i in range(4)]
    sc_dump = nc.dram_tensor("sc_dump", [128, 16], F32, kind=DKIND).ap() if dbg else None
    ln_dump = nc.dram_tensor("ln_dump", [128, 8 * NTT], F32, kind=DKIND).ap() if dbg else None
    big_dump = nc.dram_tensor("big_dump", [128, DT * NT], U16, kind=DKIND).ap() if dbg else None
    ag_wu_in = nc.dram_tensor("ag_wu_in", [128, I3], BF16).ap()
    ag_wg_in = nc.dram_tensor("ag_wg_in", [128, I3], BF16).ap()
    ag_wd_in = nc.dram_tensor("ag_wd_in", [128, I3], BF16).ap()
    ag_wu_out = nc.dram_tensor("ag_wu_out", [NCORES * 128, I3], BF16, addr_space="Shared").ap()
    ag_wg_out = nc.dram_tensor("ag_wg_out", [NCORES * 128, I3], BF16, addr_space="Shared").ap()
    ag_wd_out = nc.dram_tensor("ag_wd_out", [NCORES * 128, I3], BF16, addr_space="Shared").ap()

    prog = {k: [] for k in ("y", "v", "s", "t", "g")}
    cnt = {}
    sems = {}
    free_ev = {}

    def n_of(sem):
        return cnt.get(sem, 0)

    def do(eng, fn, inc=None, by=1, dr=None):
        if dr is None:
            dr = eng in ("v", "s")
        if eng in ("v", "s"):
            prog[eng].append(fn)
            if dr:
                prog[eng].append(lambda e: e.drain())
            if inc is not None:
                cnt[inc] = cnt.get(inc, 0) + by
                if dr:
                    prog[eng].append(lambda e, s=inc, b=by: e.nop().then_inc(sems[s], b))
                else:
                    prog[eng].pop()
                    prog[eng].append(lambda e, f=fn, s=inc, b=by: f(e).then_inc(sems[s], b))
        else:
            if inc is None:
                prog[eng].append(fn)
            else:
                cnt[inc] = cnt.get(inc, 0) + by
                prog[eng].append(lambda e, f=fn, s=inc, b=by: f(e).then_inc(sems[s], b))

    def wait(eng, sem, val=None):
        v = cnt.get(sem, 0) if val is None else val
        if v <= 0:
            return
        prog[eng].append(lambda e, s=sem, vv=v: e.wait_ge(sems[s], vv))

    def set_free(key, sem):
        free_ev[key] = (sem, n_of(sem))

    def wait_free(eng, key):
        if key in free_ev:
            s, v = free_ev[key]
            wait(eng, s, v)

    nbar = [0]
    def barrier():
        nbar[0] += 1
        for e in ("y", "v", "s", "t", "g"):
            cnt["ph"] = cnt.get("ph", 0) + 1
            if e in ("v", "s"):
                prog[e].append(lambda en: en.drain())
            prog[e].append(lambda en: en.nop().then_inc(sems["ph"], 1))
        tgt = 5 * nbar[0]
        for e in ("y", "v", "s", "t", "g"):
            prog[e].append(lambda en, t_=tgt: en.wait_ge(sems["ph"], t_))

    with contextlib.ExitStack() as top:
        sb = lambda name, shape, dt=F32: top.enter_context(nc.sbuf_tensor(name, shape, dt))
        for s in SEM_NAMES:
            sems[s] = top.enter_context(nc.semaphore(s))

        PSUM = [top.enter_context(nc.psum_tensor(f"P{i}", [128, 512], F32)) for i in range(8)]
        psum_f32 = lambda i: PSUM[i].ap()
        psum_bf16 = lambda i: PSUM[i].ap().bitcast(BF16)
        s_big = sb("s_big", [128, DT * NT], BF16)
        bigv = lambda d: s_big.ap()[:, d * NT:(d + 1) * NT]
        bigf = s_big.ap().bitcast(F32)
        bigi = s_big.ap().bitcast(I16)
        s_kT = [sb(f"kT{h}", [128, NT], BF16) for h in range(KV)]
        s_v = [sb(f"vT{h}", [128, NT], BF16) for h in range(KV)]
        s_sc = sb("s_sc", [128, 16])
        s_negM = sb("s_negM", [128, 1])
        s_id = sb("s_id", [128, 128], BF16)
        s_cm = sb("s_cm", [128, 128], BF16)
        s_sum = sb("s_sum", [128, NTT]); s_ssq = sb("s_ssq", [128, NTT])
        s_mx = sb("s_mx", [128, NTT]); s_mn = sb("s_mn", [128, NTT])
        s_m = sb("s_m", [128, NTT]); s_rstd = sb("s_rstd", [128, NTT])
        s_t1 = sb("s_t1", [128, NTT]); s_t2 = sb("s_t2", [128, NTT]); s_t3 = sb("s_t3", [128, NTT])
        s_red = sb("s_red", [128, 4])
        s_flip = sb("s_flip", [1, 4 * 128])
        s_loc = sb("s_loc", [1, 16])
        s_bc = sb("s_bc", [128, 16])

        def local_max_to_slot(src_col_ap, slot):
            wait("g", "sv", n_of("sv"))
            do("g", lambda e, a=src_col_ap, f=slot: e.dma_start(
                out=flip_hb[f:f + 1, :], in_=a), inc="dmig", by=16)
            wait("g", "dmig", n_of("dmig"))
            do("g", lambda e, f=slot: e.dma_start(
                out=s_flip.ap()[0:1, f * 128:(f + 1) * 128], in_=flip_hb[f:f + 1, :]), inc="dmig", by=16)
            wait("v", "dmig", n_of("dmig"))
            do("v", lambda e, f=slot: e.tensor_reduce(
                s_loc.ap()[0:1, f:f + 1], s_flip.ap()[0:1, f * 128:(f + 1) * 128], AX.X, ALU.max), inc="sv")

        def all_reduce(ridx):
            wait("g", "sv", n_of("sv"))
            do("g", lambda e, r=ridx: e.dma_start(out=cc_in[r][0:1, :], in_=s_loc.ap()[0:1, :]),
               inc="dmig", by=16)
            wait("g", "dmig", n_of("dmig"))
            do("g", lambda e, r=ridx: e.collective_compute(
                "AllReduce", ALU.max, replica_groups=[list(range(NCORES))],
                ins=[cc_in[r]], outs=[cc_out[r]]), inc="cc")

        def emit_scale_cols(ridx, pairs):
            wait("y", "cc", n_of("cc"))
            do("y", lambda e, r=ridx: e.dma_start(
                out=s_bc.ap(), in_=cc_out[r].broadcast_to([128, 16])), inc="dmiy", by=16)
            wait("v", "dmiy", n_of("dmiy"))
            for (bcol, scol, rscol) in pairs:
                do("v", lambda e, b=bcol, s=scol: e.tensor_scalar(
                    s_sc.ap()[:, s:s + 1], s_bc.ap()[:, b:b + 1], INV_QMAX, 1e-12,
                    ALU.mult, ALU.max), inc="sv")
                do("v", lambda e, s=scol, rs=rscol: e.reciprocal(
                    s_sc.ap()[:, rs:rs + 1], s_sc.ap()[:, s:s + 1]), inc="sv")

        do("v", lambda e: e.memset(s_negM.ap(), -MAGIC))
        do("v", lambda e: e.memset(s_loc.ap(), 0.0))
        do("y", lambda e: e.dma_start(out=s_id.ap(), in_=ident_h), inc="dwi", by=16)
        do("y", lambda e: e.dma_start(out=s_cm.ap(), in_=cmask_h), inc="dwi", by=16)

        ph0 = contextlib.ExitStack()
        sb0 = lambda name, shape, dt=F32: ph0.enter_context(nc.sbuf_tensor(name, shape, dt))
        ps0 = lambda name, shape, dt=F32: ph0.enter_context(nc.psum_tensor(name, shape, dt))

        s_wst = sb0("s_wst", [128, 4, 1024])
        wq_sb = sb0("wq_sb", [128, 8, 1024], BF16)
        wkd_sb = sb0("wkd_sb", [128, 8, 64], BF16)
        wvd_sb = sb0("wvd_sb", [128, 8, 64], BF16)
        wku_sb = sb0("wku_sb", [64, 128], BF16)
        wvu_sb = sb0("wvu_sb", [64, 128], BF16)
        s_updu = sb0("s_updu", [64, 128])

        for half in range(2):
            wait("y", "sv", n_of("sv"))
            do("y", lambda e, hf=half: e.dma_start(
                out=s_wst.ap(),
                in_=wq_h.rearrange("(mt k) d -> k mt d", k=128)[:, hf * 4:(hf + 1) * 4, :]),
                inc="dw", by=16)
            wait("v", "dw", n_of("dw"))
            do("v", lambda e, hf=half: e.tensor_copy(
                wq_sb.ap()[:, hf * 4:(hf + 1) * 4, :], s_wst.ap()), inc="sv")
        s_wkv = sb0("s_wkv", [128, 8, 64])
        for (h_ap, dst) in ((wkd_h, wkd_sb), (wvd_h, wvd_sb)):
            wait("y", "sv", n_of("sv"))
            do("y", lambda e, a=h_ap: e.dma_start(
                out=s_wkv.ap(), in_=a.rearrange("(mt k) d -> k mt d", k=128)), inc="dw", by=16)
            wait("v", "dw", n_of("dw"))
            do("v", lambda e, d=dst: e.tensor_copy(d.ap(), s_wkv.ap()), inc="sv")
        for (h_ap, dst) in ((wku_h, wku_sb), (wvu_h, wvu_sb)):
            wait("y", "sv", n_of("sv"))
            do("y", lambda e, a=h_ap: e.dma_start(out=s_updu.ap()[0:32, :], in_=a), inc="dw", by=16)
            do("y", lambda e, a=h_ap: e.dma_start(out=s_updu.ap()[32:64, :], in_=a), inc="dw", by=16)
            wait("v", "dw", n_of("dw"))
            do("v", lambda e, d=dst: e.tensor_copy(d.ap(), s_updu.ap()), inc="sv")

        s_wsh = sb0("s_wsh", [128, ITL, 1024])
        s_maxb = sb0("s_maxb", [128, 32])
        s_sb_ = sb0("s_sb_", [128, 32])
        s_rsb = sb0("s_rsb", [128, 32])
        s_wr = sb0("s_wr", [128, 1024])
        s_wrr = sb0("s_wrr", [128, 1024])
        s_wdq = sb0("s_wdq", [128, 1024], BF16)
        s_ag = sb0("s_ag", [128, I3], BF16)
        p_trw = None

        def dq_tile(tile_ap, out_col_fn):
            nb = 32
            do("v", lambda e, a=tile_ap: e.tensor_reduce(
                s_maxb.ap(), a.rearrange("p (b w) -> p b w", w=32),
                AX.X, ALU.max, apply_absolute_value=True), inc="gv")
            do("v", lambda e: e.tensor_scalar(
                s_sb_.ap(), s_maxb.ap(), INV7, 1e-12, ALU.mult, ALU.max), inc="gv")
            do("v", lambda e: e.reciprocal(s_rsb.ap(), s_sb_.ap()), inc="gv")
            do("v", lambda e, a=tile_ap: e.tensor_tensor(
                s_wr.ap().rearrange("p (b w) -> p b w", w=32),
                a.rearrange("p (b w) -> p b w", w=32),
                s_rsb.ap().unsqueeze(2).broadcast_to([128, nb, 32]), ALU.mult), inc="gv")
            do("v", lambda e: e.tensor_scalar(
                s_wrr.ap(), s_wr.ap(), MAGIC, MAGIC, ALU.add, ALU.subtract), inc="gv")
            wait("v", "st2", n_of("st2"))
            do("v", lambda e: e.tensor_tensor(
                s_wdq.ap().rearrange("p (b w) -> p b w", w=32),
                s_wrr.ap().rearrange("p (b w) -> p b w", w=32),
                s_sb_.ap().unsqueeze(2).broadcast_to([128, nb, 32]), ALU.mult), inc="gv")
            wait("t", "gv", n_of("gv"))
            wait("t", "st2", n_of("st2"))
            for mt in range(8):
                do("t", lambda e, m=mt: e.transpose(
                    psum_bf16(0)[:, m * 128:(m + 1) * 128],
                    s_wdq.ap()[:, m * 128:(m + 1) * 128], s_id.ap()),
                    inc=("dqtr" if mt == 7 else None))
            wait("s", "dqtr", n_of("dqtr"))
            for mg in range(8):
                do("s", lambda e, m=mg, oc=out_col_fn(mg): e.activation(
                    s_ag.ap()[:, oc:oc + 128],
                    psum_bf16(0)[:, m * 128:(m + 1) * 128], AF.Copy), inc="st2")

        wait("t", "dwi", 32)
        for (src_h, ag_h) in ((wup_sh_h, ag_wu_in), (wg_sh_h, ag_wg_in)):
            wait("y", "sv", n_of("sv"))
            wait("y", "st2", n_of("st2"))
            do("y", lambda e, s=src_h: e.dma_start(
                out=s_wsh.ap(), in_=s.rearrange("(t k) m -> k t m", k=128)), inc="dw", by=16)
            wait("v", "dw", n_of("dw"))
            for tl in range(ITL):
                dq_tile(s_wsh.ap()[:, tl, :], lambda mg, t=tl: mg * (ITL * 128) + t * 128)
            wait("g", "st2", n_of("st2"))
            do("g", lambda e, a=ag_h: e.dma_start(out=a, in_=s_ag.ap()), inc="dsp", by=16)
        wait("y", "st2", n_of("st2"))
        do("y", lambda e: e.dma_start(
            out=s_wsh.ap().rearrange("k t m -> k (t m)"), in_=wd_sh_h), inc="dw", by=16)
        wait("v", "dw", n_of("dw"))
        for sl in range(3):
            dq_tile(s_wsh.ap()[:, sl, :], lambda mg, s=sl: (s * 8 + mg) * 128)
        wait("g", "st2", n_of("st2"))
        do("g", lambda e: e.dma_start(out=ag_wd_in, in_=s_ag.ap()), inc="dsp", by=16)
        wait("g", "dsp", n_of("dsp"))
        for (ai, ao) in ((ag_wu_in, ag_wu_out), (ag_wg_in, ag_wg_out), (ag_wd_in, ag_wd_out)):
            do("g", lambda e, a=ai, b=ao: e.collective_compute(
                "AllGather", ALU.bypass, replica_groups=[list(range(NCORES))],
                ins=[a], outs=[b]), inc="cc")
        CC_AG = n_of("cc")

        do("s", lambda e: e.preload_act_table(AF.Square))
        s_xt = [sb0(f"s_xt{i}", [128, 1024]) for i in range(2)]
        s_xnb = sb0("s_xnb", [128, 1024], BF16)
        s_xn32 = sb0("s_xn32", [128, 1024])
        s_xn32b = sb0("s_xn32b", [128, 1024])

        def wait_xt_free(eng, par):
            wait_free(eng, ("xt", par))
            wait_free(eng, ("xt", par, "s"))

        for tt in range(NTT):
            b, tr = tt // (NTT // BL), (tt % (NTT // BL)) * 128
            par = tt % 2
            dxp = f"dx{par}"
            wait_xt_free("y", par)
            do("y", lambda e, b_=b, t_=tr, p=par: e.dma_start(
                out=s_xt[p].ap(), in_=x_h[b_, t_:t_ + 128, :]), inc=dxp, by=16)
            wait("v", dxp, n_of(dxp))
            do("v", lambda e, p=par, c=tt: e.tensor_reduce(
                s_sum.ap()[:, c:c + 1], s_xt[p].ap(), AX.X, ALU.add), inc="sv", dr=False)
            do("v", lambda e, p=par, c=tt: e.tensor_reduce(
                s_mx.ap()[:, c:c + 1], s_xt[p].ap(), AX.X, ALU.max), inc="sv", dr=False)
            do("v", lambda e, p=par, c=tt: e.tensor_reduce(
                s_mn.ap()[:, c:c + 1], s_xt[p].ap(), AX.X, ALU.min), inc="sv", dr=False)
            wait("s", dxp, n_of(dxp))
            do("s", lambda e, p=par, c=tt: e.activation(
                s_xnb.ap(), s_xt[p].ap(), AF.Square, accum_out=s_ssq.ap()[:, c:c + 1]), inc="ss")
            set_free(("xt", par), "sv")
            free_ev[("xt", par)] = ("sv", n_of("sv"))
            free_ev[("xt", par, "s")] = ("ss", n_of("ss"))
        wait("v", "ss", n_of("ss"))
        prog["v"].append(lambda e: e.drain())
        if dbg:
            wait("g", "sv", n_of("sv"))
            wait("g", "ss", n_of("ss"))
            for di, tns in enumerate((s_sum, s_ssq, s_mx, s_mn)):
                do("g", lambda e, d=di, t_=tns: e.dma_start(
                    out=ln_dump[:, (2 + d) * NTT:(3 + d) * NTT], in_=t_.ap()), inc="dmig", by=16)
            wait("v", "dmig", n_of("dmig"))
        do("v", lambda e: e.tensor_scalar(s_m.ap(), s_sum.ap(), 1.0 / D, None, ALU.mult), inc="sv")
        do("v", lambda e: e.tensor_tensor(s_t1.ap(), s_m.ap(), s_m.ap(), ALU.mult), inc="sv")
        do("v", lambda e: e.tensor_scalar(s_t2.ap(), s_ssq.ap(), 1.0 / D, None, ALU.mult), inc="sv")
        do("v", lambda e: e.tensor_tensor(s_sum.ap(), s_t2.ap(), s_t1.ap(), ALU.subtract), inc="sv")
        do("v", lambda e: e.tensor_scalar(s_t1.ap(), s_sum.ap(), EPS, None, ALU.add), inc="sv")
        wait("s", "sv", n_of("sv"))
        do("s", lambda e: e.preload_act_table(AF.Sqrt))
        do("s", lambda e: e.activation(s_t2.ap(), s_t1.ap(), AF.Sqrt), inc="ss")
        wait("v", "ss", n_of("ss"))
        prog["v"].append(lambda e: e.drain())
        if dbg:
            wait("g", "ss", n_of("ss"))
            do("g", lambda e: e.dma_start(out=ln_dump[:, 6 * NTT:7 * NTT], in_=s_t1.ap()), inc="dmig", by=16)
            do("g", lambda e: e.dma_start(out=ln_dump[:, 7 * NTT:8 * NTT], in_=s_t2.ap()), inc="dmig", by=16)
            wait("v", "dmig", n_of("dmig"))
        for _ in range(2):
            do("v", lambda e: e.reciprocal(s_ssq.ap(), s_t2.ap()), inc="sv")
            do("v", lambda e: e.tensor_tensor(s_sum.ap(), s_t1.ap(), s_ssq.ap(), ALU.mult), inc="sv")
            do("v", lambda e: e.tensor_tensor(s_t3.ap(), s_t2.ap(), s_sum.ap(), ALU.add), inc="sv")
            do("v", lambda e: e.tensor_scalar(s_t2.ap(), s_t3.ap(), 0.5, None, ALU.mult), inc="sv")
        do("v", lambda e: e.reciprocal(s_rstd.ap(), s_t2.ap()), inc="sv")
        do("v", lambda e: e.tensor_tensor(s_sum.ap(), s_mx.ap(), s_m.ap(), ALU.subtract), inc="sv")
        do("v", lambda e: e.tensor_tensor(s_ssq.ap(), s_m.ap(), s_mn.ap(), ALU.subtract), inc="sv")
        do("v", lambda e: e.tensor_tensor(s_t3.ap(), s_sum.ap(), s_ssq.ap(), ALU.max), inc="sv")
        do("v", lambda e: e.tensor_tensor(s_sum.ap(), s_t3.ap(), s_rstd.ap(), ALU.mult), inc="sv")
        do("v", lambda e: e.tensor_reduce(s_red.ap()[:, 0:1], s_sum.ap(), AX.X, ALU.max), inc="sv")
        if dbg:
            wait("g", "sv", n_of("sv"))
            for di, tns in enumerate((s_m, s_rstd)):
                do("g", lambda e, d=di, t_=tns: e.dma_start(
                    out=ln_dump[:, d * NTT:(d + 1) * NTT], in_=t_.ap()), inc="dmig", by=16)
            wait("g", "dmig", n_of("dmig"))
        local_max_to_slot(s_red.ap()[:, 0:1], 0)
        all_reduce(0)
        emit_scale_cols(0, [(0, SC_SX, SC_RSX)])

        def emit_xpass(quant):
            for tt in range(NTT):
                b, tr = tt // (NTT // BL), (tt % (NTT // BL)) * 128
                par = tt % 2
                dxp = f"dx{par}"
                wait_xt_free("y", par)
                do("y", lambda e, b_=b, t_=tr, p=par: e.dma_start(
                    out=s_xt[p].ap(), in_=x_h[b_, t_:t_ + 128, :]), inc=dxp, by=16)
                wait("v", dxp, n_of(dxp))
                wait_free("v", "xnb")
                if not quant:
                    do("v", lambda e, p=par, c=tt: e.tensor_scalar(
                        s_xnb.ap(), s_xt[p].ap(), s_m.ap()[:, c:c + 1], s_rstd.ap()[:, c:c + 1],
                        ALU.subtract, ALU.mult), inc="sv")
                else:
                    do("v", lambda e, p=par, c=tt: e.tensor_scalar(
                        s_xn32.ap(), s_xt[p].ap(), s_m.ap()[:, c:c + 1], s_rstd.ap()[:, c:c + 1],
                        ALU.subtract, ALU.mult), inc="sv")
                    do("v", lambda e: e.tensor_scalar(
                        s_xn32b.ap(), s_xn32.ap(), s_sc.ap()[:, SC_RSX:SC_RSX + 1], MAGIC,
                        ALU.mult, ALU.add), inc="sv")
                    do("v", lambda e: e.tensor_scalar(
                        s_xnb.ap(), s_xn32b.ap(), MAGIC, s_sc.ap()[:, SC_SX:SC_SX + 1],
                        ALU.subtract, ALU.mult), inc="sv")
                free_ev[("xt", par)] = ("sv", n_of("sv"))
                free_ev[("xt", par, "s")] = ("ss", 0)
                wait("t", "sv", n_of("sv"))
                wait_free("t", "ptx")
                for dt in range(DT):
                    do("t", lambda e, d=dt: e.transpose(
                        psum_bf16(1)[:, d * 128:(d + 1) * 128],
                        s_xnb.ap()[:, d * 128:(d + 1) * 128], s_id.ap()),
                        inc=("st2" if dt == DT - 1 else None))
                set_free("xnb", "st2")
                wait("s", "st2", n_of("st2"))
                for dt in range(DT):
                    do("s", lambda e, d=dt, c=tt: e.activation(
                        bigv(d)[:, c * 128:(c + 1) * 128],
                        psum_bf16(1)[:, d * 128:(d + 1) * 128], AF.Copy), inc="ss",
                        dr=(dt == DT - 1))
                set_free("ptx", "ss")

        emit_xpass(quant=False)

        barrier()
        if dbg:
            do("g", lambda e: e.dma_start(
                out=big_dump, in_=s_big.ap().bitcast(U16)), inc="dmig", by=16)
            wait("g", "dmig", n_of("dmig"))
        s_lat = sb0("s_lat", [64, NT], BF16)
        s_latv = sb0("s_latv", [64, NT], BF16)
        s_qtb = sb0("s_qtb", [128, NT], BF16)
        p_q = [psum_f32(2), psum_f32(3)]
        p_l = psum_f32(4)[0:64, :]
        for (dst, wsb) in ((s_lat, wkd_sb), (s_latv, wvd_sb)):
            for tc in range(NTC):
                wait("t", "lev", n_of("lev"))
                for mt in range(DT):
                    do("t", lambda e, m=mt, c=tc, w=wsb: e.matmul(
                        p_l, w.ap()[:, m, :], bigv(m)[:, c * 512:(c + 1) * 512],
                        start=(m == 0), stop=(m == DT - 1)), inc=("lmm" if mt == DT - 1 else None))
                wait("s", "lmm", n_of("lmm"))
                do("s", lambda e, c=tc, d=dst: e.activation(
                    d.ap()[:, c * 512:(c + 1) * 512], p_l, AF.Copy), inc="lev")
        nk = [0]
        for h in range(KV):
            for tc in range(NTC):
                par = nk[0] % 2
                if nk[0] >= 2:
                    wait("t", "kev", nk[0] - 1)
                wait("t", "lev", NTC)
                do("t", lambda e, h_=h, c=tc, p=par: e.matmul(
                    p_q[p], wku_sb.ap()[h_ * 32:(h_ + 1) * 32, :],
                    s_lat.ap()[h_ * 32:(h_ + 1) * 32, c * 512:(c + 1) * 512],
                    start=True, stop=True), inc="kmm")
                wait("s", "kmm", n_of("kmm"))
                do("s", lambda e, h_=h, c=tc, p=par: e.activation(
                    s_kT[h_].ap()[:, c * 512:(c + 1) * 512], p_q[p], AF.Copy), inc="kev")
                nk[0] += 1
        nv = [0]
        wait("t", "kev", n_of("kev"))
        for h in range(KV):
            for tt in range(NTT):
                par = nv[0] % 2
                if nv[0] >= 2:
                    wait("t", "vev", nv[0] - 1)
                wait("t", "lev", 2 * NTC)
                do("t", lambda e, h_=h, c=tt, p=par: e.matmul(
                    p_q[p][:, 0:128], s_latv.ap()[h_ * 32:(h_ + 1) * 32, c * 128:(c + 1) * 128],
                    wvu_sb.ap()[h_ * 32:(h_ + 1) * 32, :], start=True, stop=True), inc="vmm")
                wait("s", "vmm", n_of("vmm"))
                do("s", lambda e, h_=h, c=tt, p=par: e.activation(
                    s_v[h_].ap()[:, c * 128:(c + 1) * 128], p_q[p][:, 0:128], AF.Copy), inc="vev")
                nv[0] += 1
        nq = [0]
        wait("t", "vev", n_of("vev"))
        for dt in range(DT):
            wait_free("s", ("qtb",))
            for tc in range(NTC):
                par = nq[0] % 2
                if nq[0] >= 2:
                    wait("t", "qev", nq[0] - 1)
                for mt in range(DT):
                    do("t", lambda e, m=mt, d=dt, c=tc, p=par: e.matmul(
                        p_q[p], wq_sb.ap()[:, m, d * 128:(d + 1) * 128],
                        bigv(m)[:, c * 512:(c + 1) * 512],
                        start=(m == 0), stop=(m == DT - 1)), inc=("qmm" if mt == DT - 1 else None))
                wait("s", "qmm", n_of("qmm"))
                do("s", lambda e, c=tc, p=par: e.activation(
                    s_qtb.ap()[:, c * 512:(c + 1) * 512], p_q[p], AF.Identity,
                    scale=SCALE_ATT), inc="qev")
                nq[0] += 1
            wait("g", "qev", n_of("qev"))
            do("g", lambda e, d=dt: e.dma_start(out=qt_sp[d], in_=s_qtb.ap()), inc="dsp", by=16)
            set_free(("qtb",), "dsp")

        barrier()
        emit_xpass(quant=True)

        barrier()
        ph0.close()
        phB = contextlib.ExitStack()
        sbB = lambda name, shape, dt=F32: phB.enter_context(nc.sbuf_tensor(name, shape, dt))
        psB = lambda name, shape, dt=F32: phB.enter_context(nc.psum_tensor(name, shape, dt))
        wl = [sbB(f"wl{i}", [128, 8, 128], BF16) for i in range(2)]
        stg = [sbB(f"stg{i}", [128, NT]) for i in range(2)]
        s_mxu = sbB("s_mxu", [128, 2 * IT * NTC])
        p_b = [psum_f32(i) for i in range(4)]
        wait("y", "cc", CC_AG)
        nmmB = [0]
        nspB = [0]
        for src_idx, (ag_h, sp_h) in enumerate(((ag_wu_out, upraw_sp), (ag_wg_out, gtraw_sp))):
            for it in range(IT):
                k = src_idx * IT + it
                wpar, spar = k % 2, k % 2
                rk, il = it // ITL, it % ITL
                dwlp = f"dwl{wpar}"
                wait_free("y", ("wl", wpar))
                do("y", lambda e, a=ag_h, r=rk, i=il, w=wpar: e.dma_start(
                    out=wl[w].ap(),
                    in_=a[r * 128:(r + 1) * 128, :].rearrange(
                        "p (mt q) -> p mt q", q=ITL * 128)[:, :, i * 128:(i + 1) * 128]),
                    inc=dwlp, by=16)
                for tc in range(NTC):
                    slot = nmmB[0] % 4
                    wait("t", dwlp, n_of(dwlp))
                    if nmmB[0] >= 4:
                        wait("t", "brd", nmmB[0] - 3)
                        wait("t", "bev", nmmB[0] - 3)
                    for mt in range(DT):
                        do("t", lambda e, m=mt, c=tc, w=wpar, sl=slot: e.matmul(
                            p_b[sl], wl[w].ap()[:, m, :], bigv(m)[:, c * 512:(c + 1) * 512],
                            start=(m == 0), stop=(m == DT - 1)), inc=("bmm" if mt == DT - 1 else None))
                    wait("v", "bmm", n_of("bmm"))
                    do("v", lambda e, sl=slot, mc=nmmB[0]: e.tensor_reduce(
                        s_mxu.ap()[:, mc:mc + 1], p_b[sl], AX.X, ALU.max,
                        apply_absolute_value=True), inc="brd")
                    wait("s", "brd", n_of("brd"))
                    if tc == 0:
                        wait_free("s", ("stg", spar))
                    do("s", lambda e, sl=slot, c=tc, sp=spar: e.activation(
                        stg[sp].ap()[:, c * 512:(c + 1) * 512], p_b[sl], AF.Copy), inc="bev")
                    nmmB[0] += 1
                set_free(("wl", wpar), "bmm")
                wait("g", "bev", n_of("bev"))
                do("g", lambda e, i=it, sp=spar, h=sp_h: e.dma_start(
                    out=h[i], in_=stg[sp].ap()), inc=f"dspb{spar}", by=16)
                set_free(("stg", spar), f"dspb{spar}")
                nspB[0] += 1
        wait("v", "brd", n_of("brd"))
        do("v", lambda e: e.tensor_reduce(
            s_red.ap()[:, 0:1], s_mxu.ap()[:, 0:IT * NTC], AX.X, ALU.max), inc="sv")
        do("v", lambda e: e.tensor_reduce(
            s_red.ap()[:, 1:2], s_mxu.ap()[:, IT * NTC:2 * IT * NTC], AX.X, ALU.max), inc="sv")
        local_max_to_slot(s_red.ap()[:, 0:1], 0)
        local_max_to_slot(s_red.ap()[:, 1:2], 1)
        all_reduce(1)
        emit_scale_cols(1, [(0, SC_SU, SC_RSU), (1, SC_SG, SC_RSG)])
        do("v", lambda e: e.tensor_scalar(
            s_sc.ap()[:, SC_SG16:SC_SG16 + 1], s_sc.ap()[:, SC_SG:SC_SG + 1], P2_16, None,
            ALU.mult), inc="sv")
        wait("g", "dspb0", n_of("dspb0"))
        wait("g", "dspb1", n_of("dspb1"))
        phB.close()
        phM = contextlib.ExitStack()
        sbM = lambda name, shape, dt=F32: phM.enter_context(nc.sbuf_tensor(name, shape, dt))
        uld = [bigf[:, 0:NT], bigf[:, NT:2 * NT]]
        g2t = bigf[:, 2 * NT:3 * NT]
        sigt = bigf[:, 3 * NT:4 * NT]
        phAT = contextlib.ExitStack()
        sbT = lambda name, shape, dt=F32: phAT.enter_context(nc.sbuf_tensor(name, shape, dt, side="right"))
        s_qhr = [sbT(f"s_qhr{i}", [128, NT], BF16) for i in range(2)]
        s_pb = [sbT(f"s_pb{i}", [128, 2048], BF16) for i in range(2)]
        s_pT = [sbT(f"s_pT{i}", [128, 1024], BF16) for i in range(2)]
        s_cmx = [sbT(f"s_cmx{i}", [128, 4]) for i in range(4)]
        s_csm = [sbT(f"s_csm{i}", [128, 4]) for i in range(4)]
        s_ngm = [sbT(f"s_ngm{i}", [128, 1]) for i in range(4)]
        s_rcp = [sbT(f"s_rcp{i}", [128, 1]) for i in range(4)]
        s_ystA = [sbT(f"s_ystA{i}", [128, 2048], BF16) for i in range(2)]
        p_s = [psum_f32(i) for i in range(4)]
        p_t2 = [psum_bf16(4), psum_bf16(5)]
        p_y2 = [psum_f32(6), psum_f32(7)]

        gchunk = [0]
        gq_ctr = [0]
        tr_grp = [0]
        pend = []
        trcnt_after = {}
        pv_after_grp = {}
        sc_after_hr = {}
        chunk_free = {}
        exp_after_tile = {}

        def emit_tp_pv(info):
            (h, r, b, qi, tile_id) = info
            ntl = qi + 1
            ngr = (ntl + 7) // 8
            for g in range(ngr):
                j0, j1 = g * 8, min(g * 8 + 7, qi)
                bank = tr_grp[0] % 2
                wait_free("t", ("pt2", bank))
                wait("t", "a_ex", exp_after_tile[tile_id])
                for j in range(j0, j1 + 1):
                    do("t", lambda e, j_=j, bk=bank, pb=(tile_id % 2): e.transpose(
                        p_t2[bk][:, (j_ % 8) * 128:((j_ % 8) + 1) * 128],
                        s_pb[pb].ap()[:, j_ * 128:(j_ + 1) * 128], s_id.ap()),
                        inc=("a_tr" if j == j1 else None))
                wait("s", "a_tr", n_of("a_tr"))
                wait_free("s", ("spT", bank))
                do("s", lambda e, bk=bank, n=(j1 - j0 + 1) * 128: e.activation(
                    s_pT[bk].ap()[:, 0:n], p_t2[bk][:, 0:n], AF.Copy), inc="a_pt")
                free_ev[("pt2", bank)] = ("a_pt", n_of("a_pt"))
                wait("t", "a_pt", n_of("a_pt"))
                if g == 0:
                    wait_free("t", ("py2", tile_id % 2))
                for j in range(j0, j1 + 1):
                    gtile = b * NQT + j
                    do("t", lambda e, h_=h, gt=gtile, bk=bank, jj=(j - j0), tb=(tile_id % 2), q_=qi, j_=j: e.matmul(
                        p_y2[tb][:, 0:128],
                        s_pT[bk].ap()[:, jj * 128:(jj + 1) * 128],
                        s_v[h_].ap()[:, gt * 128:(gt + 1) * 128],
                        start=(j_ == 0), stop=(j_ == q_)),
                        inc=("a_pv" if j == j1 else None))
                free_ev[("spT", bank)] = ("a_pv", n_of("a_pv"))
                tr_grp[0] += 1
            trcnt_after[tile_id] = n_of("a_tr")
            wait("s", "a_pv", n_of("a_pv"))
            wait("s", "a_rc", n_of("a_rc"))
            blkp = (tile_id // NQT) % 2
            lastq = (qi == NQT - 1)
            do("s", lambda e, tb=(tile_id % 2), p=(tile_id % 4), bp=blkp, q_=qi: e.activation(
                s_ystA[bp].ap()[:, q_ * 128:(q_ + 1) * 128], p_y2[tb][:, 0:128],
                AF.Identity, scale=s_rcp[p].ap()), inc="a_ar", dr=lastq)
            free_ev[("py2", tile_id % 2)] = ("a_ar", n_of("a_ar"))
            free_ev[("rcp", tile_id % 4)] = ("a_ar", n_of("a_ar"))

        def attn_stream():
            for h in range(KV):
                for r in range(REP):
                    hri = h * REP + r
                    qpar = hri % 2
                    dqtp = f"dqt{qpar}"
                    if hri >= 2:
                        wait("y", "a_sc", sc_after_hr[hri - 2])
                    do("y", lambda e, d=hri, p=qpar: e.dma_start(
                        out=s_qhr[p].ap(), in_=qt_sp[d]), inc=dqtp, by=16)
                    wait("t", dqtp, n_of(dqtp))
                    for b in range(BL):
                        blk = hri * BL + b
                        for qi in range(NQT):
                            tile_id = gq_ctr[0]
                            kl = (qi + 1) * 128
                            nch = (kl + 511) // 512
                            pbp = tile_id % 2
                            sp = tile_id % 4
                            base = gchunk[0]
                            for c in range(nch):
                                cn = min(512, kl - c * 512)
                                slot = gchunk[0] % 4
                                if gchunk[0] >= 4:
                                    wait("t", "a_ex", chunk_free[gchunk[0] - 4])
                                last = (c == nch - 1)
                                do("t", lambda e, h_=h, q_=qi, c_=c, n=cn, sl=slot, b_=b, p=qpar, lst=last: e.matmul(
                                    p_s[sl][:, 0:n],
                                    s_qhr[p].ap()[:, b_ * T + q_ * 128: b_ * T + (q_ + 1) * 128],
                                    s_kT[h_].ap()[:, b_ * T + c_ * 512: b_ * T + c_ * 512 + n],
                                    start=True, stop=(not lst)), inc=(None if last else "a_sc"))
                                if last:
                                    do("t", lambda e, sl=slot, o=cn - 128: e.matmul(
                                        p_s[sl][:, o:o + 128], s_id.ap(), s_cm.ap(),
                                        start=False, stop=True), inc="a_sc")
                                gchunk[0] += 1
                            for c in range(nch):
                                cn = min(512, kl - c * 512)
                                slot = (base + c) % 4
                                wait("v", "a_sc", base + c + 1)
                                if nch == 1:
                                    do("v", lambda e, sl=slot, n=cn, p=sp: e.tensor_reduce(
                                        s_ngm[p].ap(), p_s[sl][:, 0:n], AX.X,
                                        ALU.max, negate=True), inc="a_ngm")
                                else:
                                    do("v", lambda e, sl=slot, n=cn, c_=c, p=sp: e.tensor_reduce(
                                        s_cmx[p].ap()[:, c_:c_ + 1], p_s[sl][:, 0:n], AX.X,
                                        ALU.max), inc=None, dr=(c == nch - 1))
                            if nch > 1:
                                do("v", lambda e, p=sp, n=nch: e.tensor_reduce(
                                    s_ngm[p].ap(), s_cmx[p].ap()[:, 0:n], AX.X, ALU.max,
                                    negate=True), inc="a_ngm")
                            wait("s", "a_ngm", n_of("a_ngm"))
                            if tile_id >= 2:
                                wait("s", "a_tr", trcnt_after[tile_id - 2])
                            if tile_id >= 4:
                                wait("s", "a_rc", n_of("a_rc") - 3)
                            for c in range(nch):
                                cn = min(512, kl - c * 512)
                                slot = (base + c) % 4
                                do("s", lambda e, sl=slot, n=cn, c_=c, pb=pbp, p=sp: e.activation(
                                    s_pb[pb].ap()[:, c_ * 512:c_ * 512 + n], p_s[sl][:, 0:n],
                                    AF.Exp, bias=s_ngm[p].ap(),
                                    accum_out=s_csm[p].ap()[:, c_:c_ + 1]),
                                    inc=("a_ex" if c == nch - 1 else None), by=nch,
                                    dr=(c == nch - 1))
                            exp_after_tile[tile_id] = n_of("a_ex")
                            for c in range(nch):
                                chunk_free[base + c] = n_of("a_ex")
                            wait("v", "a_ex", n_of("a_ex"))
                            wait_free("v", ("rcp", sp))
                            if nch == 1:
                                do("v", lambda e, p=sp: e.reciprocal(
                                    s_rcp[p].ap(), s_csm[p].ap()[:, 0:1]), inc="a_rc")
                            else:
                                do("v", lambda e, p=sp, n=nch: e.tensor_reduce(
                                    s_rcp[p].ap(), s_csm[p].ap()[:, 0:n], AX.X, ALU.add),
                                    inc=None, dr=True)
                                do("v", lambda e, p=sp: e.reciprocal(
                                    s_rcp[p].ap(), s_rcp[p].ap()), inc="a_rc")
                            if pend:
                                emit_tp_pv(pend.pop())
                            pend.append((h, r, b, qi, tile_id))
                            gq_ctr[0] += 1
                            yield
                        if pend:
                            emit_tp_pv(pend.pop())
                        bp = blk % 2
                        wait("y", "a_ar", n_of("a_ar"))
                        do("y", lambda e, d=hri, b_=b, p=bp: e.dma_start(
                            out=yt_sp[d][:, b_ * T:(b_ + 1) * T], in_=s_ystA[p].ap()),
                            inc="dyt", by=16)
                        free_ev[("ystA", bp)] = ("dyt", n_of("dyt"))
                        yst_guard[bp] = n_of("dyt")
                        sc_after_hr[hri] = n_of("a_sc")
                    yield

        yst_guard = {}
        attn_gen = attn_stream()
        n_pumped = [0]

        def pump(n):
            for _ in range(n):
                if next(attn_gen, "END") == "END":
                    return

        _orig_emit = emit_tp_pv
        def emit_tp_pv(info):
            (h, r, b, qi, tile_id) = info
            if qi == 0:
                bp = (tile_id // NQT) % 2
                if bp in yst_guard:
                    wait("s", "dyt", yst_guard[bp])
            _orig_emit(info)

        s_pool = sbM("s_pool", [128, 8 * NT], BF16)
        poolf = s_pool.ap().bitcast(F32)
        pooli = s_pool.ap().bitcast(I16)
        poolu = s_pool.ap().bitcast(U16)
        gld = [poolf[:, 0:NT], poolf[:, NT:2 * NT]]
        s_qg = [pooli[:, 4 * NT:5 * NT], pooli[:, 5 * NT:6 * NT]]
        s_qsig = [poolu[:, 6 * NT:7 * NT], poolu[:, 7 * NT:8 * NT]]
        g2tb = [bigf[:, 2 * NT:3 * NT], bigf[:, 0:NT]]
        sigtb = [bigf[:, 3 * NT:4 * NT], bigf[:, NT:2 * NT]]
        s_mxc = sbM("s_mxc", [128, IT])
        sv_after_c = {}
        ss_after_c = {}
        dsr_after_c = {}
        wait("v", "bmm", n_of("bmm"))
        wait("y", "dspb0", n_of("dspb0"))
        wait("y", "dspb1", n_of("dspb1"))
        do("y", lambda e: e.dma_start(out=gld[0], in_=gtraw_sp[0]), inc="dsr0", by=16)
        dsr_after_c[0] = n_of("dsr0")
        do("s", lambda e: e.preload_act_table(AF.Exp))
        for k in range(IT + 2):
            par = k % 2
            if k >= 2:
                kk = k - 2
                p2 = kk % 2
                wait("v", "ss", ss_after_c[kk])
                do("v", lambda e, p=p2: e.tensor_scalar(
                    g2tb[p], sigtb[p], 65536.0, MAGIC, ALU.mult, ALU.add), inc="sv")
                wait_free("v", ("qsig", p2))
                do("v", lambda e, p=p2: e.tensor_scalar(
                    s_qsig[p], g2tb[p], MAGIC, 65535.0, ALU.subtract, ALU.min), inc="sv")
                do("v", lambda e, p=p2: e.scalar_tensor_tensor(
                    g2tb[p], s_qg[p], s_sc.ap()[:, SC_SG16:SC_SG16 + 1], s_qsig[p],
                    ALU.mult, ALU.mult), inc="sv")
                do("v", lambda e, i=kk, p=p2: e.tensor_reduce(
                    s_mxc.ap()[:, i:i + 1], g2tb[p], AX.X, ALU.max,
                    apply_absolute_value=True), inc="sv", dr=(kk >= IT - 2))
            if k < IT:
                wait("v", f"dsr{par}", dsr_after_c[k])
                do("v", lambda e, p=par: e.tensor_scalar(
                    g2tb[p], gld[p], s_sc.ap()[:, SC_RSG:SC_RSG + 1], MAGIC,
                    ALU.mult, ALU.add), inc="sv")
                free_ev[("gld", par)] = ("sv", n_of("sv"))
                sv_after_c[k] = n_of("sv")
            if 1 <= k <= IT:
                km = k - 1
                pm = km % 2
                wait("s", "sv", sv_after_c[km])
                wait_free("s", ("qg", pm))
                do("s", lambda e: e.preload_act_table(AF.Sigmoid))
                do("s", lambda e, p=pm: e.activation(
                    s_qg[p], g2tb[p], AF.Identity, bias=s_negM.ap()), inc="ss")
                do("s", lambda e, p=pm: e.activation(
                    sigtb[p], s_qg[p], AF.Sigmoid,
                    scale=s_sc.ap()[:, SC_SG:SC_SG + 1]), inc="ss")
                do("s", lambda e: e.preload_act_table(AF.Exp))
                ss_after_c[km] = n_of("ss")
            if k + 1 < IT:
                pn = (k + 1) % 2
                wait_free("y", ("gld", pn))
                do("y", lambda e, i=k + 1, p=pn: e.dma_start(
                    out=gld[p], in_=gtraw_sp[i]), inc=f"dsr{pn}", by=16)
                dsr_after_c[k + 1] = n_of(f"dsr{pn}")
            if k >= 2:
                kk = k - 2
                p2 = kk % 2
                wait("g", "ss", ss_after_c[kk])
                wait("g", "sv", n_of("sv"))
                do("g", lambda e, i=kk, p=p2: e.dma_start(
                    out=qg_sp[i], in_=s_qg[p]), inc=f"dspc{p2}", by=16)
                do("g", lambda e, i=kk, p=p2: e.dma_start(
                    out=qsig_sp[i], in_=s_qsig[p]), inc=f"dspc{p2}", by=16)
                free_ev[("qg", p2)] = (f"dspc{p2}", n_of(f"dspc{p2}"))
                free_ev[("qsig", p2)] = (f"dspc{p2}", n_of(f"dspc{p2}"))
            pump(4)
        do("v", lambda e: e.tensor_reduce(
            s_red.ap()[:, 0:1], s_mxc.ap(), AX.X, ALU.max), inc="sv")
        local_max_to_slot(s_red.ap()[:, 0:1], 0)
        all_reduce(2)
        pump(10)
        emit_scale_cols(2, [(0, SC_SG2, SC_RSG2)])

        HIT = 2 * IT
        NTH = NT // 2
        gqh = [pooli[:, 0:NTH], pooli[:, NTH:2 * NTH]]
        gsh = [poolu[:, 2 * NTH:3 * NTH], poolu[:, 3 * NTH:4 * NTH]]
        qg2h = [pooli[:, 4 * NTH:5 * NTH], pooli[:, 5 * NTH:6 * NTH]]
        quph = [pooli[:, 6 * NTH:7 * NTH], pooli[:, 7 * NTH:8 * NTH]]
        g2tD = [poolf[:, 8 * (NTH // 2):10 * (NTH // 2)],
                poolf[:, 10 * (NTH // 2):12 * (NTH // 2)]]
        sigtD = [poolf[:, 12 * (NTH // 2):14 * (NTH // 2)],
                 poolf[:, 14 * (NTH // 2):16 * (NTH // 2)]]
        uldh = [bigf[:, 0:NTH], bigf[:, NTH:2 * NTH]]
        s_mxd = sbM("s_mxd", [128, HIT])
        sv_after_d = {}
        ss_after_d = {}
        dsr_after_d = {}

        def d_load(e, k, p, sem):
            i, hh = k // 2, k % 2
            e.dma_start(out=gqh[p], in_=qg_sp[i][:, hh * NTH:(hh + 1) * NTH]).then_inc(sem, 16)
            e.dma_start(out=gsh[p], in_=qsig_sp[i][:, hh * NTH:(hh + 1) * NTH]).then_inc(sem, 16)
            e.dma_start(out=uldh[p], in_=upraw_sp[i][:, hh * NTH:(hh + 1) * NTH]).then_inc(sem, 16)
        wait("y", "dspc0", n_of("dspc0"))
        wait("y", "dspc1", n_of("dspc1"))
        wait("y", "sv", n_of("sv"))
        wait("y", "ss", n_of("ss"))
        wait("v", "dspc0", n_of("dspc0"))
        wait("v", "dspc1", n_of("dspc1"))
        wait("s", "dspc0", n_of("dspc0"))
        wait("s", "dspc1", n_of("dspc1"))
        cnt["dsr0"] = cnt.get("dsr0", 0) + 48
        prog["y"].append(lambda e: d_load(e, 0, 0, sems["dsr0"]))
        dsr_after_d[0] = n_of("dsr0")
        for k in range(HIT + 2):
            par = k % 2
            if k >= 2:
                kk = k - 2
                p2 = kk % 2
                wait("v", "ss", ss_after_d[kk])
                do("v", lambda e, p=p2: e.scalar_tensor_tensor(
                    g2tD[p], qg2h[p], s_sc.ap()[:, SC_SG2:SC_SG2 + 1], sigtD[p],
                    ALU.mult, ALU.mult), inc="sv")
                do("v", lambda e, i=kk, p=p2: e.tensor_reduce(
                    s_mxd.ap()[:, i:i + 1], g2tD[p], AX.X, ALU.max,
                    apply_absolute_value=True), inc="sv", dr=(kk >= HIT - 2))
            if k < HIT:
                wait("v", f"dsr{par}", dsr_after_d[k])
                do("v", lambda e, p=par: e.scalar_tensor_tensor(
                    g2tD[p], gqh[p], s_sc.ap()[:, SC_SG16:SC_SG16 + 1], gsh[p],
                    ALU.mult, ALU.mult), inc="sv")
                free_ev[("gqh", par)] = ("sv", n_of("sv"))
                do("v", lambda e, p=par: e.tensor_scalar(
                    sigtD[p], g2tD[p], s_sc.ap()[:, SC_RSG2:SC_RSG2 + 1], MAGIC,
                    ALU.mult, ALU.add), inc="sv", dr=False)
                do("v", lambda e, p=par: e.tensor_scalar(
                    g2tD[p], uldh[p], s_sc.ap()[:, SC_RSU:SC_RSU + 1], MAGIC,
                    ALU.mult, ALU.add), inc="sv")
                free_ev[("uldh", par)] = ("sv", n_of("sv"))
                sv_after_d[k] = n_of("sv")
            if 1 <= k <= HIT:
                km = k - 1
                pm = km % 2
                wait("s", "sv", sv_after_d[km])
                wait_free("s", ("qg2h", pm))
                do("s", lambda e, p=pm: e.activation(
                    qg2h[p], sigtD[p], AF.Identity, bias=s_negM.ap()), inc="ss", dr=False)
                wait_free("s", ("quph", pm))
                do("s", lambda e, p=pm: e.activation(
                    quph[p], g2tD[p], AF.Identity, bias=s_negM.ap()), inc="ss")
                do("s", lambda e, p=pm: e.activation(
                    sigtD[p], quph[p], AF.Identity,
                    scale=s_sc.ap()[:, SC_SU:SC_SU + 1]), inc="ss")
                ss_after_d[km] = n_of("ss")
            if k + 1 < HIT:
                pn = (k + 1) % 2
                wait_free("y", ("gqh", pn))
                wait_free("y", ("uldh", pn))
                cnt[f"dsr{pn}"] = cnt.get(f"dsr{pn}", 0) + 48
                prog["y"].append(lambda e, kn=k + 1, p=pn, sm=sems[f"dsr{pn}"]: d_load(e, kn, p, sm))
                dsr_after_d[k + 1] = n_of(f"dsr{pn}")
            if k >= 2:
                kk = k - 2
                p2 = kk % 2
                it, hh = kk // 2, kk % 2
                wait("g", "ss", ss_after_d[kk])
                wait("g", "sv", n_of("sv"))
                do("g", lambda e, i=it, h_=hh, p=p2: e.dma_start(
                    out=qg2_sp[h_ * 4:(h_ + 1) * 4, i, :, :].rearrange("tc p c -> p tc c"),
                    in_=qg2h[p].rearrange("p (tc c) -> p tc c", c=512)), inc=f"dspd{p2}", by=16)
                do("g", lambda e, i=it, h_=hh, p=p2: e.dma_start(
                    out=qup_sp[h_ * 4:(h_ + 1) * 4, i, :, :].rearrange("tc p c -> p tc c"),
                    in_=quph[p].rearrange("p (tc c) -> p tc c", c=512)), inc=f"dspd{p2}", by=16)
                free_ev[("qg2h", p2)] = (f"dspd{p2}", n_of(f"dspd{p2}"))
                free_ev[("quph", p2)] = (f"dspd{p2}", n_of(f"dspd{p2}"))
            pump(2)
        do("v", lambda e: e.tensor_reduce(
            s_red.ap()[:, 0:1], s_mxd.ap(), AX.X, ALU.max), inc="sv")
        local_max_to_slot(s_red.ap()[:, 0:1], 0)
        all_reduce(3)
        pump(10)
        emit_scale_cols(3, [(0, SC_SO, SC_RSO)])

        wdT = s_pool.ap()[:, 0:IT * 1024].rearrange("p (i c) -> p i c", c=1024)
        s_oTx = s_pool.ap()[:, IT * 1024:IT * 1024 + NT]
        _spare = 14336
        s_up5 = [poolf[:, _spare:_spare + 512], poolf[:, _spare + 512:_spare + 1024]]
        s_or5 = poolf[:, _spare + 1024:_spare + 1536]
        s_or5b = poolf[:, _spare + 1536:_spare + 2048]
        s_fout = [None, None]
        p_e = [psum_f32(0), psum_f32(1)]
        s_up2 = [poolf[:, _spare:_spare + 1024], poolf[:, _spare + 1024:_spare + 2048]]

        def qg22_ap(c):
            return bigi[:, c * 1024:(c + 1) * 1024]

        def qup2_ap(c):
            return bigi[:, 24 * 512 + c * 1024:24 * 512 + (c + 1) * 1024]

        def oT_ap(par, it):
            if par == 0:
                if it < 16:
                    return s_big.ap()[:, (6 * NT) + it * 512:(6 * NT) + (it + 1) * 512]
                return s_oTx[:, (it - 16) * 512:(it - 15) * 512]
            if it < 8:
                return s_kT[0].ap()[:, it * 512:(it + 1) * 512]
            if it < 16:
                return s_kT[1].ap()[:, (it - 8) * 512:(it - 7) * 512]
            return s_v[0].ap()[:, (it - 16) * 512:(it - 15) * 512]

        def oT2_ap(par, c):
            if par == 0:
                if c < 8:
                    return s_big.ap()[:, (6 * NT) + c * 1024:(6 * NT) + (c + 1) * 1024]
                return s_oTx[:, (c - 8) * 1024:(c - 7) * 1024]
            if c < 4:
                return s_kT[0].ap()[:, c * 1024:(c + 1) * 1024]
            if c < 8:
                return s_kT[1].ap()[:, (c - 4) * 1024:(c - 3) * 1024]
            return s_v[0].ap()[:, (c - 8) * 1024:(c - 7) * 1024]

        wait("y", "dspd0", n_of("dspd0"))
        wait("y", "dspd1", n_of("dspd1"))
        wait("y", "sv", n_of("sv"))
        pump(10000)
        barrier()
        phAT.close()
        s_fout[0] = sbM("s_fout0", [128, 1024])
        s_fout[1] = sbM("s_fout1", [128, 1024])
        wo_sb = sbM("wo_sb", [128, 8, 1024], BF16)
        s_yld = [sbM(f"s_yld{i}", [128, 8, 128], BF16) for i in range(2)]
        s_ylhs = [sbM(f"s_ylhs{i}", [128, 8, 128], BF16) for i in range(2)]
        s_xw = [sbM(f"s_xw{i}", [128, 1024]) for i in range(2)]
        s_or2 = s_v[1].ap().bitcast(F32)[:, 0:1024]
        s_ow_ap = s_v[1].ap().bitcast(F32)[:, 1024:2048]
        p_tr = [psum_bf16(2), psum_bf16(3)]
        p_w = [psum_f32(4), psum_f32(5)]
        for ch in range(8):
            wait_free("y", ("xw", ch % 2))
            do("y", lambda e, c=ch, p=(ch % 2): e.dma_start(
                out=s_xw[p].ap(),
                in_=wo_h.rearrange("(mt k) d -> k mt d", k=128)[:, c, :]),
                inc=f"dx{ch % 2}", by=16)
            wait("v", f"dx{ch % 2}", n_of(f"dx{ch % 2}"))
            do("v", lambda e, c=ch, p=(ch % 2): e.tensor_copy(
                wo_sb.ap()[:, c, :], s_xw[p].ap()), inc="sv")
            free_ev[("xw", ch % 2)] = ("sv", n_of("sv"))
        WOC = [n_of("sv")]
        wait("y", "dyt", n_of("dyt"))
        for it in range(IT):
            do("y", lambda e, i=it: e.dma_start(
                out=wdT[:, i, :].rearrange("p (r h) -> p r h", r=NCORES),
                in_=ag_wd_out.rearrange("(r p) (it h) -> p r it h", p=128, h=128)[:, :, i, :]),
                inc="dw", by=16)

        dse_after = {}
        sv_after_deq = {}
        sv_after_chunk = {}
        gch = [0]

        def emit_loads(tc):
            wait_free("y", "ebig")
            wait_free("y", "ebig_s")
            for grp in range(3):
                do("y", lambda e, t_=tc, g=grp: e.dma_start(
                    out=bigi[:, g * 8 * 512:(g + 1) * 8 * 512].rearrange("p (i c) -> p i c", c=512),
                    in_=qg2_sp[t_, g * 8:(g + 1) * 8, :, :].rearrange("i p c -> p i c")),
                    inc="dse", by=16)
                do("y", lambda e, t_=tc, g=grp: e.dma_start(
                    out=bigi[:, (24 + g * 8) * 512:(24 + (g + 1) * 8) * 512].rearrange(
                        "p (i c) -> p i c", c=512),
                    in_=qup_sp[t_, g * 8:(g + 1) * 8, :, :].rearrange("i p c -> p i c")),
                    inc="dse", by=16)
            dse_after[tc] = n_of("dse")

        def emit_deq(tc):
            par = tc % 2
            wait_free("v", ("eoT", par))
            wait("v", "dse", dse_after[tc])
            wait("s", "dse", dse_after[tc])
            for c in range(12):
                g = gch[0]
                cp = g % 2
                if g >= 2:
                    wait("s", "sv", sv_after_chunk[g - 2])
                do("s", lambda e, c_=c, p=cp: e.activation(
                    s_up2[p], qup2_ap(c_), AF.Identity,
                    scale=s_sc.ap()[:, SC_SU:SC_SU + 1]), inc="ss")
                wait("v", "ss", n_of("ss"))
                do("v", lambda e, c_=c, p=cp: e.scalar_tensor_tensor(
                    s_or2, qg22_ap(c_), s_sc.ap()[:, SC_SG2:SC_SG2 + 1], s_up2[p],
                    ALU.mult, ALU.mult), inc="sv")
                do("v", lambda e, p=cp: e.tensor_scalar(
                    s_up2[p], s_or2, s_sc.ap()[:, SC_RSO:SC_RSO + 1], MAGIC,
                    ALU.mult, ALU.add), inc="sv")
                do("v", lambda e, c_=c, p=cp, pr=par: e.tensor_scalar(
                    oT2_ap(pr, c_), s_up2[p], MAGIC, s_sc.ap()[:, SC_SO:SC_SO + 1],
                    ALU.subtract, ALU.mult), inc="sv", dr=(c == 11))
                sv_after_chunk[g] = n_of("sv")
                gch[0] += 1
            free_ev["ebig"] = ("sv", n_of("sv"))
            free_ev["ebig_s"] = ("ss", n_of("ss"))
            sv_after_deq[tc] = n_of("sv")

        emit_loads(0)
        emit_deq(0)
        for tc in range(NTC):
            if tc + 1 < NTC:
                emit_loads(tc + 1)
                emit_deq(tc + 1)
            wait("t", "sv", sv_after_deq[tc])
            wait("t", "dw", n_of("dw"))
            ng = [0]
            for ttl in range(4):
                gtt = tc * 4 + ttl
                b, tr = gtt // (NTT // BL), (gtt % (NTT // BL)) * 128
                wpar = gtt % 2
                wait_free("y", ("xw", wpar))
                do("y", lambda e, b_=b, t_=tr, p=wpar: e.dma_start(
                    out=s_xw[p].ap(), in_=x_h[b_, t_:t_ + 128, :]), inc=f"dx{wpar}", by=16)
                wait_free("y", ("yld", wpar))
                do("y", lambda e, c=gtt, p=wpar: e.dma_start(
                    out=s_yld[p].ap(),
                    in_=yt_sp.rearrange("d p n -> p d n")[:, :, c * 128:(c + 1) * 128]),
                    inc=f"dwy{wpar}", by=16)
                for mc in range(2):
                    slot = ng[0] % 2
                    if n_of("emm") >= 2:
                        wait("t", "eev", n_of("emm") - 1)
                    for it in range(IT):
                        do("t", lambda e, i=it, tl=ttl, m=mc, sl=slot, pr=(tc % 2): e.matmul(
                            p_e[sl], oT_ap(pr, i)[:, tl * 128:(tl + 1) * 128],
                            wdT[:, i, m * 512:(m + 1) * 512],
                            start=(i == 0), stop=(i == IT - 1)),
                            inc=("emm" if it == IT - 1 else None))
                    wait("s", "emm", n_of("emm"))
                    if mc == 0:
                        wait_free("s", ("fout", ttl % 2))
                    do("s", lambda e, sl=slot, m=mc, tl=ttl: e.activation(
                        s_fout[tl % 2].ap()[:, m * 512:(m + 1) * 512], p_e[sl],
                        AF.Copy), inc="eev")
                    ng[0] += 1
                wait("t", f"dwy{wpar}", n_of(f"dwy{wpar}"))
                wait_free("t", ("ptr", wpar))
                for dt in range(DT):
                    do("t", lambda e, d=dt, p=wpar: e.transpose(
                        p_tr[p][:, d * 128:(d + 1) * 128], s_yld[p].ap()[:, d, :],
                        s_id.ap()), inc=("wtr" if dt == DT - 1 else None))
                free_ev[("yld", wpar)] = ("wtr", n_of("wtr"))
                wait("s", "wtr", n_of("wtr"))
                wait_free("s", ("ylhs", wpar))
                do("s", lambda e, p=wpar: e.activation(
                    s_ylhs[p].ap().rearrange("p a b -> p (a b)"), p_tr[p][:, 0:1024],
                    AF.Copy), inc="wev")
                free_ev[("ptr", wpar)] = ("wev", n_of("wev"))
                wait("t", "sv", WOC[0])
                wait("t", "wev", n_of("wev"))
                for mc in range(2):
                    if gtt > 0 or mc > 0:
                        wait("t", "wadd", max(0, 2 * gtt + mc - 1))
                    for dt in range(DT):
                        do("t", lambda e, d=dt, m=mc, p=wpar: e.matmul(
                            p_w[m], s_ylhs[p].ap()[:, d, :],
                            wo_sb.ap()[:, d, m * 512:(m + 1) * 512],
                            start=(d == 0), stop=(d == DT - 1)),
                            inc=("wmm" if dt == DT - 1 else None))
                free_ev[("ylhs", wpar)] = ("wmm", n_of("wmm"))
                wait("v", "wmm", n_of("wmm"))
                wait("v", f"dx{wpar}", n_of(f"dx{wpar}"))
                wait("v", "eev", n_of("eev"))
                for mc in range(2):
                    do("v", lambda e, m=mc, p=wpar: e.tensor_tensor(
                        s_ow_ap[:, m * 512:(m + 1) * 512], p_w[m],
                        s_xw[p].ap()[:, m * 512:(m + 1) * 512], ALU.add), inc=None)
                    do("v", lambda e, m=mc, p=wpar, tl=ttl: e.tensor_tensor(
                        s_xw[p].ap()[:, m * 512:(m + 1) * 512],
                        s_ow_ap[:, m * 512:(m + 1) * 512],
                        s_fout[tl % 2].ap()[:, m * 512:(m + 1) * 512], ALU.add), inc="wadd")
                free_ev[("fout", ttl % 2)] = ("wadd", n_of("wadd"))
                wait("g", "wadd", n_of("wadd"))
                do("g", lambda e, b_=b, t_=tr, p=wpar: e.dma_start(
                    out=out_h[b_, t_:t_ + 128, :], in_=s_xw[p].ap()), inc=f"do{wpar}", by=16)
                free_ev[("xw", wpar)] = (f"do{wpar}", n_of(f"do{wpar}"))
            free_ev[("eoT", tc % 2)] = ("emm", n_of("emm"))
        wait("g", "do0", n_of("do0"))
        wait("g", "do1", n_of("do1"))
        wait("y", "do0", n_of("do0"))
        wait("y", "do1", n_of("do1"))
        phM.close()

        block = top.enter_context(nc.Block())
        @block.sync
        def _(eng):
            for f in prog["y"]:
                f(eng)
        @block.vector
        def _(eng):
            for f in prog["v"]:
                f(eng)
        @block.scalar
        def _(eng):
            for f in prog["s"]:
                f(eng)
        @block.tensor
        def _(eng):
            for f in prog["t"]:
                f(eng)
        @block.gpsimd
        def _(eng):
            for f in prog["g"]:
                f(eng)

    return nc


_NC = None


def kernel(**inputs):
    global _NC
    x = np.ascontiguousarray(inputs["L0_attn_x"])
    wq = np.ascontiguousarray(inputs["L0_attn_wq"])
    wkd = np.ascontiguousarray(inputs["L0_attn_wk_down"])
    wvd = np.ascontiguousarray(inputs["L0_attn_wv_down"])
    wku = np.ascontiguousarray(inputs["L0_attn_wk_up"])
    wvu = np.ascontiguousarray(inputs["L0_attn_wv_up"])
    wo = np.ascontiguousarray(inputs["L0_attn_wo"])
    wgate = np.ascontiguousarray(inputs["L0_ffn_w_gate"])
    wup = np.ascontiguousarray(inputs["L0_ffn_w_up"])
    wdown = np.ascontiguousarray(inputs["L0_ffn_w_down"])

    ident = np.eye(128, dtype=ml_dtypes.bfloat16)
    cmask = np.where(np.tril(np.ones((128, 128), bool)), 0.0, -1e38).astype(ml_dtypes.bfloat16)

    if _NC is None:
        _NC = build_nc()
    nc = _NC

    in_maps = []
    for c in range(NCORES):
        in_maps.append(dict(
            x=x[c * BL:(c + 1) * BL],
            wq=wq, wk_down=wkd, wv_down=wvd, wk_up=wku, wv_up=wvu, wo=wo,
            wup_shard=wup[c * ITL * 128:(c + 1) * ITL * 128],
            wgate_shard=wgate[c * ITL * 128:(c + 1) * ITL * 128],
            wdown_shard=wdown[c * 128:(c + 1) * 128],
            ident=ident, cmask=cmask,
        ))
    res = run_bass_kernel_spmd(nc, in_maps, core_ids=list(range(NCORES)))
    out = np.concatenate([r["out"] for r in res.results], axis=0)
    return out
